# version 01
# baseline (speedup 1.0000x reference)
import numpy as np

import concourse.bass as bass
import concourse.mybir as mybir
from concourse.bass_utils import run_bass_kernel_spmd
from concourse.replica_groups import maybe_share_collective_output_space

F32 = mybir.dt.float32
BF16 = mybir.dt.bfloat16
OP = mybir.AluOpType
AX = mybir.AxisListType
AF = mybir.ActivationFunctionType

MAGIC = 12582912.0
QMAX = 32767.0
RG_ALL = [list(range(8))]
RG_HALF = [[0, 1, 2, 3], [4, 5, 6, 7]]

D = 1024
DC = 8
I3 = 3072
IC = 24
NT = 16
T = NT * 128


class Sem:
    def __init__(self, name):
        self.name, self.val, self.h = name, 0, None

    def inc(self, n=1):
        self.val += n
        return self.val


class Prog:
    ENGS = ("sync", "scalar", "vector", "tensor", "gpsimd")

    def __init__(self):
        self.ev = {e: [] for e in self.ENGS}
        self.sems = []

    def sem(self, name):
        s = Sem(name)
        self.sems.append(s)
        return s

    def op(self, eng, fn, waits=(), incs=(), reads=(), writes=()):
        self.ev[eng].append((tuple(waits), fn, tuple(incs),
                             frozenset(reads), frozenset(writes)))

    def emit(self, block):
        def runner(name):
            def run(e):
                dirty = set()
                for waits, fn, incs, rd, wr in self.ev[name]:
                    for sem, val in waits:
                        if val > 0:
                            e.wait_ge(sem.h, val)
                    if name in ("vector", "scalar", "gpsimd") and (rd | wr) & dirty:
                        e.drain()
                        dirty = set()
                    ins = fn(e)
                    for sem, n in incs:
                        ins.then_inc(sem.h, n)
                    dirty |= wr
            return run
        block.sync(runner("sync"))
        block.scalar(runner("scalar"))
        block.vector(runner("vector"))
        block.tensor(runner("tensor"))
        block.gpsimd(runner("gpsimd"))


class Fam:

    def __init__(self, P, name, nslots):
        self.P = P
        self.sems = [P.sem(f"{name}{i}") for i in range(nslots)]
        self.rel = [[] for _ in range(nslots)]
        self.i = -1

    def dma(self, eng, fn, extra_waits=(), reads=(), writes=()):
        self.i += 1
        s = self.i % len(self.sems)
        waits = tuple(self.rel[s]) + tuple(extra_waits)
        self.rel[s] = []
        sem = self.sems[s]
        tgt = sem.inc(16)
        self.P.op(eng, (lambda fn, s: lambda e: fn(e, s))(fn, s), waits=waits,
                  incs=((sem, 16),), reads=reads, writes=writes)
        return s, (sem, tgt)

    def release(self, *sem_vals):
        self.rel[self.i % len(self.sems)].extend(sem_vals)

    def release_slot(self, slot, *sem_vals):
        self.rel[slot].extend(sem_vals)

    def totals(self):
        return tuple((s, s.val) for s in self.sems if s.val > 0)


def build():
    nc = bass.Bass("TRN2", target_bir_lowering=False)

    x_in = nc.dram_tensor("x", [T, D], F32, kind="ExternalInput").ap()
    wqkv_in = nc.dram_tensor("wqkv", [I3, D], F32, kind="ExternalInput").ap()
    wout_in = nc.dram_tensor("wout", [D, D], F32, kind="ExternalInput").ap()
    bout_in = nc.dram_tensor("bout", [1, D], F32, kind="ExternalInput").ap()
    wg_in = nc.dram_tensor("wg", [I3, D], F32, kind="ExternalInput").ap()
    wu_in = nc.dram_tensor("wu", [I3, D], F32, kind="ExternalInput").ap()
    wd_in = nc.dram_tensor("wd", [D, I3], F32, kind="ExternalInput").ap()
    y_out = nc.dram_tensor("y", [T, D], F32, kind="ExternalOutput").ap()

    wqkv_bf_d = nc.dram_tensor("wqkv_bf_d", [I3, D], BF16)
    wout_bf_d = nc.dram_tensor("wout_bf_d", [D, D], BF16)
    ln1_d = nc.dram_tensor("ln1_d", [T, D], BF16)
    rnd0_d = nc.dram_tensor("rnd0_d", [T, D], BF16)
    wg_dq_d = nc.dram_tensor("wg_dq_d", [I3, D], BF16)
    wu_dq_d = nc.dram_tensor("wu_dq_d", [I3, D], BF16)
    wd_dq_d = nc.dram_tensor("wd_dq_d", [D, I3], BF16)
    y1_d = nc.dram_tensor("y1_d", [T, D], F32)
    mg_d = nc.dram_tensor("mg_d", [I3, T], F32)
    mu_d = nc.dram_tensor("mu_d", [I3, T], F32)
    p_d = nc.dram_tensor("p_d", [I3, T], F32)
    q_d = nc.dram_tensor("q_d", [I3, T], F32)
    kvag_i = nc.dram_tensor("kvag_i", [D, D + 8], F32)
    kvag_o = nc.dram_tensor(
        "kvag_o", [4 * D, D + 8], F32,
        addr_space=maybe_share_collective_output_space("AllGather", RG_HALF))
    sc_i = nc.dram_tensor("sc_i", [1, 16], F32)
    sc_o = nc.dram_tensor(
        "sc_o", [1, 16], F32,
        addr_space=maybe_share_collective_output_space("AllReduce", RG_ALL))

    AL = nc.alloc_sbuf_tensor
    sqst = AL("sqst", [128, D], F32)
    ln2st = AL("ln2st", [128, D], F32)
    t0st = AL("t0st", [128, D], F32)
    boutbf = AL("boutbf", [1, D], BF16)
    ones_row = AL("ones_row", [1, 128], BF16)
    ones_f = AL("ones_f", [1, 128], F32)
    ones_col = AL("ones_col", [128, 1], BF16)
    ksumbf = AL("ksumbf", [128, DC], BF16)
    mean2 = AL("mean2", [128, NT], F32)
    rstd2 = AL("rstd2", [128, NT], F32)
    rstd0 = AL("rstd0", [128, NT], F32)
    negm0 = AL("negm0", [128, NT], F32)
    stat = AL("stat", [128, 8], F32)
    am0 = AL("am0", [128, 1], F32)
    m1r = AL("m1r", [128, 1], F32)
    m2r = AL("m2r", [128, 1], F32)
    m3r = AL("m3r", [128, 1], F32)
    m4r = AL("m4r", [128, 1], F32)
    rc0 = AL("rc0", [128, 1], F32)
    c1s = AL("c1s", [128, 1], F32)
    c2s = AL("c2s", [128, 1], F32)
    c3s = AL("c3s", [128, 1], F32)
    c4s = AL("c4s", [128, 1], F32)
    S2s = AL("S2s", [128, 1], F32)
    nS2M = AL("nS2M", [128, 1], F32)
    S4s = AL("S4s", [128, 1], F32)
    tmps = AL("tmps", [128, 8], F32)
    epsln = AL("epsln", [128, 1], F32)
    magicv = AL("magicv", [128, 1], F32)

    MBASE = 34816
    used = 229344 - nc.sbuf_bytes_remaining
    assert used <= MBASE, f"smalls overflowed into manual pool: {used}"

    def at(name, shape, dtype, off_kb):
        return nc.alloc_sbuf_tensor_at(name, shape, dtype,
                                       offset=MBASE + int(off_kb * 1024))

    wqkvT = at("wqkvT", [128, DC, I3], BF16, 0)
    ln1T = at("ln1T", [128, DC, T], BF16, 48)
    kact = at("kact", [128, NT, D], BF16, 80)
    vbf = at("vbf", [128, NT, D], BF16, 112)
    qact = at("qact", [128, DC, T], BF16, 80)
    woutT = at("woutT", [128, DC, D], BF16, 112)
    kvbf = at("kvbf", [128, DC, D], BF16, 128)
    attnz = at("attnz", [128, DC, T], BF16, 0)
    zrow = at("zrow", [1, T], F32, 144)
    z_bc = at("z_bc", [128, T], F32, 152)
    y1sb = at("y1sb", [128, NT, D], F32, 32)
    XT = at("XT", [128, DC, T], BF16, 96)
    wgT = at("wgT", [128, DC, I3], BF16, 128)
    wuT = at("wuT", [128, DC, I3], BF16, 0)
    wdT = at("wdT", [128, IC, D], BF16, 96)
    o_lo = at("o_lo", [128, IC // 2, T], BF16, 0)
    o_hi = at("o_hi", [128, IC // 2, T], BF16, 48)
    xld = at("xld", [128, 2, D], F32, 144)
    ln1st = at("ln1st", [128, 2, D], BF16, 152)
    kvst = at("kvst", [128, 2, D + 8], F32, 156)
    kvsum = at("kvsum", [128, 4, D + 8], F32, 144)
    wldst = at("wldst", [128, 2, D], F32, 160.5)
    dqst = at("dqst", [128, 2, D], BF16, 169)
    rnd0st = at("rnd0st", [128, 2, D], BF16, 152)
    mst = at("mst", [128, 4, D], F32, 48)
    rdmg = at("rdmg", [128, 2, 2 * D], F32, 64)
    pwr = at("pwr", [128, 2, 2 * D], F32, 80)
    b2st = at("b2st", [128, 1, 2 * D], F32, 176)
    rdp = at("rdp", [128, 2, 2 * D], F32, 48)
    rdmu = at("rdmu", [128, 2, 2 * D], F32, 64)
    qwr = at("qwr", [128, 2, 2 * D], F32, 80)
    rdq = at("rdq", [128, 2, 2 * D], F32, 144)
    y1rd = at("y1rd", [128, 4, D], F32, 160)
    youts = at("youts", [128, 2, D], F32, 176)

    ps = [nc.alloc_psum_tensor(f"ps{i}", [128, D], F32) for i in range(3)]
    ps4 = nc.alloc_psum_tensor("ps4", [128, 512], F32)

    P = Prog()
    TE, DV, AC = P.sem("TE"), P.sem("DV"), P.sem("AC")
    CCK, CC0, CC2, CC13, CC4 = (P.sem(n) for n in
                                ("CCK", "CC0", "CC2", "CC13", "CC4"))
    GC = P.sem("GC")
    GA = P.sem("GA")

    fx = Fam(P, "fx", 2)
    flt = Fam(P, "flt", 1)
    fwq = Fam(P, "fwq", 1)
    fwo = Fam(P, "fwo", 1)
    fxt = Fam(P, "fxt", 1)
    fwg = Fam(P, "fwg", 1)
    fwu = Fam(P, "fwu", 1)
    fwd = Fam(P, "fwd", 1)
    fwl = Fam(P, "fwl", 2)
    fkv = Fam(P, "fkv", 1)
    fmg = Fam(P, "fmg", 2)
    frp = Fam(P, "frp", 2)
    fru = Fam(P, "fru", 2)
    frq = Fam(P, "frq", 2)
    fy1 = Fam(P, "fy1", 4)
    sln = Fam(P, "sln", 2)
    skv = Fam(P, "skv", 2)
    sdq = Fam(P, "sdq", 2)
    sy1 = Fam(P, "sy1", 2)
    sr0 = Fam(P, "sr0", 2)
    smt = Fam(P, "smt", 4)
    spw = Fam(P, "spw", 2)
    sqw = Fam(P, "sqw", 2)
    syo = Fam(P, "syo", 2)

    rot = {"i": -1, "rel": [(), (), ()]}

    def ps_acquire():
        rot["i"] += 1
        r = rot["i"] % 3
        w = rot["rel"][r]
        rot["rel"][r] = ()
        return ps[r], w

    def ps_release(sem, val):
        rot["rel"][rot["i"] % 3] = ((sem, val),)

    def mm(out, lh, rh, st, sp, waits=(), incs=(), skip=False):
        P.op("tensor",
             (lambda o, l, r, s1, s2, sk: lambda t: t.matmul(
                 o, l, r, start=s1, stop=s2, skip_group_check=sk))
             (out, lh, rh, st, sp, skip), waits=waits, incs=incs)

    for tns, val in ((ones_row, 1.0), (ones_f, 1.0), (ones_col, 1.0),
                     (am0, 0.0), (m1r, 0.0), (m2r, 0.0), (m3r, 0.0),
                     (m4r, 0.0), (kvst, 0.0), (epsln, 1e-5), (magicv, MAGIC)):
        P.op("vector", (lambda t_, v_: lambda v: v.memset(t_[:], v_))(tns, val),
             writes=(tns.name,))

    P.op("gpsimd", lambda g: g.dma_start(out=boutbf[:], in_=bout_in),
         incs=((GC, 16),), writes=("boutbf",))
    GC.inc(16)
    for j in range(4):
        sl = slice(j * (I3 // 4), (j + 1) * (I3 // 4))
        P.op("gpsimd",
             (lambda sl: lambda g: g.dma_start(out=wqkv_bf_d[sl, :], in_=wqkv_in[sl, :]))(sl),
             incs=((GC, 16),))
        GC.inc(16)
    for j in range(2):
        sl = slice(j * (D // 2), (j + 1) * (D // 2))
        P.op("gpsimd",
             (lambda sl: lambda g: g.dma_start(out=wout_bf_d[sl, :], in_=wout_in[sl, :]))(sl),
             incs=((GC, 16),))
        GC.inc(16)
    g_cast = GC.val

    x_t = x_in.rearrange("(n p) d -> n p d", p=128)
    for tt in range(NT):
        s, (lsem, ltgt) = fx.dma(
            "sync", (lambda tt: lambda e, s: e.dma_start(out=xld[:, s, :], in_=x_t[tt]))(tt))
        P.op("vector",
             (lambda s: lambda v: v.tensor_tensor(sqst[:], xld[:, s, :], xld[:, s, :], OP.mult))(s),
             waits=((lsem, ltgt),), reads=("xld",), writes=("sqst",))
        P.op("vector",
             (lambda s: lambda v: v.reduce_sum(stat[:, 0:1], xld[:, s, :], AX.X))(s),
             reads=("xld",), writes=("stat",))
        P.op("vector", lambda v: v.reduce_sum(stat[:, 1:2], sqst[:], AX.X),
             reads=("sqst",), writes=("stat",))
        P.op("vector", lambda v: v.tensor_scalar(stat[:, 2:3], stat[:, 0:1], 1.0 / D, None, OP.mult),
             reads=("stat",), writes=("stat",))
        P.op("vector", lambda v: v.tensor_scalar(stat[:, 3:4], stat[:, 1:2], 1.0 / D, None, OP.mult),
             reads=("stat",), writes=("stat",))
        P.op("vector", lambda v: v.tensor_tensor(stat[:, 4:5], stat[:, 2:3], stat[:, 2:3], OP.mult),
             reads=("stat",), writes=("stat",))
        P.op("vector", lambda v: v.tensor_tensor(stat[:, 5:6], stat[:, 3:4], stat[:, 4:5], OP.subtract),
             reads=("stat",), writes=("stat",), incs=((DV, 1),))
        dvv = DV.inc(1)
        P.op("scalar", lambda e: e.activation(stat[:, 6:7], stat[:, 5:6], AF.Sqrt, bias=epsln[:]),
             waits=((DV, dvv),), reads=("stat", "epsln"), writes=("stat",), incs=((AC, 1),))
        acv = AC.inc(1)
        P.op("vector", lambda v: v.reciprocal(stat[:, 7:8], stat[:, 6:7]),
             waits=((AC, acv),), reads=("stat",), writes=("stat",))
        sl2 = tt % 2
        wfree2 = tuple(sln.rel[sl2])
        sln.rel[sl2] = []
        P.op("vector",
             (lambda s, sl2: lambda v: v.tensor_scalar(ln1st[:, sl2, :], xld[:, s, :],
                                                       stat[:, 2:3], stat[:, 7:8],
                                                       OP.subtract, OP.mult))(s, sl2),
             waits=wfree2, reads=("xld", "stat"), writes=("ln1st",), incs=((DV, 1),))
        dvv = DV.inc(1)
        fx.release((DV, dvv))
        sem = sln.sems[sl2]
        tgt = sem.inc(16)
        P.op("scalar",
             (lambda tt, sl2: lambda e: e.dma_start(out=ln1_d[tt * 128:(tt + 1) * 128, :],
                                                    in_=ln1st[:, sl2, :]))(tt, sl2),
             waits=((DV, dvv),), incs=((sem, 16),), reads=("ln1st",))
        sln.i = tt
        sln.rel[sl2] = [(sem, tgt)]
    w_ln1d = sln.totals()

    for dc in range(DC):
        flt.dma("sync",
                (lambda dc: lambda e, s: e.dma_start(
                    out=ln1T[:, dc, :], in_=ln1_d[:, dc * 128:(dc + 1) * 128],
                    transpose=True))(dc),
                extra_waits=w_ln1d if dc == 0 else ())
    w_ln1T = flt.totals()
    for dc in range(DC):
        fwq.dma("sync",
                (lambda dc: lambda e, s: e.dma_start(
                    out=wqkvT[:, dc, :], in_=wqkv_bf_d[:, dc * 128:(dc + 1) * 128],
                    transpose=True))(dc),
                extra_waits=((GC, g_cast),) if dc == 0 else ())
    w_wqkvT = fwq.totals()

    for tt in range(NT):
        rk, wk = ps_acquire()
        rv, wv = ps_acquire()
        w0 = w_ln1T + w_wqkvT + wk + wv
        for dc in range(DC):
            lh = ln1T[:, dc, tt * 128:(tt + 1) * 128]
            st, sp = dc == 0, dc == DC - 1
            for hf in range(2):
                mm(rk[:, hf * 512:(hf + 1) * 512], lh,
                   wqkvT[:, dc, D + hf * 512:D + (hf + 1) * 512], st, sp,
                   waits=w0 if (dc == 0 and hf == 0) else (), skip=True)
                w0 = ()
                incs = ()
                if sp and hf == 1:
                    incs = ((TE, 1),)
                    TE.inc(1)
                mm(rv[:, hf * 512:(hf + 1) * 512], lh,
                   wqkvT[:, dc, 2 * D + hf * 512:2 * D + (hf + 1) * 512], st, sp,
                   incs=incs, skip=True)
        tev = TE.val
        P.op("vector",
             (lambda tt, rk: lambda v: v.tensor_scalar(kact[:, tt, :], rk[:], 0.0, 1e-6,
                                                       OP.max, OP.add))(tt, rk),
             waits=((TE, tev),), writes=("kact",), incs=((DV, 1),))
        dv_k = DV.inc(1)
        P.op("scalar",
             (lambda tt, rv: lambda e: e.copy(vbf[:, tt, :], rv[:]))(tt, rv),
             waits=((TE, tev),), writes=("vbf",), incs=((AC, 1),))
        ac_v = AC.inc(1)
        rot["rel"][(rot["i"] - 1) % 3] = ((DV, dv_k),)
        rot["rel"][rot["i"] % 3] = ((AC, ac_v),)
    dv_2a, ac_2a = DV.val, AC.val

    for dc in range(DC):
        r, wfree = ps_acquire()
        for tt in range(NT):
            lh = kact[:, tt, dc * 128:(dc + 1) * 128]
            st, sp = tt == 0, tt == NT - 1
            w0 = ((DV, dv_2a), (AC, ac_2a)) + wfree if tt == 0 else ()
            mm(r[:, 0:512], lh, vbf[:, tt, 0:512], st, sp, waits=w0, skip=True)
            mm(r[:, 512:1024], lh, vbf[:, tt, 512:1024], st, sp, skip=True)
            incs = ()
            if sp:
                incs = ((TE, 1),)
                TE.inc(1)
            mm(ps4[:, dc:dc + 1], lh, ones_col[:], st, sp, incs=incs, skip=True)
        tev = TE.val
        sl2 = dc % 2
        wfree2 = tuple(skv.rel[sl2])
        skv.rel[sl2] = []
        P.op("vector",
             (lambda sl2, r: lambda v: v.tensor_copy(kvst[:, sl2, 0:D], r[:]))(sl2, r),
             waits=((TE, tev),) + wfree2, writes=("kvst",))
        P.op("vector",
             (lambda sl2, dc: lambda v: v.tensor_copy(kvst[:, sl2, D:D + 1], ps4[:, dc:dc + 1]))(sl2, dc),
             writes=("kvst",), incs=((DV, 1),))
        dvv = DV.inc(1)
        ps_release(DV, dvv)
        sem = skv.sems[sl2]
        tgt = sem.inc(16)
        P.op("scalar",
             (lambda sl2, dc: lambda e: e.dma_start(out=kvag_i[dc * 128:(dc + 1) * 128, :],
                                                    in_=kvst[:, sl2, :]))(sl2, dc),
             waits=((DV, dvv),), incs=((sem, 16),), reads=("kvst",))
        skv.i = dc
        skv.rel[sl2] = [(sem, tgt)]
    w_kvag = skv.totals()
    P.op("gpsimd", lambda g: g.collective_compute(
        "AllGather", OP.bypass, replica_groups=RG_HALF,
        ins=[kvag_i[:, :].opt()], outs=[kvag_o[:, :].opt()]),
        waits=w_kvag, incs=((CCK, 1),))
    CCK.inc(1)

    for do in range(DC):
        for th in range(2):
            r, wfree = ps_acquire()
            for dc in range(DC):
                st, sp = dc == 0, dc == DC - 1
                for hq in range(2):
                    w0 = wfree if (dc == 0 and hq == 0) else ()
                    incs = ()
                    if sp and hq == 1:
                        incs = ((TE, 1),)
                        TE.inc(1)
                    mm(r[:, hq * 512:(hq + 1) * 512],
                       wqkvT[:, dc, do * 128:(do + 1) * 128],
                       ln1T[:, dc, th * 1024 + hq * 512:th * 1024 + (hq + 1) * 512],
                       st, sp, waits=w0, incs=incs)
            tev = TE.val
            P.op("vector",
                 (lambda do, th, r: lambda v: v.tensor_scalar(
                     qact[:, do, th * 1024:(th + 1) * 1024], r[:], 0.0, 1e-6,
                     OP.max, OP.add))(do, th, r),
                 waits=((TE, tev),), writes=("qact",), incs=((DV, 1),))
            ps_release(DV, DV.inc(1))
    dv_q = DV.val

    def dequant_chunk(src_ap, dst_ap, first_waits=()):
        nb = D // 32
        s, (lsem, ltgt) = fwl.dma(
            "sync", (lambda src_ap: lambda e, s: e.dma_start(out=wldst[:, s, :], in_=src_ap))(src_ap),
            extra_waits=first_waits)
        P.op("vector",
             (lambda s: lambda v: v.tensor_reduce(
                 sqst[:, 0:nb], wldst[:, s, :].rearrange("p (b k) -> p b k", k=32),
                 AX.X, OP.max, apply_absolute_value=True))(s),
             waits=((lsem, ltgt),), reads=("wldst",), writes=("sqst",))
        P.op("vector",
             lambda v: v.tensor_scalar(sqst[:, nb:2 * nb], sqst[:, 0:nb],
                                       1.0 / 7, 1e-12, OP.mult, OP.max),
             reads=("sqst",), writes=("sqst",))
        P.op("vector",
             lambda v: v.reciprocal(sqst[:, 2 * nb:3 * nb], sqst[:, nb:2 * nb]),
             reads=("sqst",), writes=("sqst",))
        P.op("vector",
             (lambda s: lambda v: v.tensor_tensor(
                 ln2st[:].rearrange("p (b k) -> p b k", k=32),
                 wldst[:, s, :].rearrange("p (b k) -> p b k", k=32),
                 sqst[:, 2 * nb:3 * nb].to_broadcast([128, nb, 32]), OP.mult))(s),
             reads=("wldst", "sqst"), writes=("ln2st",), incs=((DV, 1),))
        fwl.release((DV, DV.inc(1)))
        P.op("vector",
             lambda v: v.tensor_scalar(ln2st[:], ln2st[:], MAGIC, MAGIC, OP.add, OP.subtract),
             reads=("ln2st",), writes=("ln2st",))
        sd = (sdq.i + 1) % 2
        sdq.i += 1
        wfree2 = tuple(sdq.rel[sd])
        sdq.rel[sd] = []
        P.op("vector",
             (lambda sd: lambda v: v.tensor_tensor(
                 dqst[:, sd, :].rearrange("p (b k) -> p b k", k=32),
                 ln2st[:].rearrange("p (b k) -> p b k", k=32),
                 sqst[:, nb:2 * nb].to_broadcast([128, nb, 32]), OP.mult))(sd),
             waits=wfree2, reads=("ln2st", "sqst"), writes=("dqst",), incs=((DV, 1),))
        dvv = DV.inc(1)
        sem = sdq.sems[sd]
        tgt = sem.inc(16)
        P.op("scalar",
             (lambda sd, dst_ap: lambda e: e.dma_start(out=dst_ap, in_=dqst[:, sd, :]))(sd, dst_ap),
             waits=((DV, dvv),), incs=((sem, 16),), reads=("dqst",))
        sdq.rel[sd] = [(sem, tgt)]

    first = w_kvag + ((DV, dv_q),)
    for oc in range(IC):
        dequant_chunk(wg_in[oc * 128:(oc + 1) * 128, :],
                      wg_dq_d[oc * 128:(oc + 1) * 128, :], first)
        first = ()
    for oc in range(IC):
        dequant_chunk(wu_in[oc * 128:(oc + 1) * 128, :],
                      wu_dq_d[oc * 128:(oc + 1) * 128, :])
    for oc in range(DC):
        for c3 in range(3):
            dequant_chunk(wd_in[oc * 128:(oc + 1) * 128, c3 * D:(c3 + 1) * D],
                          wd_dq_d[oc * 128:(oc + 1) * 128, c3 * D:(c3 + 1) * D])
    w_dq = sdq.totals()

    for dc in range(DC):
        fwo.dma("sync",
                (lambda dc: lambda e, s: e.dma_start(
                    out=woutT[:, dc, :], in_=wout_bf_d[:, dc * 128:(dc + 1) * 128],
                    transpose=True))(dc),
                extra_waits=(((GC, g_cast),) + w_kvag) if dc == 0 else ())
    w_woutT = fwo.totals()

    kvo = kvag_o.rearrange("(r c p) e -> p r c e", p=128, c=DC)
    for dc in range(DC):
        _, (lsem, ltgt) = fkv.dma(
            "sync", (lambda dc: lambda e, s: e.dma_start(out=kvsum[:], in_=kvo[:, :, dc, :]))(dc),
            extra_waits=((CCK, 1),) if dc == 0 else ())
        P.op("vector", lambda v: v.tensor_add(kvsum[:, 0, :], kvsum[:, 0, :], kvsum[:, 1, :]),
             waits=((lsem, ltgt),), reads=("kvsum",), writes=("kvsum",))
        P.op("vector", lambda v: v.tensor_add(kvsum[:, 2, :], kvsum[:, 2, :], kvsum[:, 3, :]),
             reads=("kvsum",), writes=("kvsum",))
        P.op("vector",
             (lambda dc: lambda v: v.tensor_add(kvbf[:, dc, :], kvsum[:, 0, 0:D], kvsum[:, 2, 0:D]))(dc),
             reads=("kvsum",), writes=("kvbf",))
        P.op("vector",
             (lambda dc: lambda v: v.tensor_add(ksumbf[:, dc:dc + 1], kvsum[:, 0, D:D + 1],
                                                kvsum[:, 2, D:D + 1]))(dc),
             reads=("kvsum",), writes=("ksumbf",), incs=((DV, 1),))
        fkv.release((DV, DV.inc(1)))
    dv_kv = DV.val

    for tq in range(4):
        for dc in range(DC):
            st, sp = dc == 0, dc == DC - 1
            if tq == 0 and dc == 0:
                w0 = ((DV, dv_kv),)
            elif dc == 0:
                w0 = ((DV, DV.val),)
            else:
                w0 = ()
            incs = ()
            if sp:
                incs = ((TE, 1),)
                TE.inc(1)
            mm(ps4[0:1, 0:512], ksumbf[:, dc:dc + 1],
               qact[:, dc, tq * 512:(tq + 1) * 512], st, sp, waits=w0, incs=incs,
               skip=True)
        P.op("vector",
             (lambda tq: lambda v: v.tensor_scalar(zrow[0:1, tq * 512:(tq + 1) * 512],
                                                   ps4[0:1, 0:512], 1e-6, None, OP.add))(tq),
             waits=((TE, TE.val),), writes=("zrow",), incs=((DV, 1),))
        DV.inc(1)
    P.op("vector", lambda v: v.reciprocal(zrow[:], zrow[:]),
         reads=("zrow",), writes=("zrow",), incs=((DV, 1),))
    dv_z = DV.inc(1)
    for tq in range(4):
        TE.inc(1)
        mm(ps4[:, 0:512], ones_f[:], zrow[0:1, tq * 512:(tq + 1) * 512], True, True,
           waits=((DV, dv_z if tq == 0 else DV.val),), incs=((TE, 1),))
        P.op("vector",
             (lambda tq: lambda v: v.tensor_copy(z_bc[:, tq * 512:(tq + 1) * 512], ps4[:, 0:512]))(tq),
             waits=((TE, TE.val),), writes=("z_bc",), incs=((DV, 1),))
        DV.inc(1)
    dv_zbc = DV.val

    for ec in range(DC):
        for th in range(2):
            r, wfree = ps_acquire()
            for dc in range(DC):
                st, sp = dc == 0, dc == DC - 1
                for hq in range(2):
                    w0 = (((DV, dv_zbc),) + wfree) if (dc == 0 and hq == 0) else ()
                    incs = ()
                    if sp and hq == 1:
                        incs = ((TE, 1),)
                        TE.inc(1)
                    mm(r[:, hq * 512:(hq + 1) * 512],
                       kvbf[:, dc, ec * 128:(ec + 1) * 128],
                       qact[:, dc, th * 1024 + hq * 512:th * 1024 + (hq + 1) * 512],
                       st, sp, waits=w0, incs=incs)
            P.op("vector",
                 (lambda ec, th, r: lambda v: v.tensor_tensor(
                     attnz[:, ec, th * 1024:(th + 1) * 1024], r[:],
                     z_bc[:, th * 1024:(th + 1) * 1024], OP.mult))(ec, th, r),
                 waits=((TE, TE.val),), reads=("z_bc",), writes=("attnz",), incs=((DV, 1),))
            ps_release(DV, DV.inc(1))
    dv_attnz = DV.val

    for tt in range(NT):
        s, (lsem, ltgt) = fx.dma(
            "sync", (lambda tt: lambda e, s: e.dma_start(out=xld[:, s, :], in_=x_t[tt]))(tt),
            extra_waits=((DV, dv_attnz),) if tt == 0 else ())
        r, wfree2 = ps_acquire()
        for ec in range(DC):
            st = ec == 0
            for hf in range(2):
                w0 = (((DV, dv_attnz),) + w_woutT + wfree2) if (ec == 0 and hf == 0) else ()
                mm(r[:, hf * 512:(hf + 1) * 512],
                   attnz[:, ec, tt * 128:(tt + 1) * 128],
                   woutT[:, ec, hf * 512:(hf + 1) * 512], st, False, waits=w0, skip=True)
        for hf in range(2):
            incs = ()
            if hf == 1:
                incs = ((TE, 1),)
                TE.inc(1)
            mm(r[:, hf * 512:(hf + 1) * 512], ones_row[:],
               boutbf[0:1, hf * 512:(hf + 1) * 512], False, True, incs=incs, skip=True)
        tev = TE.val
        P.op("vector",
             (lambda tt, s, r: lambda v: v.tensor_tensor(y1sb[:, tt, :], r[:], xld[:, s, :], OP.add))(tt, s, r),
             waits=((TE, tev), (lsem, ltgt)), reads=("xld",), writes=("y1sb",), incs=((DV, 1),))
        dv_y1 = DV.inc(1)
        fx.release((DV, dv_y1))
        ps_release(DV, dv_y1)
        sl2 = tt % 2
        wfree3 = tuple(sy1.rel[sl2])
        sy1.rel[sl2] = []
        sem = sy1.sems[sl2]
        tgt = sem.inc(16)
        P.op("scalar",
             (lambda tt: lambda e: e.dma_start(out=y1_d[tt * 128:(tt + 1) * 128, :],
                                               in_=y1sb[:, tt, :]))(tt),
             waits=((DV, dv_y1),) + wfree3, incs=((sem, 16),), reads=("y1sb",))
        sy1.i = tt
        sy1.rel[sl2] = [(sem, tgt)]
        P.op("scalar",
             (lambda tt: lambda e: e.activation(sqst[:], y1sb[:, tt, :], AF.Square))(tt),
             waits=((DV, dv_y1), (DV, DV.val)), reads=("y1sb",), writes=("sqst",), incs=((AC, 1),))
        ac_sq = AC.inc(1)
        P.op("vector",
             (lambda tt: lambda v: v.reduce_sum(stat[:, 0:1], y1sb[:, tt, :], AX.X))(tt),
             reads=("y1sb",), writes=("stat",))
        P.op("vector", lambda v: v.reduce_sum(stat[:, 1:2], sqst[:], AX.X),
             waits=((AC, ac_sq),), reads=("sqst",), writes=("stat",), incs=((DV, 1),))
        DV.inc(1)
        P.op("vector",
             (lambda tt: lambda v: v.tensor_scalar(mean2[:, tt:tt + 1], stat[:, 0:1],
                                                   1.0 / D, None, OP.mult))(tt),
             reads=("stat",), writes=("mean2",))
        P.op("vector", lambda v: v.tensor_scalar(stat[:, 3:4], stat[:, 1:2], 1.0 / D, None, OP.mult),
             reads=("stat",), writes=("stat",))
        P.op("vector",
             (lambda tt: lambda v: v.tensor_tensor(stat[:, 4:5], mean2[:, tt:tt + 1],
                                                   mean2[:, tt:tt + 1], OP.mult))(tt),
             reads=("mean2",), writes=("stat",))
        P.op("vector", lambda v: v.tensor_tensor(stat[:, 5:6], stat[:, 3:4], stat[:, 4:5], OP.subtract),
             reads=("stat",), writes=("stat",), incs=((DV, 1),))
        dvv = DV.inc(1)
        P.op("scalar", lambda e: e.activation(stat[:, 6:7], stat[:, 5:6], AF.Sqrt, bias=epsln[:]),
             waits=((DV, dvv),), reads=("stat", "epsln"), writes=("stat",), incs=((AC, 1),))
        acv = AC.inc(1)
        P.op("vector",
             (lambda tt: lambda v: v.reciprocal(rstd2[:, tt:tt + 1], stat[:, 6:7]))(tt),
             waits=((AC, acv),), reads=("stat",), writes=("rstd2",))
        P.op("vector",
             (lambda tt: lambda v: v.tensor_tensor(stat[:, 2:3], mean2[:, tt:tt + 1],
                                                   rstd2[:, tt:tt + 1], OP.mult))(tt),
             reads=("mean2", "rstd2"), writes=("stat",))
        P.op("vector", lambda v: v.tensor_scalar(stat[:, 2:3], stat[:, 2:3], -1.0, None, OP.mult),
             reads=("stat",), writes=("stat",), incs=((DV, 1),))
        dvv = DV.inc(1)
        P.op("scalar",
             (lambda tt: lambda e: e.activation(ln2st[:], y1sb[:, tt, :], AF.Identity,
                                                bias=stat[:, 2:3], scale=rstd2[:, tt:tt + 1]))(tt),
             waits=((DV, dvv), (DV, DV.val)), reads=("y1sb", "stat"), writes=("ln2st",),
             incs=((AC, 1),))
        acv = AC.inc(1)
        P.op("vector",
             lambda v: v.tensor_reduce(stat[:, 0:1], ln2st[:], AX.X, OP.max,
                                       apply_absolute_value=True),
             waits=((AC, acv),), reads=("ln2st",), writes=("stat",))
        P.op("vector", lambda v: v.tensor_tensor(am0[:], am0[:], stat[:, 0:1], OP.max),
             reads=("stat", "am0"), writes=("am0",), incs=((DV, 1),))
        DV.inc(1)
    dv_s0 = DV.val
    w_y1d = sy1.totals()

    def scalar_ar(cols, maxes, wait, ccsem):
        for j, (ap_, nm) in enumerate(maxes):
            P.op("gpsimd",
                 (lambda ap_: lambda g: g.tensor_reduce(tmps[0:1, 7:8], ap_, AX.XYZWC, OP.max))(ap_),
                 waits=(wait,) if j == 0 else (), reads=(nm,), writes=("tmps",))
            P.op("gpsimd",
                 (lambda cols, j: lambda g: g.dma_start(
                     out=sc_i[0:1, cols + j:cols + j + 1], in_=tmps[0:1, 7:8]))(cols, j),
                 incs=((GA, 16),), reads=("tmps",))
            P.op("gpsimd", lambda g: None, waits=((GA, GA.inc(16)),))
        n = len(maxes)
        P.op("gpsimd", (lambda n, cols: lambda g: g.collective_compute(
            "AllReduce", OP.max, replica_groups=RG_ALL,
            ins=[sc_i[0:1, cols:cols + n].opt()],
            outs=[sc_o[0:1, cols:cols + n].opt()]))(n, cols),
            incs=((ccsem, 1),))
        return ccsem.inc(1)

    def bcast(cols, j, dst, dname, ccsem, ccw):
        P.op("gpsimd",
             (lambda dst, cols, j: lambda g: g.dma_start(
                 out=dst, in_=sc_o[0:1, cols + j:cols + j + 1].to_broadcast([128, 1])))(dst, cols, j),
             waits=((ccsem, ccw), (GA, GA.val)), incs=((GA, 16),), writes=(dname,))
        return (GA, GA.inc(16))

    ccw = scalar_ar(0, [(am0[:], "am0")], (DV, dv_s0), CC0)
    gw = bcast(0, 0, tmps[:, 0:1], "tmps", CC0, ccw)
    P.op("vector", lambda v: v.tensor_scalar(tmps[:, 1:2], tmps[:, 0:1], 1.0 / QMAX, 1e-12,
                                             OP.mult, OP.max),
         waits=(gw,), reads=("tmps",), writes=("tmps",))
    P.op("vector", lambda v: v.reciprocal(rc0[:], tmps[:, 1:2]),
         reads=("tmps",), writes=("rc0",), incs=((DV, 1),))
    dv_rc0 = DV.inc(1)

    P.op("vector", lambda v: v.tensor_tensor(rstd0[:], rstd2[:],
                                             rc0[:].to_broadcast([128, NT]), OP.mult),
         waits=((DV, dv_rc0),), reads=("rstd2", "rc0"), writes=("rstd0",))
    P.op("vector", lambda v: v.tensor_tensor(negm0[:], mean2[:], rstd0[:], OP.mult),
         reads=("mean2", "rstd0"), writes=("negm0",))
    P.op("vector", lambda v: v.tensor_scalar(negm0[:], negm0[:], -1.0, MAGIC, OP.mult, OP.add),
         reads=("negm0",), writes=("negm0",), incs=((DV, 1),))
    dv_q0 = DV.inc(1)
    for tt in range(NT):
        P.op("scalar",
             (lambda tt: lambda e: e.activation(t0st[:], y1sb[:, tt, :], AF.Identity,
                                                bias=negm0[:, tt:tt + 1],
                                                scale=rstd0[:, tt:tt + 1]))(tt),
             waits=((DV, dv_q0 if tt == 0 else DV.val),),
             reads=("y1sb", "negm0", "rstd0"), writes=("t0st",), incs=((AC, 1),))
        acv = AC.inc(1)
        sl2 = tt % 2
        wfree = tuple(sr0.rel[sl2])
        sr0.rel[sl2] = []
        P.op("vector",
             (lambda sl2: lambda v: v.tensor_scalar(rnd0st[:, sl2, :], t0st[:], MAGIC, None,
                                                    OP.subtract))(sl2),
             waits=((AC, acv),) + wfree, reads=("t0st",), writes=("rnd0st",), incs=((DV, 1),))
        dvv = DV.inc(1)
        sem = sr0.sems[sl2]
        tgt = sem.inc(16)
        P.op("scalar",
             (lambda tt, sl2: lambda e: e.dma_start(out=rnd0_d[tt * 128:(tt + 1) * 128, :],
                                                    in_=rnd0st[:, sl2, :]))(tt, sl2),
             waits=((DV, dvv),), incs=((sem, 16),), reads=("rnd0st",))
        sr0.i = tt
        sr0.rel[sl2] = [(sem, tgt)]
    w_rnd0 = sr0.totals()

    for dc in range(DC):
        fxt.dma("sync",
                (lambda dc: lambda e, s: e.dma_start(
                    out=XT[:, dc, :], in_=rnd0_d[:, dc * 128:(dc + 1) * 128],
                    transpose=True))(dc),
                extra_waits=w_rnd0 if dc == 0 else ())
    w_XT = fxt.totals()
    for hc in range(DC):
        fwg.dma("sync",
                (lambda hc: lambda e, s: e.dma_start(
                    out=wgT[:, hc, :], in_=wg_dq_d[:, hc * 128:(hc + 1) * 128],
                    transpose=True))(hc),
                extra_waits=(w_dq + ((DV, dv_attnz),)) if hc == 0 else ())
    w_wgT = fwg.totals()
    for hc in range(DC):
        fwu.dma("sync",
                (lambda hc: lambda e, s: e.dma_start(
                    out=wuT[:, hc, :], in_=wu_dq_d[:, hc * 128:(hc + 1) * 128],
                    transpose=True))(hc),
                extra_waits=w_y1d if hc == 0 else ())
    w_wuT = fwu.totals()

    def ffn_mm_step(wT, w_w, ic, th, dst_d, mrun, mname):
        r, wfree = ps_acquire()
        for hc in range(DC):
            st, sp = hc == 0, hc == DC - 1
            for hq in range(2):
                w0 = (w_w + w_XT + wfree) if (hc == 0 and hq == 0) else ()
                incs = ()
                if sp and hq == 1:
                    incs = ((TE, 1),)
                    TE.inc(1)
                mm(r[:, hq * 512:(hq + 1) * 512], wT[:, hc, ic * 128:(ic + 1) * 128],
                   XT[:, hc, th * 1024 + hq * 512:th * 1024 + (hq + 1) * 512],
                   st, sp, waits=w0, incs=incs)
        tev = TE.val
        sm = (smt.i + 1) % 4
        smt.i += 1
        wfree2 = tuple(smt.rel[sm])
        smt.rel[sm] = []
        P.op("scalar",
             (lambda sm, r: lambda e: e.copy(mst[:, sm, :], r[:]))(sm, r),
             waits=((TE, tev),) + wfree2, writes=("mst",), incs=((AC, 1),))
        acv = AC.inc(1)
        ps_release(AC, acv)
        P.op("vector",
             (lambda sm: lambda v: v.tensor_reduce(tmps[:, 1:2], mst[:, sm, :], AX.X,
                                                   OP.max, apply_absolute_value=True))(sm),
             waits=((AC, acv),), reads=("mst",), writes=("tmps",))
        P.op("vector",
             (lambda m: lambda v: v.tensor_tensor(m, m, tmps[:, 1:2], OP.max))(mrun),
             reads=("tmps", mname), writes=(mname,), incs=((DV, 1),))
        dvm = DV.inc(1)
        sem = smt.sems[sm]
        tgt = sem.inc(16)
        P.op("scalar",
             (lambda ic, th, sm: lambda e: e.dma_start(
                 out=dst_d[ic * 128:(ic + 1) * 128, th * 1024:(th + 1) * 1024],
                 in_=mst[:, sm, :]))(ic, th, sm),
             waits=((DV, dvm),), incs=((sem, 16),), reads=("mst",))
        smt.rel[sm] = [(sem, tgt)]

    for ic in range(IC):
        for th in range(2):
            ffn_mm_step(wgT, w_wgT, ic, th, mg_d, m2r[:], "m2r")
    dv_m2 = DV.val
    w_mg = smt.totals()

    ccw = scalar_ar(4, [(m2r[:], "m2r")], (DV, dv_m2), CC2)
    gw = bcast(4, 0, tmps[:, 2:3], "tmps", CC2, ccw)
    P.op("vector", lambda v: v.tensor_scalar(tmps[:, 3:4], tmps[:, 2:3], 1e-30, None, OP.max),
         waits=(gw,), reads=("tmps",), writes=("tmps",))
    P.op("vector", lambda v: v.reciprocal(c2s[:], tmps[:, 3:4]), reads=("tmps",), writes=("c2s",))
    P.op("vector", lambda v: v.tensor_scalar(c2s[:], c2s[:], QMAX, None, OP.mult),
         reads=("c2s",), writes=("c2s",))
    P.op("vector", lambda v: v.reciprocal(tmps[:, 4:5], rc0[:]), reads=("rc0",), writes=("tmps",))
    P.op("vector", lambda v: v.tensor_tensor(S2s[:], tmps[:, 2:3], tmps[:, 4:5], OP.mult),
         reads=("tmps",), writes=("S2s",))
    P.op("vector", lambda v: v.tensor_scalar(S2s[:], S2s[:], 1.0 / QMAX, None, OP.mult),
         reads=("S2s",), writes=("S2s",))
    P.op("vector", lambda v: v.tensor_scalar(nS2M[:], S2s[:], -MAGIC, None, OP.mult),
         reads=("S2s",), writes=("nS2M",), incs=((DV, 1),))
    dv_s2 = DV.inc(1)

    b2_free = ()
    for ic in range(IC):
        for th in range(2):
            ffn_mm_step(wuT, w_wuT, ic, th, mu_d, m1r[:], "m1r")
        s, (lsem, ltgt) = fmg.dma(
            "sync", (lambda ic: lambda e, s: e.dma_start(
                out=rdmg[:, s, :], in_=mg_d[ic * 128:(ic + 1) * 128, :]))(ic),
            extra_waits=(w_mg + ((DV, dv_s2),)) if ic == 0 else ())
        P.op("vector",
             (lambda s: lambda v: v.tensor_scalar(rdmg[:, s, :], rdmg[:, s, :],
                                                  c2s[:], MAGIC, OP.mult, OP.add))(s),
             waits=((lsem, ltgt), (DV, dv_s2)), reads=("rdmg", "c2s"), writes=("rdmg",),
             incs=((DV, 1),))
        dva = DV.inc(1)
        for hf in range(2):
            P.op("scalar",
                 (lambda s, hf: lambda e: e.activation(
                     b2st[:, 0, hf * D:(hf + 1) * D], rdmg[:, s, hf * D:(hf + 1) * D],
                     AF.Sigmoid, bias=nS2M[:], scale=S2s[:]))(s, hf),
                 waits=((DV, dva),) + (b2_free if hf == 0 else ()),
                 reads=("rdmg", "nS2M", "S2s"), writes=("b2st",), incs=((AC, 1),))
            AC.inc(1)
        ac_sig = AC.val
        P.op("vector",
             lambda v: v.tensor_scalar(b2st[:, 0, :], b2st[:, 0, :], 65536.0, MAGIC,
                                       OP.mult, OP.add),
             waits=((AC, ac_sig),), reads=("b2st",), writes=("b2st",))
        P.op("vector",
             (lambda s: lambda v: v.tensor_scalar(rdmg[:, s, :], rdmg[:, s, :], MAGIC, None,
                                                  OP.subtract))(s),
             reads=("rdmg",), writes=("rdmg",))
        sp2 = (spw.i + 1) % 2
        spw.i += 1
        wfree2 = tuple(spw.rel[sp2])
        spw.rel[sp2] = []
        P.op("vector",
             (lambda s, sp2: lambda v: v.scalar_tensor_tensor(
                 pwr[:, sp2, :], b2st[:, 0, :], MAGIC, rdmg[:, s, :],
                 OP.subtract, OP.mult))(s, sp2),
             waits=wfree2, reads=("b2st", "rdmg"), writes=("pwr",))
        P.op("vector",
             (lambda sp2: lambda v: v.tensor_reduce(tmps[:, 2:3], pwr[:, sp2, :], AX.X,
                                                    OP.max, apply_absolute_value=True))(sp2),
             reads=("pwr",), writes=("tmps",))
        P.op("vector", lambda v: v.tensor_tensor(m3r[:], m3r[:], tmps[:, 2:3], OP.max),
             reads=("tmps", "m3r"), writes=("m3r",), incs=((DV, 1),))
        dvp = DV.inc(1)
        fmg.release((DV, dvp))
        b2_free = ((DV, dvp),)
        sem = spw.sems[sp2]
        tgt = sem.inc(16)
        P.op("scalar",
             (lambda ic, sp2: lambda e: e.dma_start(out=p_d[ic * 128:(ic + 1) * 128, :],
                                                    in_=pwr[:, sp2, :]))(ic, sp2),
             waits=((DV, dvp),), incs=((sem, 16),), reads=("pwr",))
        spw.rel[sp2] = [(sem, tgt)]
    dv_m13 = DV.val
    w_mu = smt.totals()
    w_p = spw.totals()

    ccw = scalar_ar(8, [(m1r[:], "m1r"), (m3r[:], "m3r")], (DV, dv_m13), CC13)
    gw = bcast(8, 0, tmps[:, 0:1], "tmps", CC13, ccw)
    gw2 = bcast(8, 1, tmps[:, 1:2], "tmps", CC13, ccw)
    P.op("vector", lambda v: v.tensor_scalar(tmps[:, 2:3], tmps[:, 0:1], 1e-30, None, OP.max),
         waits=(gw, gw2), reads=("tmps",), writes=("tmps",))
    P.op("vector", lambda v: v.reciprocal(c1s[:], tmps[:, 2:3]), reads=("tmps",), writes=("c1s",))
    P.op("vector", lambda v: v.tensor_scalar(c1s[:], c1s[:], QMAX, None, OP.mult),
         reads=("c1s",), writes=("c1s",))
    P.op("vector", lambda v: v.tensor_scalar(tmps[:, 3:4], tmps[:, 1:2], 1e-30, None, OP.max),
         reads=("tmps",), writes=("tmps",))
    P.op("vector", lambda v: v.reciprocal(c3s[:], tmps[:, 3:4]), reads=("tmps",), writes=("c3s",))
    P.op("vector", lambda v: v.tensor_scalar(c3s[:], c3s[:], QMAX, None, OP.mult),
         reads=("c3s",), writes=("c3s",))
    P.op("vector", lambda v: v.reciprocal(tmps[:, 4:5], rc0[:]), reads=("rc0",), writes=("tmps",))
    P.op("vector", lambda v: v.tensor_tensor(S4s[:], tmps[:, 0:1], tmps[:, 4:5], OP.mult),
         reads=("tmps",), writes=("S4s",))
    P.op("vector", lambda v: v.tensor_tensor(S4s[:], S4s[:], tmps[:, 1:2], OP.mult),
         reads=("S4s", "tmps"), writes=("S4s",))
    P.op("vector", lambda v: v.tensor_tensor(S4s[:], S4s[:], S2s[:], OP.mult),
         reads=("S4s", "S2s"), writes=("S4s",))
    P.op("vector", lambda v: v.tensor_scalar(S4s[:], S4s[:], 1.0 / (65536.0 * QMAX * QMAX * QMAX),
                                             None, OP.mult),
         reads=("S4s",), writes=("S4s",), incs=((DV, 1),))
    dv_c13 = DV.inc(1)

    b2_free = ()
    for ic in range(IC):
        s, (lps, lpt) = frp.dma(
            "sync", (lambda ic: lambda e, s: e.dma_start(
                out=rdp[:, s, :], in_=p_d[ic * 128:(ic + 1) * 128, :]))(ic),
            extra_waits=(w_p + w_mu + ((DV, dv_c13),)) if ic == 0 else ())
        su, (lus, lut) = fru.dma(
            "sync", (lambda ic: lambda e, s: e.dma_start(
                out=rdmu[:, s, :], in_=mu_d[ic * 128:(ic + 1) * 128, :]))(ic),
            extra_waits=(w_mu + w_p) if ic == 0 else ())
        P.op("scalar",
             (lambda s: lambda e: e.activation(b2st[:, 0, :], rdp[:, s, :], AF.Identity,
                                               bias=magicv[:], scale=c3s[:]))(s),
             waits=((lps, lpt), (DV, dv_c13)) + b2_free, reads=("rdp", "c3s", "magicv"),
             writes=("b2st",), incs=((AC, 1),))
        acv = AC.inc(1)
        frp.release((AC, acv))
        P.op("vector",
             lambda v: v.tensor_scalar(b2st[:, 0, :], b2st[:, 0, :], MAGIC, None, OP.subtract),
             waits=((AC, acv),), reads=("b2st",), writes=("b2st",))
        P.op("scalar",
             (lambda su: lambda e: e.activation(rdmu[:, su, :], rdmu[:, su, :], AF.Identity,
                                                bias=magicv[:], scale=c1s[:]))(su),
             waits=((lus, lut), (DV, dv_c13)), reads=("rdmu", "c1s", "magicv"),
             writes=("rdmu",), incs=((AC, 1),))
        acu = AC.inc(1)
        sq = (sqw.i + 1) % 2
        sqw.i += 1
        wfreeq = tuple(sqw.rel[sq])
        sqw.rel[sq] = []
        P.op("vector",
             (lambda su, sq: lambda v: v.scalar_tensor_tensor(
                 qwr[:, sq, :], rdmu[:, su, :], MAGIC, b2st[:, 0, :],
                 OP.subtract, OP.mult))(su, sq),
             waits=((AC, acu),) + wfreeq, reads=("rdmu", "b2st"), writes=("qwr",))
        P.op("vector",
             (lambda sq: lambda v: v.tensor_reduce(tmps[:, 2:3], qwr[:, sq, :], AX.X,
                                                   OP.max, apply_absolute_value=True))(sq),
             reads=("qwr",), writes=("tmps",))
        P.op("vector", lambda v: v.tensor_tensor(m4r[:], m4r[:], tmps[:, 2:3], OP.max),
             reads=("tmps", "m4r"), writes=("m4r",), incs=((DV, 1),))
        dvq = DV.inc(1)
        fru.release((DV, dvq))
        b2_free = ((DV, dvq),)
        sem = sqw.sems[sq]
        tgt = sem.inc(16)
        P.op("scalar",
             (lambda ic, sq: lambda e: e.dma_start(out=q_d[ic * 128:(ic + 1) * 128, :],
                                                   in_=qwr[:, sq, :]))(ic, sq),
             waits=((DV, dvq),), incs=((sem, 16),), reads=("qwr",))
        sqw.rel[sq] = [(sem, tgt)]
        fwd.dma("sync",
                (lambda ic: lambda e, s: e.dma_start(
                    out=wdT[:, ic, :], in_=wd_dq_d[:, ic * 128:(ic + 1) * 128],
                    transpose=True))(ic),
                extra_waits=(w_mu + w_dq) if ic == 0 else ())
    dv_m4 = DV.val
    w_q = sqw.totals()
    w_wdT = fwd.totals()

    ccw = scalar_ar(12, [(m4r[:], "m4r")], (DV, dv_m4), CC4)
    gw = bcast(12, 0, tmps[:, 0:1], "tmps", CC4, ccw)
    P.op("vector", lambda v: v.tensor_scalar(tmps[:, 1:2], tmps[:, 0:1], 1e-30, None, OP.max),
         waits=(gw,), reads=("tmps",), writes=("tmps",))
    P.op("vector", lambda v: v.reciprocal(c4s[:], tmps[:, 1:2]), reads=("tmps",), writes=("c4s",))
    P.op("vector", lambda v: v.tensor_scalar(c4s[:], c4s[:], QMAX, None, OP.mult),
         reads=("c4s",), writes=("c4s",))
    P.op("vector", lambda v: v.tensor_tensor(S4s[:], S4s[:], tmps[:, 0:1], OP.mult),
         reads=("S4s", "tmps"), writes=("S4s",), incs=((DV, 1),))
    dv_c4 = DV.inc(1)

    o_ready = []
    for ic in range(IC):
        s, (lqs, lqt) = frq.dma(
            "sync", (lambda ic: lambda e, s: e.dma_start(
                out=rdq[:, s, :], in_=q_d[ic * 128:(ic + 1) * 128, :]))(ic),
            extra_waits=(w_q + ((DV, dv_c4),)) if ic == 0 else ())
        P.op("vector",
             (lambda s: lambda v: v.tensor_scalar(rdq[:, s, :], rdq[:, s, :], c4s[:], MAGIC,
                                                  OP.mult, OP.add))(s),
             waits=((lqs, lqt), (DV, dv_c4)), reads=("rdq", "c4s"), writes=("rdq",))
        ot, oname = (o_lo, "o_lo") if ic < IC // 2 else (o_hi, "o_hi")
        P.op("vector",
             (lambda s, ic, ot: lambda v: v.tensor_scalar(
                 ot[:, ic % (IC // 2), :], rdq[:, s, :], MAGIC, None, OP.subtract))(s, ic, ot),
             reads=("rdq",), writes=(oname,), incs=((DV, 1),))
        dvo = DV.inc(1)
        frq.release((DV, dvo))
        o_ready.append(dvo)

    prev_grp_dv = 0
    for tg in range(0, NT, 3):
        tts = list(range(tg, min(tg + 3, NT)))
        regs = {}
        lw = {}
        for tt in tts:
            s, (lys, lyt) = fy1.dma(
                "sync", (lambda tt: lambda e, s: e.dma_start(
                    out=y1rd[:, s, :], in_=y1_d[tt * 128:(tt + 1) * 128, :]))(tt),
                extra_waits=(w_y1d + w_q) if (tg == 0 and tt == tts[0]) else ())
            lw[tt] = (s, lys, lyt)
            regs[tt], _ = ps_acquire()
        for ic in range(IC):
            ot = o_lo if ic < IC // 2 else o_hi
            for j, tt in enumerate(tts):
                lh = ot[:, ic % (IC // 2), tt * 128:(tt + 1) * 128]
                for hf in range(2):
                    st, sp = ic == 0, ic == IC - 1
                    w0 = ()
                    if hf == 0 and j == 0:
                        w0 = ((DV, o_ready[ic]),) + w_wdT
                        if ic == 0 and prev_grp_dv:
                            w0 = w0 + ((DV, prev_grp_dv),)
                    incs = ()
                    if sp and hf == 1 and j == len(tts) - 1:
                        incs = ((TE, 1),)
                        TE.inc(1)
                    mm(regs[tt][:, hf * 512:(hf + 1) * 512], lh,
                       wdT[:, ic, hf * 512:(hf + 1) * 512], st, sp,
                       waits=w0, incs=incs, skip=True)
        tev = TE.val
        for tt in tts:
            s, lys, lyt = lw[tt]
            so = (syo.i + 1) % 2
            syo.i += 1
            wfreeo = tuple(syo.rel[so])
            syo.rel[so] = []
            P.op("vector",
                 (lambda tt, s, so, r: lambda v: v.scalar_tensor_tensor(
                     youts[:, so, :], r[:], S4s[:], y1rd[:, s, :], OP.mult, OP.add))(tt, s, so, regs[tt]),
                 waits=((TE, tev), (lys, lyt)) + wfreeo,
                 reads=("y1rd", "S4s"), writes=("youts",), incs=((DV, 1),))
            dvv = DV.inc(1)
            fy1.release_slot(s, (DV, dvv))
            sem = syo.sems[so]
            tgt = sem.inc(16)
            P.op("scalar",
                 (lambda tt, so: lambda e: e.dma_start(out=y_out[tt * 128:(tt + 1) * 128, :],
                                                       in_=youts[:, so, :]))(tt, so),
                 waits=((DV, dvv),), incs=((sem, 16),), reads=("youts",))
            syo.rel[so] = [(sem, tgt)]
        prev_grp_dv = DV.val
        rot["rel"] = [((DV, prev_grp_dv),)] * 3
    w_out = syo.totals()

    P.op("gpsimd", lambda g: None, waits=w_out)

    from contextlib import ExitStack
    with ExitStack() as stk:
        for s in P.sems:
            s.h = stk.enter_context(nc.semaphore(s.name))
        block = stk.enter_context(nc.Block())
        P.emit(block)
    return nc


_NC_CACHE = {}


def kernel(L0_attn_x, L0_attn_W_qkv, L0_attn_W_out, L0_attn_b_out,
           L0_ffn_w_gate, L0_ffn_w_up, L0_ffn_w_down):
    x = np.ascontiguousarray(np.asarray(L0_attn_x, dtype=np.float32)).reshape(-1, D)
    n = 8
    tpc = x.shape[0] // n
    assert tpc == T, f"expected {T} tokens/core, got {tpc}"
    if "nc" not in _NC_CACHE:
        _NC_CACHE["nc"] = build()
    nc = _NC_CACHE["nc"]
    w = dict(
        wqkv=np.ascontiguousarray(np.asarray(L0_attn_W_qkv, np.float32)),
        wout=np.ascontiguousarray(np.asarray(L0_attn_W_out, np.float32)),
        bout=np.ascontiguousarray(np.asarray(L0_attn_b_out, np.float32)).reshape(1, D),
        wg=np.ascontiguousarray(np.asarray(L0_ffn_w_gate, np.float32)),
        wu=np.ascontiguousarray(np.asarray(L0_ffn_w_up, np.float32)),
        wd=np.ascontiguousarray(np.asarray(L0_ffn_w_down, np.float32)),
    )
    in_maps = [dict(x=np.ascontiguousarray(x[c * tpc:(c + 1) * tpc]), **w)
               for c in range(n)]
    res = run_bass_kernel_spmd(nc, in_maps, core_ids=list(range(n)))
    out = np.concatenate([np.asarray(r["y"]) for r in res.results], axis=0)
    return out.reshape(np.asarray(L0_attn_x).shape).astype(np.float32)


# version 03
# speedup vs baseline: 1.0281x; 1.0281x over previous
import numpy as np

import concourse.bass as bass
import concourse.mybir as mybir
from concourse.bass_utils import run_bass_kernel_spmd
from concourse.replica_groups import maybe_share_collective_output_space

F32 = mybir.dt.float32
BF16 = mybir.dt.bfloat16
OP = mybir.AluOpType
AX = mybir.AxisListType
AF = mybir.ActivationFunctionType

MAGIC = 12582912.0
QMAX = 32767.0
RG_ALL = [list(range(8))]
RG_HALF = [[0, 1, 2, 3], [4, 5, 6, 7]]

D = 1024
DC = 8
I3 = 3072
IC = 24
NT = 16
T = NT * 128


class Sem:
    def __init__(self, name):
        self.name, self.val, self.h = name, 0, None

    def inc(self, n=1):
        self.val += n
        return self.val


class Prog:
    ENGS = ("sync", "scalar", "vector", "tensor", "gpsimd")

    def __init__(self):
        self.ev = {e: [] for e in self.ENGS}
        self.sems = []

    def sem(self, name):
        s = Sem(name)
        self.sems.append(s)
        return s

    def op(self, eng, fn, waits=(), incs=(), reads=(), writes=()):
        self.ev[eng].append((tuple(waits), fn, tuple(incs),
                             frozenset(reads), frozenset(writes)))

    def emit(self, block):
        def runner(name):
            def run(e):
                dW, dR = set(), set()
                for waits, fn, incs, rd, wr in self.ev[name]:
                    for sem, val in waits:
                        if val > 0:
                            e.wait_ge(sem.h, val)
                    if name in ("vector", "scalar", "gpsimd") and (
                            (wr & (dW | dR)) or (rd & dW)):
                        e.drain()
                        dW, dR = set(), set()
                    ins = fn(e)
                    for sem, n in incs:
                        ins.then_inc(sem.h, n)
                    dW |= wr
                    dR |= rd
            return run
        block.sync(runner("sync"))
        block.scalar(runner("scalar"))
        block.vector(runner("vector"))
        block.tensor(runner("tensor"))
        block.gpsimd(runner("gpsimd"))


class Fam:

    def __init__(self, P, name, nslots):
        self.P = P
        self.sems = [P.sem(f"{name}{i}") for i in range(nslots)]
        self.rel = [[] for _ in range(nslots)]
        self.i = -1

    def dma(self, eng, fn, extra_waits=(), reads=(), writes=()):
        self.i += 1
        s = self.i % len(self.sems)
        waits = tuple(self.rel[s]) + tuple(extra_waits)
        self.rel[s] = []
        sem = self.sems[s]
        tgt = sem.inc(16)
        self.P.op(eng, (lambda fn, s: lambda e: fn(e, s))(fn, s), waits=waits,
                  incs=((sem, 16),), reads=reads, writes=writes)
        return s, (sem, tgt)

    def release(self, *sem_vals):
        self.rel[self.i % len(self.sems)].extend(sem_vals)

    def release_slot(self, slot, *sem_vals):
        self.rel[slot].extend(sem_vals)

    def totals(self):
        return tuple((s, s.val) for s in self.sems if s.val > 0)


def build():
    nc = bass.Bass("TRN2", target_bir_lowering=False)

    x_in = nc.dram_tensor("x", [T, D], F32, kind="ExternalInput").ap()
    wqkv_in = nc.dram_tensor("wqkv", [I3, D], F32, kind="ExternalInput").ap()
    wout_in = nc.dram_tensor("wout", [D, D], F32, kind="ExternalInput").ap()
    bout_in = nc.dram_tensor("bout", [1, D], F32, kind="ExternalInput").ap()
    wg_in = nc.dram_tensor("wg", [I3, D], F32, kind="ExternalInput").ap()
    wu_in = nc.dram_tensor("wu", [I3, D], F32, kind="ExternalInput").ap()
    wd_in = nc.dram_tensor("wd", [D, I3], F32, kind="ExternalInput").ap()
    y_out = nc.dram_tensor("y", [T, D], F32, kind="ExternalOutput").ap()

    wqkv_bf_d = nc.dram_tensor("wqkv_bf_d", [I3, D], BF16)
    wout_bf_d = nc.dram_tensor("wout_bf_d", [D, D], BF16)
    ln1_d = nc.dram_tensor("ln1_d", [T, D], BF16)
    rnd0_d = nc.dram_tensor("rnd0_d", [T, D], BF16)
    wg_dq_d = nc.dram_tensor("wg_dq_d", [I3, D], BF16)
    wu_dq_d = nc.dram_tensor("wu_dq_d", [I3, D], BF16)
    wd_dq_d = nc.dram_tensor("wd_dq_d", [D, I3], BF16)
    y1_d = nc.dram_tensor("y1_d", [T, D], F32)
    mg_d = nc.dram_tensor("mg_d", [I3, T], F32)
    mu_d = nc.dram_tensor("mu_d", [I3, T], F32)
    p_d = nc.dram_tensor("p_d", [I3, T], F32)
    q_d = nc.dram_tensor("q_d", [I3, T], F32)
    kvag_i = nc.dram_tensor("kvag_i", [D, D + 8], F32)
    kvag_o = nc.dram_tensor(
        "kvag_o", [4 * D, D + 8], F32,
        addr_space=maybe_share_collective_output_space("AllGather", RG_HALF))
    sc_i = nc.dram_tensor("sc_i", [1, 16], F32)
    sc_o = nc.dram_tensor(
        "sc_o", [1, 16], F32,
        addr_space=maybe_share_collective_output_space("AllReduce", RG_ALL))

    AL = nc.alloc_sbuf_tensor
    sqst = AL("sqst", [128, D], F32)
    t0st = AL("t0st", [128, D], F32)
    ln2st2 = AL("ln2st2", [128, 2 * D], F32)
    boutbf = AL("boutbf", [1, D], BF16)
    ones_row = AL("ones_row", [1, 128], BF16)
    ones_f = AL("ones_f", [1, 128], F32)
    ones_col = AL("ones_col", [128, 1], BF16)
    ksumbf = AL("ksumbf", [128, DC], BF16)
    mean2 = AL("mean2", [128, NT], F32)
    rstd2 = AL("rstd2", [128, NT], F32)
    rstd0 = AL("rstd0", [128, NT], F32)
    negm0 = AL("negm0", [128, NT], F32)
    stat = AL("stat", [128, 8], F32)
    am0 = AL("am0", [128, 1], F32)
    m1r = AL("m1r", [128, 1], F32)
    m2r = AL("m2r", [128, 1], F32)
    m3r = AL("m3r", [128, 1], F32)
    m4r = AL("m4r", [128, 1], F32)
    rc0 = AL("rc0", [128, 1], F32)
    c1s = AL("c1s", [128, 1], F32)
    c2s = AL("c2s", [128, 1], F32)
    c3s = AL("c3s", [128, 1], F32)
    c4s = AL("c4s", [128, 1], F32)
    S2s = AL("S2s", [128, 1], F32)
    nS2M = AL("nS2M", [128, 1], F32)
    S4s = AL("S4s", [128, 1], F32)
    tmps = AL("tmps", [128, 8], F32)
    s1c = AL("s1c", [128, NT], F32)
    s2c = AL("s2c", [128, NT], F32)
    mean1 = AL("mean1", [128, NT], F32)
    rstd1 = AL("rstd1", [128, NT], F32)
    rmaxc = AL("rmaxc", [128, NT], F32)
    rminc = AL("rminc", [128, NT], F32)
    c65k = AL("c65k", [128, 1], F32)
    arsc = AL("arsc", [128, 1], F32)
    epsln = AL("epsln", [128, 1], F32)
    magicv = AL("magicv", [128, 1], F32)

    MBASE = 40928
    used = 229344 - nc.sbuf_bytes_remaining
    assert used <= MBASE, f"smalls overflowed into manual pool: {used}"

    def at(name, shape, dtype, off_kb):
        return nc.alloc_sbuf_tensor_at(name, shape, dtype,
                                       offset=MBASE + int(off_kb * 1024))

    wqkvT = at("wqkvT", [128, DC, I3], BF16, 0)
    ln1T = at("ln1T", [128, DC, T], BF16, 48)
    kact = at("kact", [128, NT, D], BF16, 80)
    vbf = at("vbf", [128, NT, D], BF16, 112)
    qact = at("qact", [128, DC, T], BF16, 80)
    woutT = at("woutT", [128, DC, D], BF16, 112)
    kvbf = at("kvbf", [128, DC, D], BF16, 128)
    attnz = at("attnz", [128, DC, T], BF16, 0)
    zrow = at("zrow", [1, T], F32, 144)
    z_bc = at("z_bc", [128, T], F32, 152)
    y1sb = at("y1sb", [128, NT, D], F32, 32)
    XT = at("XT", [128, DC, T], BF16, 96)
    wgT = at("wgT", [128, DC, I3], BF16, 128)
    wuT = at("wuT", [128, DC, I3], BF16, 0)
    wdT = at("wdT", [128, IC, D], BF16, 96)
    o_lo = at("o_lo", [128, IC // 2, T], BF16, 0)
    o_hi = at("o_hi", [128, IC // 2, T], BF16, 48)
    x_all = at("x_all", [128, NT, D], F32, 32)
    xld = at("xld", [128, 2, D], F32, 144)
    ln1st = at("ln1st", [128, 2, D], BF16, 152)
    kvst = at("kvst", [128, 2, D + 8], F32, 156)
    kvsum = at("kvsum", [128, 4, D + 8], F32, 0)
    wldst = at("wldst", [128, 2, 2, D], F32, 160)
    dqst = at("dqst", [128, 2, 2 * D], BF16, 176)
    rnd0st = at("rnd0st", [128, 2, D], BF16, 152)
    mst = at("mst", [128, 4, D], F32, 48)
    rdmg = at("rdmg", [128, 2, 2 * D], F32, 64)
    pwr = at("pwr", [128, 2, 2 * D], F32, 80)
    b2st = at("b2st", [128, 1, 2 * D], F32, 176)
    rdp = at("rdp", [128, 2, 2 * D], F32, 48)
    rdmu = at("rdmu", [128, 2, 2 * D], F32, 64)
    qwr = at("qwr", [128, 2, 2 * D], F32, 80)
    rdq = at("rdq", [128, 2, 2 * D], F32, 144)
    y1rd = at("y1rd", [128, 4, D], F32, 160)
    youts = at("youts", [128, 2, D], F32, 176)

    ps = [nc.alloc_psum_tensor(f"ps{i}", [128, D], F32) for i in range(3)]
    ps4 = nc.alloc_psum_tensor("ps4", [128, 512], F32)

    P = Prog()
    TE, DV, AC, GP = P.sem("TE"), P.sem("DV"), P.sem("AC"), P.sem("GP")
    CCK, CC0, CC2, CC13, CC4 = (P.sem(n) for n in
                                ("CCK", "CC0", "CC2", "CC13", "CC4"))
    GC = P.sem("GC")
    GA = P.sem("GA")

    fx = Fam(P, "fx", 2)
    flt = Fam(P, "flt", 1)
    fwq = Fam(P, "fwq", 1)
    fwo = Fam(P, "fwo", 1)
    fxt = Fam(P, "fxt", 1)
    fwg = Fam(P, "fwg", 1)
    fwu = Fam(P, "fwu", 1)
    fwd = Fam(P, "fwd", 1)
    fwl = Fam(P, "fwl", 2)
    fkv = Fam(P, "fkv", 1)
    fmg = Fam(P, "fmg", 2)
    frp = Fam(P, "frp", 2)
    fru = Fam(P, "fru", 2)
    frq = Fam(P, "frq", 2)
    fy1 = Fam(P, "fy1", 4)
    sln = Fam(P, "sln", 2)
    skv = Fam(P, "skv", 2)
    sdq = Fam(P, "sdq", 2)
    sy1 = Fam(P, "sy1", 2)
    sr0 = Fam(P, "sr0", 2)
    smt = Fam(P, "smt", 4)
    spw = Fam(P, "spw", 2)
    sqw = Fam(P, "sqw", 2)
    syo = Fam(P, "syo", 2)

    rot = {"i": -1, "rel": [(), (), ()]}

    def ps_acquire():
        rot["i"] += 1
        r = rot["i"] % 3
        w = rot["rel"][r]
        rot["rel"][r] = ()
        return ps[r], w

    def ps_release(sem, val):
        rot["rel"][rot["i"] % 3] = ((sem, val),)

    def mm(out, lh, rh, st, sp, waits=(), incs=(), skip=False):
        P.op("tensor",
             (lambda o, l, r, s1, s2, sk: lambda t: t.matmul(
                 o, l, r, start=s1, stop=s2, skip_group_check=sk))
             (out, lh, rh, st, sp, skip), waits=waits, incs=incs)

    for tns, val in ((ones_row, 1.0), (ones_f, 1.0), (ones_col, 1.0),
                     (am0, 0.0), (m1r, 0.0), (m2r, 0.0), (m3r, 0.0),
                     (m4r, 0.0), (kvst, 0.0), (epsln, 1e-5), (magicv, MAGIC), (c65k, 65536.0)):
        P.op("vector", (lambda t_, v_: lambda v: v.memset(t_[:], v_))(tns, val),
             writes=(tns.name,))

    P.op("gpsimd", lambda g: g.dma_start(out=boutbf[:], in_=bout_in),
         incs=((GC, 16),), writes=("boutbf",))
    GC.inc(16)
    for j in range(4):
        sl = slice(j * (I3 // 4), (j + 1) * (I3 // 4))
        P.op("gpsimd",
             (lambda sl: lambda g: g.dma_start(out=wqkv_bf_d[sl, :], in_=wqkv_in[sl, :]))(sl),
             incs=((GC, 16),))
        GC.inc(16)
    for j in range(2):
        sl = slice(j * (D // 2), (j + 1) * (D // 2))
        P.op("gpsimd",
             (lambda sl: lambda g: g.dma_start(out=wout_bf_d[sl, :], in_=wout_in[sl, :]))(sl),
             incs=((GC, 16),))
        GC.inc(16)
    g_cast = GC.val

    x_t = x_in.rearrange("(n p) d -> n p d", p=128)
    x_all3 = x_in.rearrange("(n p) d -> p n d", p=128)
    _, (lxs, lxt) = fx.dma(
        "sync", lambda e, s: e.dma_start(out=x_all[:], in_=x_all3), writes=("x_all",))
    for tt in range(NT):
        P.op("vector",
             (lambda tt: lambda v: v.reduce_sum(s1c[:, tt:tt + 1], x_all[:, tt, :], AX.X))(tt),
             waits=((lxs, lxt),) if tt == 0 else (), reads=("x_all",), writes=("s1c",))
        P.op("scalar",
             (lambda tt: lambda e: e.activation(sqst[:], x_all[:, tt, :], AF.Square,
                                                accum_out=s2c[:, tt:tt + 1]))(tt),
             waits=((lxs, lxt),) if tt == 0 else (), reads=("x_all",),
             writes=("sqst", "s2c"),
             incs=((AC, 1),) if tt == NT - 1 else ())
    acv = AC.inc(1)
    P.op("vector", lambda v: v.tensor_scalar(mean1[:], s1c[:], 1.0 / D, None, OP.mult),
         waits=((AC, acv),), reads=("s1c",), writes=("mean1",))
    P.op("vector", lambda v: v.tensor_scalar(s2c[:], s2c[:], 1.0 / D, None, OP.mult),
         reads=("s2c",), writes=("s2c",))
    P.op("vector", lambda v: v.tensor_tensor(s1c[:], mean1[:], mean1[:], OP.mult),
         reads=("mean1",), writes=("s1c",))
    P.op("vector", lambda v: v.tensor_tensor(s2c[:], s2c[:], s1c[:], OP.subtract),
         reads=("s2c", "s1c"), writes=("s2c",), incs=((DV, 1),))
    dvv = DV.inc(1)
    P.op("scalar", lambda e: e.activation(s2c[:], s2c[:], AF.Sqrt, bias=epsln[:]),
         waits=((DV, dvv),), reads=("s2c", "epsln"), writes=("s2c",), incs=((AC, 1),))
    acv = AC.inc(1)
    P.op("vector", lambda v: v.reciprocal(rstd1[:], s2c[:]),
         waits=((AC, acv),), reads=("s2c",), writes=("rstd1",), incs=((DV, 1),))
    DV.inc(1)
    for tt in range(NT):
        sl2 = tt % 2
        wfree2 = tuple(sln.rel[sl2])
        sln.rel[sl2] = []
        P.op("vector",
             (lambda tt, sl2: lambda v: v.tensor_scalar(ln1st[:, sl2, :], x_all[:, tt, :],
                                                        mean1[:, tt:tt + 1], rstd1[:, tt:tt + 1],
                                                        OP.subtract, OP.mult))(tt, sl2),
             waits=wfree2, reads=("x_all", "mean1", "rstd1"), writes=("ln1st",), incs=((DV, 1),))
        dvv = DV.inc(1)
        sem = sln.sems[sl2]
        tgt = sem.inc(16)
        P.op("scalar",
             (lambda tt, sl2: lambda e: e.dma_start(out=ln1_d[tt * 128:(tt + 1) * 128, :],
                                                    in_=ln1st[:, sl2, :]))(tt, sl2),
             waits=((DV, dvv),), incs=((sem, 16),), reads=("ln1st",))
        sln.i = tt
        sln.rel[sl2] = [(sem, tgt)]
    dv_ln1 = DV.val
    w_ln1d = sln.totals()

    for dc in range(DC):
        flt.dma("sync",
                (lambda dc: lambda e, s: e.dma_start(
                    out=ln1T[:, dc, :], in_=ln1_d[:, dc * 128:(dc + 1) * 128],
                    transpose=True))(dc),
                extra_waits=w_ln1d if dc == 0 else ())
    w_ln1T = flt.totals()
    for dc in range(DC):
        fwq.dma("sync",
                (lambda dc: lambda e, s: e.dma_start(
                    out=wqkvT[:, dc, :], in_=wqkv_bf_d[:, dc * 128:(dc + 1) * 128],
                    transpose=True))(dc),
                extra_waits=((GC, g_cast),) if dc == 0 else ())
    w_wqkvT = fwq.totals()

    for tt in range(NT):
        rk, wk = ps_acquire()
        rv, wv = ps_acquire()
        w0 = w_ln1T + w_wqkvT + wk + wv
        for dc in range(DC):
            lh = ln1T[:, dc, tt * 128:(tt + 1) * 128]
            st, sp = dc == 0, dc == DC - 1
            for hf in range(2):
                mm(rk[:, hf * 512:(hf + 1) * 512], lh,
                   wqkvT[:, dc, D + hf * 512:D + (hf + 1) * 512], st, sp,
                   waits=w0 if (dc == 0 and hf == 0) else (), skip=True)
                w0 = ()
                incs = ()
                if sp and hf == 1:
                    incs = ((TE, 1),)
                    TE.inc(1)
                mm(rv[:, hf * 512:(hf + 1) * 512], lh,
                   wqkvT[:, dc, 2 * D + hf * 512:2 * D + (hf + 1) * 512], st, sp,
                   incs=incs, skip=True)
        tev = TE.val
        P.op("vector",
             (lambda tt, rk: lambda v: v.tensor_scalar(kact[:, tt, :], rk[:], 0.0, 1e-6,
                                                       OP.max, OP.add))(tt, rk),
             waits=((TE, tev),), writes=("kact",), incs=((DV, 1),))
        dv_k = DV.inc(1)
        P.op("scalar",
             (lambda tt, rv: lambda e: e.copy(vbf[:, tt, :], rv[:]))(tt, rv),
             waits=((TE, tev),), writes=("vbf",), incs=((AC, 1),))
        ac_v = AC.inc(1)
        rot["rel"][(rot["i"] - 1) % 3] = ((DV, dv_k),)
        rot["rel"][rot["i"] % 3] = ((AC, ac_v),)
    dv_2a, ac_2a = DV.val, AC.val

    for dc in range(DC):
        r, wfree = ps_acquire()
        for tt in range(NT):
            lh = kact[:, tt, dc * 128:(dc + 1) * 128]
            st, sp = tt == 0, tt == NT - 1
            w0 = ((DV, dv_2a), (AC, ac_2a)) + wfree if tt == 0 else ()
            mm(r[:, 0:512], lh, vbf[:, tt, 0:512], st, sp, waits=w0, skip=True)
            mm(r[:, 512:1024], lh, vbf[:, tt, 512:1024], st, sp, skip=True)
            incs = ()
            if sp:
                incs = ((TE, 1),)
                TE.inc(1)
            mm(ps4[:, dc:dc + 1], lh, ones_col[:], st, sp, incs=incs, skip=True)
        tev = TE.val
        sl2 = dc % 2
        wfree2 = tuple(skv.rel[sl2])
        skv.rel[sl2] = []
        P.op("vector",
             (lambda sl2, r: lambda v: v.tensor_copy(kvst[:, sl2, 0:D], r[:]))(sl2, r),
             waits=((TE, tev),) + wfree2, writes=("kvst",))
        P.op("vector",
             (lambda sl2, dc: lambda v: v.tensor_copy(kvst[:, sl2, D:D + 1], ps4[:, dc:dc + 1]))(sl2, dc),
             writes=("kvst",), incs=((DV, 1),))
        dvv = DV.inc(1)
        ps_release(DV, dvv)
        sem = skv.sems[sl2]
        tgt = sem.inc(16)
        P.op("scalar",
             (lambda sl2, dc: lambda e: e.dma_start(out=kvag_i[dc * 128:(dc + 1) * 128, :],
                                                    in_=kvst[:, sl2, :]))(sl2, dc),
             waits=((DV, dvv),), incs=((sem, 16),), reads=("kvst",))
        skv.i = dc
        skv.rel[sl2] = [(sem, tgt)]
    w_kvag = skv.totals()
    P.op("gpsimd", lambda g: g.collective_compute(
        "AllGather", OP.bypass, replica_groups=RG_HALF,
        ins=[kvag_i[:, :].opt()], outs=[kvag_o[:, :].opt()]),
        waits=w_kvag, incs=((CCK, 1),))
    CCK.inc(1)

    for do in range(DC):
        for th in range(2):
            r, wfree = ps_acquire()
            for dc in range(DC):
                st, sp = dc == 0, dc == DC - 1
                for hq in range(2):
                    w0 = wfree if (dc == 0 and hq == 0) else ()
                    incs = ()
                    if sp and hq == 1:
                        incs = ((TE, 1),)
                        TE.inc(1)
                    mm(r[:, hq * 512:(hq + 1) * 512],
                       wqkvT[:, dc, do * 128:(do + 1) * 128],
                       ln1T[:, dc, th * 1024 + hq * 512:th * 1024 + (hq + 1) * 512],
                       st, sp, waits=w0, incs=incs)
            tev = TE.val
            P.op("vector",
                 (lambda do, th, r: lambda v: v.tensor_scalar(
                     qact[:, do, th * 1024:(th + 1) * 1024], r[:], 0.0, 1e-6,
                     OP.max, OP.add))(do, th, r),
                 waits=((TE, tev),), writes=("qact",), incs=((DV, 1),))
            ps_release(DV, DV.inc(1))
    dv_q = DV.val

    b2w = AL("b2w", [128, 3 * 64], F32)

    def dequant_chunk(src_ap, dst_ap, first_waits=()):
        nb = 2 * D // 32
        s, (lsem, ltgt) = fwl.dma(
            "sync", (lambda src_ap: lambda e, s: e.dma_start(out=wldst[:, s, :, :], in_=src_ap))(src_ap),
            extra_waits=first_waits)
        wv = wldst[:, s, :, :].rearrange("p a (b k) -> p (a b) k", k=32)
        P.op("vector",
             (lambda wv: lambda v: v.tensor_reduce(
                 b2w[:, 0:nb], wv, AX.X, OP.max, apply_absolute_value=True))(wv),
             waits=((lsem, ltgt),), reads=("wldst",), writes=("b2w",))
        P.op("vector",
             lambda v: v.tensor_scalar(b2w[:, nb:2 * nb], b2w[:, 0:nb],
                                       1.0 / 7, 1e-12, OP.mult, OP.max),
             reads=("b2w",), writes=("b2w",))
        P.op("vector",
             lambda v: v.reciprocal(b2w[:, 2 * nb:3 * nb], b2w[:, nb:2 * nb]),
             reads=("b2w",), writes=("b2w",))
        P.op("vector",
             (lambda wv: lambda v: v.tensor_tensor(
                 ln2st2[:].rearrange("p (b k) -> p b k", k=32), wv,
                 b2w[:, 2 * nb:3 * nb].to_broadcast([128, nb, 32]), OP.mult))(wv),
             reads=("wldst", "b2w"), writes=("ln2st2",), incs=((DV, 1),))
        fwl.release((DV, DV.inc(1)))
        P.op("vector",
             lambda v: v.tensor_scalar(ln2st2[:], ln2st2[:], MAGIC, MAGIC, OP.add, OP.subtract),
             reads=("ln2st2",), writes=("ln2st2",))
        sd = (sdq.i + 1) % 2
        sdq.i += 1
        wfree2 = tuple(sdq.rel[sd])
        sdq.rel[sd] = []
        P.op("vector",
             (lambda sd: lambda v: v.tensor_tensor(
                 dqst[:, sd, :].rearrange("p (b k) -> p b k", k=32),
                 ln2st2[:].rearrange("p (b k) -> p b k", k=32),
                 b2w[:, nb:2 * nb].to_broadcast([128, nb, 32]), OP.mult))(sd),
             waits=wfree2, reads=("ln2st2", "b2w"), writes=("dqst",), incs=((DV, 1),))
        dvv = DV.inc(1)
        sem = sdq.sems[sd]
        tgt = sem.inc(16)
        P.op("scalar",
             (lambda sd, dst_ap: lambda e: e.dma_start(out=dst_ap, in_=dqst[:, sd, :]))(sd, dst_ap),
             waits=((DV, dvv),), incs=((sem, 16),), reads=("dqst",))
        sdq.rel[sd] = [(sem, tgt)]

    first = w_kvag + ((DV, dv_q),)
    def rows2(w_dram, a):
        return w_dram.rearrange("(a p) d -> p a d", p=128)[:, 2 * a:2 * a + 2, :]
    for a in range(IC // 2):
        dequant_chunk(rows2(wg_in, a),
                      wg_dq_d.rearrange("(a p) d -> p a d", p=128)[:, 2 * a:2 * a + 2, :],
                      first)
        first = ()
    for a in range(IC // 2):
        dequant_chunk(rows2(wu_in, a),
                      wu_dq_d.rearrange("(a p) d -> p a d", p=128)[:, 2 * a:2 * a + 2, :])
    wd_v = wd_in.rearrange("(a p) e -> p a e", p=128)
    wd_dq_v = wd_dq_d.rearrange("(a p) e -> p a e", p=128)
    for a in range(DC // 2):
        for c3 in range(3):
            dequant_chunk(wd_v[:, 2 * a:2 * a + 2, c3 * D:(c3 + 1) * D],
                          wd_dq_v[:, 2 * a:2 * a + 2, c3 * D:(c3 + 1) * D])
    w_dq = sdq.totals()

    for dc in range(DC):
        fwo.dma("sync",
                (lambda dc: lambda e, s: e.dma_start(
                    out=woutT[:, dc, :], in_=wout_bf_d[:, dc * 128:(dc + 1) * 128],
                    transpose=True))(dc),
                extra_waits=(((GC, g_cast),) + w_kvag) if dc == 0 else ())
    w_woutT = fwo.totals()

    GS = P.sem("GS")
    kvo = kvag_o.rearrange("(r c p) e -> p r c e", p=128, c=DC)
    for dc in range(DC):
        _, (lsem, ltgt) = fkv.dma(
            "sync", (lambda dc: lambda e, s: e.dma_start(out=kvsum[:], in_=kvo[:, :, dc, :]))(dc),
            extra_waits=((CCK, 1), (DV, dv_q)) if dc == 0 else ())
        P.op("gpsimd", lambda g: g.tensor_add(kvsum[:, 0, :], kvsum[:, 0, :], kvsum[:, 1, :]),
             waits=((lsem, ltgt),), reads=("kvsum",), writes=("kvsum",))
        P.op("gpsimd", lambda g: g.tensor_add(kvsum[:, 2, :], kvsum[:, 2, :], kvsum[:, 3, :]),
             reads=("kvsum",), writes=("kvsum",))
        P.op("gpsimd",
             (lambda dc: lambda g: g.tensor_add(kvbf[:, dc, :], kvsum[:, 0, 0:D], kvsum[:, 2, 0:D]))(dc),
             reads=("kvsum",), writes=("kvbf",))
        P.op("gpsimd",
             (lambda dc: lambda g: g.tensor_add(ksumbf[:, dc:dc + 1], kvsum[:, 0, D:D + 1],
                                                kvsum[:, 2, D:D + 1]))(dc),
             reads=("kvsum",), writes=("ksumbf",), incs=((GS, 1),))
        fkv.release((GS, GS.inc(1)))
    dv_kv = None
    gs_kv = GS.val

    for tq in range(4):
        for dc in range(DC):
            st, sp = dc == 0, dc == DC - 1
            if tq == 0 and dc == 0:
                w0 = ((GS, gs_kv),)
            elif dc == 0:
                w0 = ((DV, DV.val),)
            else:
                w0 = ()
            incs = ()
            if sp:
                incs = ((TE, 1),)
                TE.inc(1)
            mm(ps4[0:1, 0:512], ksumbf[:, dc:dc + 1],
               qact[:, dc, tq * 512:(tq + 1) * 512], st, sp, waits=w0, incs=incs,
               skip=True)
        P.op("vector",
             (lambda tq: lambda v: v.tensor_scalar(zrow[0:1, tq * 512:(tq + 1) * 512],
                                                   ps4[0:1, 0:512], 1e-6, None, OP.add))(tq),
             waits=((TE, TE.val),), writes=("zrow",), incs=((DV, 1),))
        DV.inc(1)
    P.op("vector", lambda v: v.reciprocal(zrow[:], zrow[:]),
         reads=("zrow",), writes=("zrow",), incs=((DV, 1),))
    dv_z = DV.inc(1)
    for tq in range(4):
        TE.inc(1)
        mm(ps4[:, 0:512], ones_f[:], zrow[0:1, tq * 512:(tq + 1) * 512], True, True,
           waits=((DV, dv_z if tq == 0 else DV.val),), incs=((TE, 1),))
        P.op("vector",
             (lambda tq: lambda v: v.tensor_copy(z_bc[:, tq * 512:(tq + 1) * 512], ps4[:, 0:512]))(tq),
             waits=((TE, TE.val),), writes=("z_bc",), incs=((DV, 1),))
        DV.inc(1)
    dv_zbc = DV.val

    for ec in range(DC):
        for th in range(2):
            r, wfree = ps_acquire()
            for dc in range(DC):
                st, sp = dc == 0, dc == DC - 1
                for hq in range(2):
                    w0 = (((DV, dv_zbc),) + wfree) if (dc == 0 and hq == 0) else ()
                    incs = ()
                    if sp and hq == 1:
                        incs = ((TE, 1),)
                        TE.inc(1)
                    mm(r[:, hq * 512:(hq + 1) * 512],
                       kvbf[:, dc, ec * 128:(ec + 1) * 128],
                       qact[:, dc, th * 1024 + hq * 512:th * 1024 + (hq + 1) * 512],
                       st, sp, waits=w0, incs=incs)
            P.op("vector",
                 (lambda ec, th, r: lambda v: v.tensor_tensor(
                     attnz[:, ec, th * 1024:(th + 1) * 1024], r[:],
                     z_bc[:, th * 1024:(th + 1) * 1024], OP.mult))(ec, th, r),
                 waits=((TE, TE.val),), reads=("z_bc",), writes=("attnz",), incs=((DV, 1),))
            ps_release(DV, DV.inc(1))
    dv_attnz = DV.val

    for tt in range(NT):
        s, (lsem, ltgt) = fx.dma(
            "sync", (lambda tt: lambda e, s: e.dma_start(out=xld[:, s, :], in_=x_t[tt]))(tt),
            extra_waits=((DV, dv_attnz),) if tt == 0 else ())
        r, wfree2 = ps_acquire()
        for ec in range(DC):
            st = ec == 0
            for hf in range(2):
                w0 = (((DV, dv_attnz),) + w_woutT + wfree2) if (ec == 0 and hf == 0) else ()
                mm(r[:, hf * 512:(hf + 1) * 512],
                   attnz[:, ec, tt * 128:(tt + 1) * 128],
                   woutT[:, ec, hf * 512:(hf + 1) * 512], st, False, waits=w0, skip=True)
        for hf in range(2):
            incs = ()
            if hf == 1:
                incs = ((TE, 1),)
                TE.inc(1)
            mm(r[:, hf * 512:(hf + 1) * 512], ones_row[:],
               boutbf[0:1, hf * 512:(hf + 1) * 512], False, True, incs=incs, skip=True)
        tev = TE.val
        P.op("vector",
             (lambda tt, s, r: lambda v: v.tensor_tensor(y1sb[:, tt, :], r[:], xld[:, s, :], OP.add))(tt, s, r),
             waits=((TE, tev), (lsem, ltgt)), reads=("xld",), writes=("y1sb",), incs=((DV, 1),))
        dv_y1 = DV.inc(1)
        fx.release((DV, dv_y1))
        ps_release(DV, dv_y1)
        sl2 = tt % 2
        wfree3 = tuple(sy1.rel[sl2])
        sy1.rel[sl2] = []
        sem = sy1.sems[sl2]
        tgt = sem.inc(16)
        P.op("scalar",
             (lambda tt: lambda e: e.dma_start(out=y1_d[tt * 128:(tt + 1) * 128, :],
                                               in_=y1sb[:, tt, :]))(tt),
             waits=((DV, dv_y1),) + wfree3, incs=((sem, 16),), reads=("y1sb",))
        sy1.i = tt
        sy1.rel[sl2] = [(sem, tgt)]
        P.op("scalar",
             (lambda tt: lambda e: e.activation(sqst[:], y1sb[:, tt, :], AF.Square,
                                                accum_out=s2c[:, tt:tt + 1]))(tt),
             waits=((DV, dv_y1),), reads=("y1sb",), writes=("sqst", "s2c"),
             incs=((AC, 1),) if tt == NT - 1 else ())
        P.op("vector",
             (lambda tt: lambda v: v.reduce_sum(s1c[:, tt:tt + 1], y1sb[:, tt, :], AX.X))(tt),
             reads=("y1sb",), writes=("s1c",))
        P.op("vector",
             (lambda tt: lambda v: v.tensor_reduce(rmaxc[:, tt:tt + 1], y1sb[:, tt, :],
                                                   AX.X, OP.max))(tt),
             reads=("y1sb",), writes=("rmaxc",))
        P.op("vector",
             (lambda tt: lambda v: v.tensor_reduce(rminc[:, tt:tt + 1], y1sb[:, tt, :],
                                                   AX.X, OP.min))(tt),
             reads=("y1sb",), writes=("rminc",))
    ac_sq = AC.inc(1)
    P.op("vector", lambda v: v.tensor_scalar(mean2[:], s1c[:], 1.0 / D, None, OP.mult),
         waits=((AC, ac_sq),), reads=("s1c",), writes=("mean2",))
    P.op("vector", lambda v: v.tensor_scalar(s2c[:], s2c[:], 1.0 / D, None, OP.mult),
         reads=("s2c",), writes=("s2c",))
    P.op("vector", lambda v: v.tensor_tensor(s1c[:], mean2[:], mean2[:], OP.mult),
         reads=("mean2",), writes=("s1c",))
    P.op("vector", lambda v: v.tensor_tensor(s2c[:], s2c[:], s1c[:], OP.subtract),
         reads=("s2c", "s1c"), writes=("s2c",), incs=((DV, 1),))
    dvv = DV.inc(1)
    P.op("scalar", lambda e: e.activation(s2c[:], s2c[:], AF.Sqrt, bias=epsln[:]),
         waits=((DV, dvv),), reads=("s2c", "epsln"), writes=("s2c",), incs=((AC, 1),))
    acv = AC.inc(1)
    P.op("vector", lambda v: v.reciprocal(rstd2[:], s2c[:]),
         waits=((AC, acv),), reads=("s2c",), writes=("rstd2",))
    P.op("vector", lambda v: v.tensor_tensor(rmaxc[:], rmaxc[:], mean2[:], OP.subtract),
         reads=("rmaxc", "mean2"), writes=("rmaxc",))
    P.op("vector", lambda v: v.tensor_tensor(rminc[:], mean2[:], rminc[:], OP.subtract),
         reads=("rminc", "mean2"), writes=("rminc",))
    P.op("vector", lambda v: v.tensor_tensor(rmaxc[:], rmaxc[:], rminc[:], OP.max),
         reads=("rmaxc", "rminc"), writes=("rmaxc",))
    P.op("vector", lambda v: v.tensor_tensor(rmaxc[:], rmaxc[:], rstd2[:], OP.mult),
         reads=("rmaxc", "rstd2"), writes=("rmaxc",))
    P.op("vector", lambda v: v.reduce_max(am0[:], rmaxc[:], AX.X),
         reads=("rmaxc",), writes=("am0",), incs=((DV, 1),))
    DV.inc(1)
    dv_s0 = DV.val
    w_y1d = sy1.totals()

    def scalar_ar(cols, maxes, wait, ccsem):
        for j, (ap_, nm) in enumerate(maxes):
            P.op("gpsimd",
                 (lambda ap_: lambda g: g.tensor_reduce(arsc[0:1, 0:1], ap_, AX.XYZWC, OP.max))(ap_),
                 waits=(wait,) if j == 0 else (), reads=(nm,), writes=("arsc",))
            P.op("gpsimd",
                 (lambda cols, j: lambda g: g.dma_start(
                     out=sc_i[0:1, cols + j:cols + j + 1], in_=arsc[0:1, 0:1]))(cols, j),
                 incs=((GA, 16),), reads=("arsc",))
            P.op("gpsimd", lambda g: None, waits=((GA, GA.inc(16)),))
        n = len(maxes)
        P.op("gpsimd", (lambda n, cols: lambda g: g.collective_compute(
            "AllReduce", OP.max, replica_groups=RG_ALL,
            ins=[sc_i[0:1, cols:cols + n].opt()],
            outs=[sc_o[0:1, cols:cols + n].opt()]))(n, cols),
            incs=((ccsem, 1),))
        return ccsem.inc(1)

    def bcast(cols, j, dst, dname, ccsem, ccw):
        P.op("gpsimd",
             (lambda dst, cols, j: lambda g: g.dma_start(
                 out=dst, in_=sc_o[0:1, cols + j:cols + j + 1].to_broadcast([128, 1])))(dst, cols, j),
             waits=((ccsem, ccw), (GA, GA.val)), incs=((GA, 16),), writes=(dname,))
        return (GA, GA.inc(16))

    ccw = scalar_ar(0, [(am0[:], "am0")], (DV, dv_s0), CC0)
    gw = bcast(0, 0, tmps[:, 0:1], "tmps", CC0, ccw)
    P.op("vector", lambda v: v.tensor_scalar(tmps[:, 1:2], tmps[:, 0:1], 1.0 / QMAX, 1e-12,
                                             OP.mult, OP.max),
         waits=(gw,), reads=("tmps",), writes=("tmps",))
    P.op("vector", lambda v: v.reciprocal(rc0[:], tmps[:, 1:2]),
         reads=("tmps",), writes=("rc0",), incs=((DV, 1),))
    dv_rc0 = DV.inc(1)

    P.op("vector", lambda v: v.tensor_tensor(rstd0[:], rstd2[:],
                                             rc0[:].to_broadcast([128, NT]), OP.mult),
         waits=((DV, dv_rc0),), reads=("rstd2", "rc0"), writes=("rstd0",))
    P.op("vector", lambda v: v.tensor_tensor(negm0[:], mean2[:], rstd0[:], OP.mult),
         reads=("mean2", "rstd0"), writes=("negm0",))
    P.op("vector", lambda v: v.tensor_scalar(negm0[:], negm0[:], -1.0, MAGIC, OP.mult, OP.add),
         reads=("negm0",), writes=("negm0",), incs=((DV, 1),))
    dv_q0 = DV.inc(1)
    for tt in range(NT):
        P.op("scalar",
             (lambda tt: lambda e: e.activation(t0st[:], y1sb[:, tt, :], AF.Identity,
                                                bias=negm0[:, tt:tt + 1],
                                                scale=rstd0[:, tt:tt + 1]))(tt),
             waits=((DV, dv_q0 if tt == 0 else DV.val),),
             reads=("y1sb", "negm0", "rstd0"), writes=("t0st",), incs=((AC, 1),))
        acv = AC.inc(1)
        sl2 = tt % 2
        wfree = tuple(sr0.rel[sl2])
        sr0.rel[sl2] = []
        P.op("vector",
             (lambda sl2: lambda v: v.tensor_scalar(rnd0st[:, sl2, :], t0st[:], MAGIC, None,
                                                    OP.subtract))(sl2),
             waits=((AC, acv),) + wfree, reads=("t0st",), writes=("rnd0st",), incs=((DV, 1),))
        dvv = DV.inc(1)
        sem = sr0.sems[sl2]
        tgt = sem.inc(16)
        P.op("scalar",
             (lambda tt, sl2: lambda e: e.dma_start(out=rnd0_d[tt * 128:(tt + 1) * 128, :],
                                                    in_=rnd0st[:, sl2, :]))(tt, sl2),
             waits=((DV, dvv),), incs=((sem, 16),), reads=("rnd0st",))
        sr0.i = tt
        sr0.rel[sl2] = [(sem, tgt)]
    w_rnd0 = sr0.totals()

    for dc in range(DC):
        fxt.dma("sync",
                (lambda dc: lambda e, s: e.dma_start(
                    out=XT[:, dc, :], in_=rnd0_d[:, dc * 128:(dc + 1) * 128],
                    transpose=True))(dc),
                extra_waits=w_rnd0 if dc == 0 else ())
    w_XT = fxt.totals()
    for hc in range(DC):
        fwg.dma("sync",
                (lambda hc: lambda e, s: e.dma_start(
                    out=wgT[:, hc, :], in_=wg_dq_d[:, hc * 128:(hc + 1) * 128],
                    transpose=True))(hc),
                extra_waits=(w_dq + ((DV, dv_attnz),)) if hc == 0 else ())
    w_wgT = fwg.totals()
    for hc in range(DC):
        fwu.dma("sync",
                (lambda hc: lambda e, s: e.dma_start(
                    out=wuT[:, hc, :], in_=wu_dq_d[:, hc * 128:(hc + 1) * 128],
                    transpose=True))(hc),
                extra_waits=w_y1d if hc == 0 else ())
    w_wuT = fwu.totals()

    def ffn_mm_step(wT, w_w, ic, th, dst_d, mrun, mname):
        r, wfree = ps_acquire()
        for hc in range(DC):
            st, sp = hc == 0, hc == DC - 1
            for hq in range(2):
                w0 = (w_w + w_XT + wfree) if (hc == 0 and hq == 0) else ()
                incs = ()
                if sp and hq == 1:
                    incs = ((TE, 1),)
                    TE.inc(1)
                mm(r[:, hq * 512:(hq + 1) * 512], wT[:, hc, ic * 128:(ic + 1) * 128],
                   XT[:, hc, th * 1024 + hq * 512:th * 1024 + (hq + 1) * 512],
                   st, sp, waits=w0, incs=incs)
        tev = TE.val
        sm = (smt.i + 1) % 4
        smt.i += 1
        wfree2 = tuple(smt.rel[sm])
        smt.rel[sm] = []
        P.op("scalar",
             (lambda sm, r: lambda e: e.copy(mst[:, sm, :], r[:]))(sm, r),
             waits=((TE, tev),) + wfree2, writes=("mst",), incs=((AC, 1),))
        acv = AC.inc(1)
        ps_release(AC, acv)
        P.op("vector",
             (lambda sm: lambda v: v.tensor_reduce(tmps[:, 1:2], mst[:, sm, :], AX.X,
                                                   OP.max, apply_absolute_value=True))(sm),
             waits=((AC, acv),), reads=("mst",), writes=("tmps",))
        P.op("vector",
             (lambda m: lambda v: v.tensor_tensor(m, m, tmps[:, 1:2], OP.max))(mrun),
             reads=("tmps", mname), writes=(mname,), incs=((DV, 1),))
        dvm = DV.inc(1)
        sem = smt.sems[sm]
        tgt = sem.inc(16)
        P.op("scalar",
             (lambda ic, th, sm: lambda e: e.dma_start(
                 out=dst_d[ic * 128:(ic + 1) * 128, th * 1024:(th + 1) * 1024],
                 in_=mst[:, sm, :]))(ic, th, sm),
             incs=((sem, 16),), reads=("mst",))
        smt.rel[sm] = [(sem, tgt), (DV, dvm)]

    for ic in range(IC):
        for th in range(2):
            ffn_mm_step(wgT, w_wgT, ic, th, mg_d, m2r[:], "m2r")
    dv_m2 = DV.val
    w_mg = smt.totals()

    ccw = scalar_ar(4, [(m2r[:], "m2r")], (DV, dv_m2), CC2)
    gw = bcast(4, 0, tmps[:, 2:3], "tmps", CC2, ccw)
    P.op("vector", lambda v: v.tensor_scalar(tmps[:, 3:4], tmps[:, 2:3], 1e-30, None, OP.max),
         waits=(gw,), reads=("tmps",), writes=("tmps",))
    P.op("vector", lambda v: v.reciprocal(c2s[:], tmps[:, 3:4]), reads=("tmps",), writes=("c2s",))
    P.op("vector", lambda v: v.tensor_scalar(c2s[:], c2s[:], QMAX, None, OP.mult),
         reads=("c2s",), writes=("c2s",))
    P.op("vector", lambda v: v.reciprocal(tmps[:, 4:5], rc0[:]), reads=("rc0",), writes=("tmps",))
    P.op("vector", lambda v: v.tensor_tensor(S2s[:], tmps[:, 2:3], tmps[:, 4:5], OP.mult),
         reads=("tmps",), writes=("S2s",))
    P.op("vector", lambda v: v.tensor_scalar(S2s[:], S2s[:], 1.0 / QMAX, None, OP.mult),
         reads=("S2s",), writes=("S2s",))
    P.op("vector", lambda v: v.tensor_scalar(nS2M[:], S2s[:], -MAGIC, None, OP.mult),
         reads=("S2s",), writes=("nS2M",), incs=((DV, 1),))
    dv_s2 = DV.inc(1)

    b2_free = ()
    for ic in range(IC):
        for th in range(2):
            ffn_mm_step(wuT, w_wuT, ic, th, mu_d, m1r[:], "m1r")
        s, (lsem, ltgt) = fmg.dma(
            "sync", (lambda ic: lambda e, s: e.dma_start(
                out=rdmg[:, s, :], in_=mg_d[ic * 128:(ic + 1) * 128, :]))(ic),
            extra_waits=(w_mg + ((DV, dv_s2),)) if ic == 0 else ())
        P.op("scalar",
             (lambda s: lambda e: e.activation(rdmg[:, s, :], rdmg[:, s, :], AF.Identity,
                                               scale=c2s[:], bias=magicv[:]))(s),
             waits=((lsem, ltgt), (DV, dv_s2)), reads=("rdmg", "c2s", "magicv"),
             writes=("rdmg",))
        for hf in range(2):
            P.op("scalar",
                 (lambda s, hf: lambda e: e.activation(
                     b2st[:, 0, hf * D:(hf + 1) * D], rdmg[:, s, hf * D:(hf + 1) * D],
                     AF.Sigmoid, bias=nS2M[:], scale=S2s[:]))(s, hf),
                 waits=(b2_free if hf == 0 else ()),
                 reads=("rdmg", "nS2M", "S2s"), writes=("b2st",))
        P.op("scalar",
             lambda e: e.activation(b2st[:, 0, :], b2st[:, 0, :], AF.Identity,
                                    scale=c65k[:], bias=magicv[:]),
             reads=("b2st", "c65k", "magicv"), writes=("b2st",), incs=((AC, 2),))
        ac_sig = AC.inc(2)
        P.op("vector",
             (lambda s: lambda v: v.tensor_scalar(rdmg[:, s, :], rdmg[:, s, :], MAGIC, None,
                                                  OP.subtract))(s),
             waits=((AC, ac_sig),), reads=("rdmg",), writes=("rdmg",))
        P.op("vector",
             lambda v: v.tensor_scalar(b2st[:, 0, :], b2st[:, 0, :], MAGIC, None, OP.subtract),
             reads=("b2st",), writes=("b2st",), incs=((DV, 1),))
        dvsub = DV.inc(1)
        sp2 = (spw.i + 1) % 2
        spw.i += 1
        wfree2 = tuple(spw.rel[sp2])
        spw.rel[sp2] = []
        P.op("gpsimd",
             (lambda s, sp2: lambda g: g.tensor_mul(
                 pwr[:, sp2, :], b2st[:, 0, :], rdmg[:, s, :]))(s, sp2),
             waits=((DV, dvsub),) + wfree2,
             reads=("b2st", "rdmg"), writes=("pwr",), incs=((GP, 1),))
        gpp = GP.inc(1)
        fmg.release((GP, gpp))
        b2_free = ((GP, gpp),)
        P.op("vector",
             (lambda sp2: lambda v: v.tensor_reduce(tmps[:, 2:3], pwr[:, sp2, :], AX.X,
                                                    OP.max, apply_absolute_value=True))(sp2),
             waits=((GP, gpp),), reads=("pwr",), writes=("tmps",))
        P.op("vector", lambda v: v.tensor_tensor(m3r[:], m3r[:], tmps[:, 2:3], OP.max),
             reads=("tmps", "m3r"), writes=("m3r",), incs=((DV, 1),))
        dvp = DV.inc(1)
        sem = spw.sems[sp2]
        tgt = sem.inc(16)
        P.op("scalar",
             (lambda ic, sp2: lambda e: e.dma_start(out=p_d[ic * 128:(ic + 1) * 128, :],
                                                    in_=pwr[:, sp2, :]))(ic, sp2),
             waits=((GP, gpp),), incs=((sem, 16),), reads=("pwr",))
        spw.rel[sp2] = [(sem, tgt), (DV, dvp)]
    dv_m13 = DV.val
    w_mu = smt.totals()
    w_p = spw.totals()

    ccw = scalar_ar(8, [(m1r[:], "m1r"), (m3r[:], "m3r")], (DV, dv_m13), CC13)
    gw = bcast(8, 0, tmps[:, 0:1], "tmps", CC13, ccw)
    gw2 = bcast(8, 1, tmps[:, 1:2], "tmps", CC13, ccw)
    P.op("vector", lambda v: v.tensor_scalar(tmps[:, 2:3], tmps[:, 0:1], 1e-30, None, OP.max),
         waits=(gw, gw2), reads=("tmps",), writes=("tmps",))
    P.op("vector", lambda v: v.reciprocal(c1s[:], tmps[:, 2:3]), reads=("tmps",), writes=("c1s",))
    P.op("vector", lambda v: v.tensor_scalar(c1s[:], c1s[:], QMAX, None, OP.mult),
         reads=("c1s",), writes=("c1s",))
    P.op("vector", lambda v: v.tensor_scalar(tmps[:, 3:4], tmps[:, 1:2], 1e-30, None, OP.max),
         reads=("tmps",), writes=("tmps",))
    P.op("vector", lambda v: v.reciprocal(c3s[:], tmps[:, 3:4]), reads=("tmps",), writes=("c3s",))
    P.op("vector", lambda v: v.tensor_scalar(c3s[:], c3s[:], QMAX, None, OP.mult),
         reads=("c3s",), writes=("c3s",))
    P.op("vector", lambda v: v.reciprocal(tmps[:, 4:5], rc0[:]), reads=("rc0",), writes=("tmps",))
    P.op("vector", lambda v: v.tensor_tensor(S4s[:], tmps[:, 0:1], tmps[:, 4:5], OP.mult),
         reads=("tmps",), writes=("S4s",))
    P.op("vector", lambda v: v.tensor_tensor(S4s[:], S4s[:], tmps[:, 1:2], OP.mult),
         reads=("S4s", "tmps"), writes=("S4s",))
    P.op("vector", lambda v: v.tensor_tensor(S4s[:], S4s[:], S2s[:], OP.mult),
         reads=("S4s", "S2s"), writes=("S4s",))
    P.op("vector", lambda v: v.tensor_scalar(S4s[:], S4s[:], 1.0 / (65536.0 * QMAX * QMAX * QMAX),
                                             None, OP.mult),
         reads=("S4s",), writes=("S4s",), incs=((DV, 1),))
    dv_c13 = DV.inc(1)

    b2_free = ()
    for ic in range(IC):
        s, (lps, lpt) = frp.dma(
            "sync", (lambda ic: lambda e, s: e.dma_start(
                out=rdp[:, s, :], in_=p_d[ic * 128:(ic + 1) * 128, :]))(ic),
            extra_waits=(w_p + w_mu + ((DV, dv_c13),)) if ic == 0 else ())
        su, (lus, lut) = fru.dma(
            "sync", (lambda ic: lambda e, s: e.dma_start(
                out=rdmu[:, s, :], in_=mu_d[ic * 128:(ic + 1) * 128, :]))(ic),
            extra_waits=(w_mu + w_p) if ic == 0 else ())
        P.op("scalar",
             (lambda s: lambda e: e.activation(b2st[:, 0, :], rdp[:, s, :], AF.Identity,
                                               bias=magicv[:], scale=c3s[:]))(s),
             waits=((lps, lpt), (DV, dv_c13)) + b2_free, reads=("rdp", "c3s", "magicv"),
             writes=("b2st",), incs=((AC, 1),))
        acv = AC.inc(1)
        frp.release((AC, acv))
        P.op("vector",
             lambda v: v.tensor_scalar(b2st[:, 0, :], b2st[:, 0, :], MAGIC, None, OP.subtract),
             waits=((AC, acv),), reads=("b2st",), writes=("b2st",), incs=((DV, 1),))
        dvb2 = DV.inc(1)
        P.op("scalar",
             (lambda su: lambda e: e.activation(rdmu[:, su, :], rdmu[:, su, :], AF.Identity,
                                                bias=magicv[:], scale=c1s[:]))(su),
             waits=((lus, lut), (DV, dv_c13)), reads=("rdmu", "c1s", "magicv"),
             writes=("rdmu",), incs=((AC, 1),))
        acu = AC.inc(1)
        sq = (sqw.i + 1) % 2
        sqw.i += 1
        wfreeq = tuple(sqw.rel[sq])
        sqw.rel[sq] = []
        P.op("vector",
             (lambda su: lambda v: v.tensor_scalar(rdmu[:, su, :], rdmu[:, su, :], MAGIC,
                                                   None, OP.subtract))(su),
             waits=((AC, acu),), reads=("rdmu",), writes=("rdmu",), incs=((DV, 1),))
        dvu = DV.inc(1)
        P.op("gpsimd",
             (lambda su, sq: lambda g: g.tensor_mul(
                 qwr[:, sq, :], rdmu[:, su, :], b2st[:, 0, :]))(su, sq),
             waits=((DV, dvu), (DV, dvb2)) + wfreeq, reads=("rdmu", "b2st"),
             writes=("qwr",), incs=((GP, 1),))
        gpq = GP.inc(1)
        fru.release((GP, gpq))
        b2_free = ((GP, gpq),)
        P.op("vector",
             (lambda sq: lambda v: v.tensor_reduce(tmps[:, 2:3], qwr[:, sq, :], AX.X,
                                                   OP.max, apply_absolute_value=True))(sq),
             waits=((GP, gpq),), reads=("qwr",), writes=("tmps",))
        P.op("vector", lambda v: v.tensor_tensor(m4r[:], m4r[:], tmps[:, 2:3], OP.max),
             reads=("tmps", "m4r"), writes=("m4r",), incs=((DV, 1),))
        dvq = DV.inc(1)
        sem = sqw.sems[sq]
        tgt = sem.inc(16)
        P.op("scalar",
             (lambda ic, sq: lambda e: e.dma_start(out=q_d[ic * 128:(ic + 1) * 128, :],
                                                   in_=qwr[:, sq, :]))(ic, sq),
             waits=((GP, gpq),), incs=((sem, 16),), reads=("qwr",))
        sqw.rel[sq] = [(sem, tgt), (DV, dvq)]
        fwd.dma("sync",
                (lambda ic: lambda e, s: e.dma_start(
                    out=wdT[:, ic, :], in_=wd_dq_d[:, ic * 128:(ic + 1) * 128],
                    transpose=True))(ic),
                extra_waits=(w_mu + w_dq) if ic == 0 else ())
    dv_m4 = DV.val
    w_q = sqw.totals()
    w_wdT = fwd.totals()

    ccw = scalar_ar(12, [(m4r[:], "m4r")], (DV, dv_m4), CC4)
    gw = bcast(12, 0, tmps[:, 0:1], "tmps", CC4, ccw)
    P.op("vector", lambda v: v.tensor_scalar(tmps[:, 1:2], tmps[:, 0:1], 1e-30, None, OP.max),
         waits=(gw,), reads=("tmps",), writes=("tmps",))
    P.op("vector", lambda v: v.reciprocal(c4s[:], tmps[:, 1:2]), reads=("tmps",), writes=("c4s",))
    P.op("vector", lambda v: v.tensor_scalar(c4s[:], c4s[:], QMAX, None, OP.mult),
         reads=("c4s",), writes=("c4s",))
    P.op("vector", lambda v: v.tensor_tensor(S4s[:], S4s[:], tmps[:, 0:1], OP.mult),
         reads=("S4s", "tmps"), writes=("S4s",), incs=((DV, 1),))
    dv_c4 = DV.inc(1)

    o_ready = []
    for ic in range(IC):
        s, (lqs, lqt) = frq.dma(
            "sync", (lambda ic: lambda e, s: e.dma_start(
                out=rdq[:, s, :], in_=q_d[ic * 128:(ic + 1) * 128, :]))(ic),
            extra_waits=(w_q + ((DV, dv_c4),)) if ic == 0 else ())
        P.op("vector",
             (lambda s: lambda v: v.tensor_scalar(rdq[:, s, :], rdq[:, s, :], c4s[:], MAGIC,
                                                  OP.mult, OP.add))(s),
             waits=((lqs, lqt), (DV, dv_c4)), reads=("rdq", "c4s"), writes=("rdq",))
        ot, oname = (o_lo, "o_lo") if ic < IC // 2 else (o_hi, "o_hi")
        P.op("vector",
             (lambda s, ic, ot: lambda v: v.tensor_scalar(
                 ot[:, ic % (IC // 2), :], rdq[:, s, :], MAGIC, None, OP.subtract))(s, ic, ot),
             reads=("rdq",), writes=(oname,), incs=((DV, 1),))
        dvo = DV.inc(1)
        frq.release((DV, dvo))
        o_ready.append(dvo)

    prev_grp_dv = 0
    for tg in range(0, NT, 3):
        tts = list(range(tg, min(tg + 3, NT)))
        regs = {}
        lw = {}
        for tt in tts:
            s, (lys, lyt) = fy1.dma(
                "sync", (lambda tt: lambda e, s: e.dma_start(
                    out=y1rd[:, s, :], in_=y1_d[tt * 128:(tt + 1) * 128, :]))(tt),
                extra_waits=(w_y1d + w_q) if (tg == 0 and tt == tts[0]) else ())
            lw[tt] = (s, lys, lyt)
            regs[tt], _ = ps_acquire()
        for ic in range(IC):
            ot = o_lo if ic < IC // 2 else o_hi
            for j, tt in enumerate(tts):
                lh = ot[:, ic % (IC // 2), tt * 128:(tt + 1) * 128]
                for hf in range(2):
                    st, sp = ic == 0, ic == IC - 1
                    w0 = ()
                    if hf == 0 and j == 0:
                        w0 = ((DV, o_ready[ic]),) + w_wdT
                        if ic == 0 and prev_grp_dv:
                            w0 = w0 + ((DV, prev_grp_dv),)
                    incs = ()
                    if sp and hf == 1 and j == len(tts) - 1:
                        incs = ((TE, 1),)
                        TE.inc(1)
                    mm(regs[tt][:, hf * 512:(hf + 1) * 512], lh,
                       wdT[:, ic, hf * 512:(hf + 1) * 512], st, sp,
                       waits=w0, incs=incs, skip=True)
        tev = TE.val
        for tt in tts:
            s, lys, lyt = lw[tt]
            so = (syo.i + 1) % 2
            syo.i += 1
            wfreeo = tuple(syo.rel[so])
            syo.rel[so] = []
            P.op("vector",
                 (lambda tt, s, so, r: lambda v: v.scalar_tensor_tensor(
                     youts[:, so, :], r[:], S4s[:], y1rd[:, s, :], OP.mult, OP.add))(tt, s, so, regs[tt]),
                 waits=((TE, tev), (lys, lyt)) + wfreeo,
                 reads=("y1rd", "S4s"), writes=("youts",), incs=((DV, 1),))
            dvv = DV.inc(1)
            fy1.release_slot(s, (DV, dvv))
            sem = syo.sems[so]
            tgt = sem.inc(16)
            P.op("scalar",
                 (lambda tt, so: lambda e: e.dma_start(out=y_out[tt * 128:(tt + 1) * 128, :],
                                                       in_=youts[:, so, :]))(tt, so),
                 waits=((DV, dvv),), incs=((sem, 16),), reads=("youts",))
            syo.rel[so] = [(sem, tgt)]
        prev_grp_dv = DV.val
        rot["rel"] = [((DV, prev_grp_dv),)] * 3
    w_out = syo.totals()

    P.op("gpsimd", lambda g: None, waits=w_out)

    from contextlib import ExitStack
    with ExitStack() as stk:
        for s in P.sems:
            s.h = stk.enter_context(nc.semaphore(s.name))
        block = stk.enter_context(nc.Block())
        P.emit(block)
    return nc


_NC_CACHE = {}


def kernel(L0_attn_x, L0_attn_W_qkv, L0_attn_W_out, L0_attn_b_out,
           L0_ffn_w_gate, L0_ffn_w_up, L0_ffn_w_down):
    x = np.ascontiguousarray(np.asarray(L0_attn_x, dtype=np.float32)).reshape(-1, D)
    n = 8
    tpc = x.shape[0] // n
    assert tpc == T, f"expected {T} tokens/core, got {tpc}"
    if "nc" not in _NC_CACHE:
        _NC_CACHE["nc"] = build()
    nc = _NC_CACHE["nc"]
    w = dict(
        wqkv=np.ascontiguousarray(np.asarray(L0_attn_W_qkv, np.float32)),
        wout=np.ascontiguousarray(np.asarray(L0_attn_W_out, np.float32)),
        bout=np.ascontiguousarray(np.asarray(L0_attn_b_out, np.float32)).reshape(1, D),
        wg=np.ascontiguousarray(np.asarray(L0_ffn_w_gate, np.float32)),
        wu=np.ascontiguousarray(np.asarray(L0_ffn_w_up, np.float32)),
        wd=np.ascontiguousarray(np.asarray(L0_ffn_w_down, np.float32)),
    )
    in_maps = [dict(x=np.ascontiguousarray(x[c * tpc:(c + 1) * tpc]), **w)
               for c in range(n)]
    res = run_bass_kernel_spmd(nc, in_maps, core_ids=list(range(n)))
    out = np.concatenate([np.asarray(r["y"]) for r in res.results], axis=0)
    return out.reshape(np.asarray(L0_attn_x).shape).astype(np.float32)


# version 04
# speedup vs baseline: 1.2276x; 1.1940x over previous
import numpy as np

import concourse.bass as bass
import concourse.mybir as mybir
from concourse.bass_utils import run_bass_kernel_spmd
from concourse.replica_groups import maybe_share_collective_output_space

F32 = mybir.dt.float32
BF16 = mybir.dt.bfloat16
OP = mybir.AluOpType
AX = mybir.AxisListType
AF = mybir.ActivationFunctionType

MAGIC = 12582912.0
QMAX = 32767.0
RG_ALL = [list(range(8))]
RG_HALF = [[0, 1, 2, 3], [4, 5, 6, 7]]

D = 1024
DC = 8
I3 = 3072
IC = 24
NT = 16
T = NT * 128


class Sem:
    def __init__(self, name):
        self.name, self.val, self.h = name, 0, None

    def inc(self, n=1):
        self.val += n
        return self.val


class Prog:
    ENGS = ("sync", "scalar", "vector", "tensor", "gpsimd")

    def __init__(self):
        self.ev = {e: [] for e in self.ENGS}
        self.sems = []

    def sem(self, name):
        s = Sem(name)
        self.sems.append(s)
        return s

    def op(self, eng, fn, waits=(), incs=(), reads=(), writes=()):
        self.ev[eng].append((tuple(waits), fn, tuple(incs),
                             frozenset(reads), frozenset(writes)))

    def emit(self, block):
        def runner(name):
            def run(e):
                dW, dR = set(), set()
                for waits, fn, incs, rd, wr in self.ev[name]:
                    for sem, val in waits:
                        if val > 0:
                            e.wait_ge(sem.h, val)
                    if name in ("vector", "scalar", "gpsimd") and (
                            (wr & (dW | dR)) or (rd & dW)):
                        e.drain()
                        dW, dR = set(), set()
                    ins = fn(e)
                    for sem, n in incs:
                        ins.then_inc(sem.h, n)
                    dW |= wr
                    dR |= rd
            return run
        block.sync(runner("sync"))
        block.scalar(runner("scalar"))
        block.vector(runner("vector"))
        block.tensor(runner("tensor"))
        block.gpsimd(runner("gpsimd"))


class Fam:

    def __init__(self, P, name, nslots):
        self.P = P
        self.sems = [P.sem(f"{name}{i}") for i in range(nslots)]
        self.rel = [[] for _ in range(nslots)]
        self.i = -1

    def dma(self, eng, fn, extra_waits=(), reads=(), writes=()):
        self.i += 1
        s = self.i % len(self.sems)
        waits = tuple(self.rel[s]) + tuple(extra_waits)
        self.rel[s] = []
        sem = self.sems[s]
        tgt = sem.inc(16)
        self.P.op(eng, (lambda fn, s: lambda e: fn(e, s))(fn, s), waits=waits,
                  incs=((sem, 16),), reads=reads, writes=writes)
        return s, (sem, tgt)

    def release(self, *sem_vals):
        self.rel[self.i % len(self.sems)].extend(sem_vals)

    def release_slot(self, slot, *sem_vals):
        self.rel[slot].extend(sem_vals)

    def totals(self):
        return tuple((s, s.val) for s in self.sems if s.val > 0)


def build():
    nc = bass.Bass("TRN2", target_bir_lowering=False)

    x_in = nc.dram_tensor("x", [T, D], F32, kind="ExternalInput").ap()
    wqkv_in = nc.dram_tensor("wqkv", [I3, D], F32, kind="ExternalInput").ap()
    wout_in = nc.dram_tensor("wout", [D, D], F32, kind="ExternalInput").ap()
    bout_in = nc.dram_tensor("bout", [1, D], F32, kind="ExternalInput").ap()
    wsh1_in = nc.dram_tensor("wsh1", [768, D], F32, kind="ExternalInput").ap()
    wsh2_in = nc.dram_tensor("wsh2", [384, D], F32, kind="ExternalInput").ap()
    y_out = nc.dram_tensor("y", [T, D], F32, kind="ExternalOutput").ap()

    wqkv_bf_d = nc.dram_tensor("wqkv_bf_d", [I3, D], BF16)
    wout_bf_d = nc.dram_tensor("wout_bf_d", [D, D], BF16)
    ln1_d = nc.dram_tensor("ln1_d", [T, D], BF16)
    rnd0_d = nc.dram_tensor("rnd0_d", [T, D], BF16)
    wdq_i1 = nc.dram_tensor("wdq_i1", [768, D], BF16)
    wdq_o1 = nc.dram_tensor(
        "wdq_o1", [2 * I3, D], BF16,
        addr_space=maybe_share_collective_output_space("AllGather", RG_ALL))
    wdq_i2 = nc.dram_tensor("wdq_i2", [384, D], BF16)
    wdq_o2 = nc.dram_tensor(
        "wdq_o2", [I3, D], BF16,
        addr_space=maybe_share_collective_output_space("AllGather", RG_ALL))
    y1_d = nc.dram_tensor("y1_d", [T, D], F32)
    mg_d = nc.dram_tensor("mg_d", [I3, T], F32)
    mu_d = nc.dram_tensor("mu_d", [I3, T], F32)
    p_d = nc.dram_tensor("p_d", [I3, T], F32)
    q_d = nc.dram_tensor("q_d", [I3, T], F32)
    kvag_i = nc.dram_tensor("kvag_i", [D, D + 8], BF16)
    kvag_o = nc.dram_tensor(
        "kvag_o", [4 * D, D + 8], BF16,
        addr_space=maybe_share_collective_output_space("AllGather", RG_HALF))
    sc_i = nc.dram_tensor("sc_i", [1, 16], F32)
    sc_o = nc.dram_tensor(
        "sc_o", [1, 16], F32,
        addr_space=maybe_share_collective_output_space("AllReduce", RG_ALL))

    AL = nc.alloc_sbuf_tensor
    sqst = AL("sqst", [128, D], F32)
    t0st = AL("t0st", [128, D], F32)
    ln2st2 = AL("ln2st2", [128, 2 * D], F32)
    boutbf = AL("boutbf", [1, D], BF16)
    ones_row = AL("ones_row", [1, 128], BF16)
    ones_f = AL("ones_f", [1, 128], F32)
    ones_col = AL("ones_col", [128, 1], BF16)
    ksumbf = AL("ksumbf", [128, DC], BF16)
    mean2 = AL("mean2", [128, NT], F32)
    rstd2 = AL("rstd2", [128, NT], F32)
    rstd0 = AL("rstd0", [128, NT], F32)
    negm0 = AL("negm0", [128, NT], F32)
    stat = AL("stat", [128, 8], F32)
    am0 = AL("am0", [128, 1], F32)
    m1r = AL("m1r", [128, 1], F32)
    m2r = AL("m2r", [128, 1], F32)
    m3r = AL("m3r", [128, 1], F32)
    m4r = AL("m4r", [128, 1], F32)
    rc0 = AL("rc0", [128, 1], F32)
    c1s = AL("c1s", [128, 1], F32)
    c2s = AL("c2s", [128, 1], F32)
    c3s = AL("c3s", [128, 1], F32)
    c4s = AL("c4s", [128, 1], F32)
    S2s = AL("S2s", [128, 1], F32)
    nS2M = AL("nS2M", [128, 1], F32)
    S4s = AL("S4s", [128, 1], F32)
    tmps = AL("tmps", [128, 8], F32)
    s1c = AL("s1c", [128, NT], F32)
    s2c = AL("s2c", [128, NT], F32)
    mean1 = AL("mean1", [128, NT], F32)
    rstd1 = AL("rstd1", [128, NT], F32)
    rmaxc = AL("rmaxc", [128, NT], F32)
    rminc = AL("rminc", [128, NT], F32)
    c65k = AL("c65k", [128, 1], F32)
    arsc = AL("arsc", [128, 1], F32)
    epsln = AL("epsln", [128, 1], F32)
    magicv = AL("magicv", [128, 1], F32)

    MBASE = 40928
    used = 229344 - nc.sbuf_bytes_remaining
    assert used <= MBASE, f"smalls overflowed into manual pool: {used}"

    def at(name, shape, dtype, off_kb):
        return nc.alloc_sbuf_tensor_at(name, shape, dtype,
                                       offset=MBASE + int(off_kb * 1024))

    wqkvT = at("wqkvT", [128, DC, I3], BF16, 0)
    ln1T = at("ln1T", [128, DC, T], BF16, 48)
    kact = at("kact", [128, NT, D], BF16, 80)
    vbf = at("vbf", [128, NT, D], BF16, 112)
    qact = at("qact", [128, DC, T], BF16, 80)
    woutT = at("woutT", [128, DC, D], BF16, 112)
    kvbf = at("kvbf", [128, DC, D], BF16, 128)
    attnz = at("attnz", [128, DC, T], BF16, 0)
    zrow = at("zrow", [1, T], F32, 144)
    z_bc = at("z_bc", [128, T], F32, 152)
    y1sb = at("y1sb", [128, NT, D], F32, 32)
    XT = at("XT", [128, DC, T], BF16, 96)
    wgT = at("wgT", [128, DC, I3], BF16, 128)
    wuT = at("wuT", [128, DC, I3], BF16, 0)
    wdT = at("wdT", [128, IC, D], BF16, 96)
    o_lo = at("o_lo", [128, IC // 2, T], BF16, 0)
    o_hi = at("o_hi", [128, IC // 2, T], BF16, 48)
    x_all = at("x_all", [128, NT, D], F32, 32)
    xld = at("xld", [128, 2, D], F32, 144)
    ln1st = at("ln1st", [128, 2, D], BF16, 152)
    kvst = at("kvst", [128, 2, D + 8], BF16, 156)
    kvsum = at("kvsum", [128, 2, 4, D + 8], BF16, 0)
    wldst = at("wldst", [128, 2, 2, D], F32, 160)
    dqst = at("dqst", [128, 2, 2 * D], BF16, 176)
    rnd0st = at("rnd0st", [128, 2, D], BF16, 152)
    mst = at("mst", [128, 4, D], F32, 48)
    rdmg = at("rdmg", [128, 2, 2 * D], F32, 64)
    pwr = at("pwr", [128, 2, 2 * D], F32, 80)
    b2st = at("b2st", [128, 1, 2 * D], F32, 176)
    rdp = at("rdp", [128, 2, 2 * D], F32, 48)
    rdmu = at("rdmu", [128, 2, 2 * D], F32, 64)
    qwr = at("qwr", [128, 2, 2 * D], F32, 80)
    rdq = at("rdq", [128, 2, 2 * D], F32, 144)
    y1rd = at("y1rd", [128, 4, D], F32, 160)
    youts = at("youts", [128, 2, D], F32, 176)

    ps = [nc.alloc_psum_tensor(f"ps{i}", [128, D], F32) for i in range(3)]
    ps4 = nc.alloc_psum_tensor("ps4", [128, 512], F32)

    P = Prog()
    TE, DV, AC, GP = P.sem("TE"), P.sem("DV"), P.sem("AC"), P.sem("GP")
    CCK, CC0, CC2, CC13, CC4 = (P.sem(n) for n in
                                ("CCK", "CC0", "CC2", "CC13", "CC4"))
    GC = P.sem("GC")
    GA = P.sem("GA")

    fx = Fam(P, "fx", 2)
    flt = Fam(P, "flt", 1)
    fwq = Fam(P, "fwq", 1)
    fwo = Fam(P, "fwo", 1)
    fxt = Fam(P, "fxt", 1)
    fwg = Fam(P, "fwg", 1)
    fwu = Fam(P, "fwu", 1)
    fwd = Fam(P, "fwd", 1)
    fwl = Fam(P, "fwl", 2)
    fkv = Fam(P, "fkv", 2)
    fmg = Fam(P, "fmg", 2)
    frp = Fam(P, "frp", 2)
    fru = Fam(P, "fru", 2)
    frq = Fam(P, "frq", 2)
    fy1 = Fam(P, "fy1", 4)
    sln = Fam(P, "sln", 2)
    skv = Fam(P, "skv", 2)
    sdq = Fam(P, "sdq", 2)
    sy1 = Fam(P, "sy1", 2)
    sr0 = Fam(P, "sr0", 2)
    smt = Fam(P, "smt", 4)
    spw = Fam(P, "spw", 2)
    sqw = Fam(P, "sqw", 2)
    syo = Fam(P, "syo", 2)

    rot = {"i": -1, "rel": [(), (), ()]}

    def ps_acquire():
        rot["i"] += 1
        r = rot["i"] % 3
        w = rot["rel"][r]
        rot["rel"][r] = ()
        return ps[r], w

    def ps_release(sem, val):
        rot["rel"][rot["i"] % 3] = ((sem, val),)

    def mm(out, lh, rh, st, sp, waits=(), incs=(), skip=False):
        P.op("tensor",
             (lambda o, l, r, s1, s2, sk: lambda t: t.matmul(
                 o, l, r, start=s1, stop=s2, skip_group_check=sk))
             (out, lh, rh, st, sp, skip), waits=waits, incs=incs)

    for tns, val in ((ones_row, 1.0), (ones_f, 1.0), (ones_col, 1.0),
                     (am0, 0.0), (m1r, 0.0), (m2r, 0.0), (m3r, 0.0),
                     (m4r, 0.0), (kvst, 0.0), (epsln, 1e-5), (magicv, MAGIC), (c65k, 65536.0)):
        P.op("vector", (lambda t_, v_: lambda v: v.memset(t_[:], v_))(tns, val),
             writes=(tns.name,))

    P.op("gpsimd", lambda g: g.dma_start(out=boutbf[:], in_=bout_in),
         incs=((GC, 16),), writes=("boutbf",))
    GC.inc(16)
    for j in range(4):
        sl = slice(j * (I3 // 4), (j + 1) * (I3 // 4))
        P.op("gpsimd",
             (lambda sl: lambda g: g.dma_start(out=wqkv_bf_d[sl, :], in_=wqkv_in[sl, :]))(sl),
             incs=((GC, 16),))
        GC.inc(16)
    for j in range(2):
        sl = slice(j * (D // 2), (j + 1) * (D // 2))
        P.op("gpsimd",
             (lambda sl: lambda g: g.dma_start(out=wout_bf_d[sl, :], in_=wout_in[sl, :]))(sl),
             incs=((GC, 16),))
        GC.inc(16)
    g_cast = GC.val

    b2w = AL("b2w", [128, 3 * 64], F32)

    def dequant_chunk(src_ap, dst_ap, nr=2, first_waits=()):
        nb = nr * D // 32
        s, (lsem, ltgt) = fwl.dma(
            "sync", (lambda src_ap, nr: lambda e, s: e.dma_start(
                out=wldst[:, s, 0:nr, :], in_=src_ap))(src_ap, nr),
            extra_waits=first_waits)
        wv = wldst[:, s, 0:nr, :].rearrange("p a (b k) -> p (a b) k", k=32)
        P.op("vector",
             (lambda wv, nb: lambda v: v.tensor_reduce(
                 b2w[:, 0:nb], wv, AX.X, OP.max, apply_absolute_value=True))(wv, nb),
             waits=((lsem, ltgt),), reads=("wldst",), writes=("b2w",))
        P.op("vector",
             (lambda nb: lambda v: v.tensor_scalar(b2w[:, nb:2 * nb], b2w[:, 0:nb],
                                                   1.0 / 7, 1e-12, OP.mult, OP.max))(nb),
             reads=("b2w",), writes=("b2w",))
        P.op("vector",
             (lambda nb: lambda v: v.reciprocal(b2w[:, 2 * nb:3 * nb], b2w[:, nb:2 * nb]))(nb),
             reads=("b2w",), writes=("b2w",))
        P.op("vector",
             (lambda wv, nb, nr: lambda v: v.tensor_tensor(
                 ln2st2[:, 0:nr * D].rearrange("p (b k) -> p b k", k=32), wv,
                 b2w[:, 2 * nb:3 * nb].to_broadcast([128, nb, 32]), OP.mult))(wv, nb, nr),
             reads=("wldst", "b2w"), writes=("ln2st2",), incs=((DV, 1),))
        fwl.release((DV, DV.inc(1)))
        P.op("vector",
             (lambda nr: lambda v: v.tensor_scalar(ln2st2[:, 0:nr * D], ln2st2[:, 0:nr * D],
                                                   MAGIC, MAGIC, OP.add, OP.subtract))(nr),
             reads=("ln2st2",), writes=("ln2st2",))
        sd = (sdq.i + 1) % 2
        sdq.i += 1
        wfree2 = tuple(sdq.rel[sd])
        sdq.rel[sd] = []
        P.op("vector",
             (lambda sd, nb, nr: lambda v: v.tensor_tensor(
                 dqst[:, sd, 0:nr * D].rearrange("p (b k) -> p b k", k=32),
                 ln2st2[:, 0:nr * D].rearrange("p (b k) -> p b k", k=32),
                 b2w[:, nb:2 * nb].to_broadcast([128, nb, 32]), OP.mult))(sd, nb, nr),
             waits=wfree2, reads=("ln2st2", "b2w"), writes=("dqst",), incs=((DV, 1),))
        dvv = DV.inc(1)
        sem = sdq.sems[sd]
        tgt = sem.inc(16)
        P.op("scalar",
             (lambda sd, nr, dst_ap: lambda e: e.dma_start(
                 out=dst_ap, in_=dqst[:, sd, 0:nr * D]))(sd, nr, dst_ap),
             waits=((DV, dvv),), incs=((sem, 16),), reads=("dqst",))
        sdq.rel[sd] = [(sem, tgt)]

    def rows2(w_dram, a, nr=2):
        return w_dram.rearrange("(a p) d -> p a d", p=128)[:, 2 * a:2 * a + nr, :]
    CCW1 = P.sem("CCW1")

    x_t = x_in.rearrange("(n p) d -> n p d", p=128)
    x_all3 = x_in.rearrange("(n p) d -> p n d", p=128)
    _, (lxs, lxt) = fx.dma(
        "sync", lambda e, s: e.dma_start(out=x_all[:], in_=x_all3), writes=("x_all",))
    for tt in range(NT):
        P.op("vector",
             (lambda tt: lambda v: v.reduce_sum(s1c[:, tt:tt + 1], x_all[:, tt, :], AX.X))(tt),
             waits=((lxs, lxt),) if tt == 0 else (), reads=("x_all",), writes=("s1c",))
        P.op("scalar",
             (lambda tt: lambda e: e.activation(sqst[:], x_all[:, tt, :], AF.Square,
                                                accum_out=s2c[:, tt:tt + 1]))(tt),
             waits=((lxs, lxt),) if tt == 0 else (), reads=("x_all",),
             writes=("sqst", "s2c"),
             incs=((AC, 1),) if tt == NT - 1 else ())
    acv = AC.inc(1)
    for a in range(3):
        dequant_chunk(rows2(wsh1_in, a),
                      wdq_i1.rearrange("(a p) d -> p a d", p=128)[:, 2 * a:2 * a + 2, :])
    w_dq1 = sdq.totals()
    P.op("gpsimd", lambda g: g.collective_compute(
        "AllGather", OP.bypass, replica_groups=RG_ALL,
        ins=[wdq_i1[:, :].opt()], outs=[wdq_o1[:, :].opt()]),
        waits=w_dq1, incs=((CCW1, 1),))
    CCW1.inc(1)
    dequant_chunk(rows2(wsh2_in, 0),
                  wdq_i2.rearrange("(a p) d -> p a d", p=128)[:, 0:2, :])
    dequant_chunk(rows2(wsh2_in, 1, nr=1),
                  wdq_i2.rearrange("(a p) d -> p a d", p=128)[:, 2:3, :], nr=1)
    w_dq2 = sdq.totals()
    P.op("vector", lambda v: v.tensor_scalar(mean1[:], s1c[:], 1.0 / D, None, OP.mult),
         waits=((AC, acv),), reads=("s1c",), writes=("mean1",))
    P.op("vector", lambda v: v.tensor_scalar(s2c[:], s2c[:], 1.0 / D, None, OP.mult),
         reads=("s2c",), writes=("s2c",))
    P.op("vector", lambda v: v.tensor_tensor(s1c[:], mean1[:], mean1[:], OP.mult),
         reads=("mean1",), writes=("s1c",))
    P.op("vector", lambda v: v.tensor_tensor(s2c[:], s2c[:], s1c[:], OP.subtract),
         reads=("s2c", "s1c"), writes=("s2c",), incs=((DV, 1),))
    dvv = DV.inc(1)
    P.op("scalar", lambda e: e.activation(s2c[:], s2c[:], AF.Sqrt, bias=epsln[:]),
         waits=((DV, dvv),), reads=("s2c", "epsln"), writes=("s2c",), incs=((AC, 1),))
    acv = AC.inc(1)
    P.op("vector", lambda v: v.reciprocal(rstd1[:], s2c[:]),
         waits=((AC, acv),), reads=("s2c",), writes=("rstd1",), incs=((DV, 1),))
    DV.inc(1)
    for tt in range(NT):
        sl2 = tt % 2
        wfree2 = tuple(sln.rel[sl2])
        sln.rel[sl2] = []
        P.op("vector",
             (lambda tt, sl2: lambda v: v.tensor_scalar(ln1st[:, sl2, :], x_all[:, tt, :],
                                                        mean1[:, tt:tt + 1], rstd1[:, tt:tt + 1],
                                                        OP.subtract, OP.mult))(tt, sl2),
             waits=wfree2, reads=("x_all", "mean1", "rstd1"), writes=("ln1st",), incs=((DV, 1),))
        dvv = DV.inc(1)
        sem = sln.sems[sl2]
        tgt = sem.inc(16)
        P.op("scalar",
             (lambda tt, sl2: lambda e: e.dma_start(out=ln1_d[tt * 128:(tt + 1) * 128, :],
                                                    in_=ln1st[:, sl2, :]))(tt, sl2),
             waits=((DV, dvv),), incs=((sem, 16),), reads=("ln1st",))
        sln.i = tt
        sln.rel[sl2] = [(sem, tgt)]
    dv_ln1 = DV.val
    w_ln1d = sln.totals()

    for dc in range(DC):
        flt.dma("sync",
                (lambda dc: lambda e, s: e.dma_start(
                    out=ln1T[:, dc, :], in_=ln1_d[:, dc * 128:(dc + 1) * 128],
                    transpose=True))(dc),
                extra_waits=w_ln1d if dc == 0 else ())
    w_ln1T = flt.totals()
    for dc in range(DC):
        fwq.dma("sync",
                (lambda dc: lambda e, s: e.dma_start(
                    out=wqkvT[:, dc, :], in_=wqkv_bf_d[:, dc * 128:(dc + 1) * 128],
                    transpose=True))(dc),
                extra_waits=((GC, g_cast),) if dc == 0 else ())
    w_wqkvT = fwq.totals()

    for tt in range(NT):
        rk, wk = ps_acquire()
        rv, wv = ps_acquire()
        w0 = w_ln1T + w_wqkvT + wk + wv
        for dc in range(DC):
            lh = ln1T[:, dc, tt * 128:(tt + 1) * 128]
            st, sp = dc == 0, dc == DC - 1
            for hf in range(2):
                mm(rk[:, hf * 512:(hf + 1) * 512], lh,
                   wqkvT[:, dc, D + hf * 512:D + (hf + 1) * 512], st, sp,
                   waits=w0 if (dc == 0 and hf == 0) else (), skip=True)
                w0 = ()
                incs = ()
                if sp and hf == 1:
                    incs = ((TE, 1),)
                    TE.inc(1)
                mm(rv[:, hf * 512:(hf + 1) * 512], lh,
                   wqkvT[:, dc, 2 * D + hf * 512:2 * D + (hf + 1) * 512], st, sp,
                   incs=incs, skip=True)
        tev = TE.val
        P.op("vector",
             (lambda tt, rk: lambda v: v.tensor_scalar(kact[:, tt, :], rk[:], 0.0, 1e-6,
                                                       OP.max, OP.add))(tt, rk),
             waits=((TE, tev),), writes=("kact",), incs=((DV, 1),))
        dv_k = DV.inc(1)
        P.op("scalar",
             (lambda tt, rv: lambda e: e.copy(vbf[:, tt, :], rv[:]))(tt, rv),
             waits=((TE, tev),), writes=("vbf",), incs=((AC, 1),))
        ac_v = AC.inc(1)
        rot["rel"][(rot["i"] - 1) % 3] = ((DV, dv_k),)
        rot["rel"][rot["i"] % 3] = ((AC, ac_v),)
    dv_2a, ac_2a = DV.val, AC.val

    for dc in range(DC):
        r, wfree = ps_acquire()
        for tt in range(NT):
            lh = kact[:, tt, dc * 128:(dc + 1) * 128]
            st, sp = tt == 0, tt == NT - 1
            w0 = ((DV, dv_2a), (AC, ac_2a)) + wfree if tt == 0 else ()
            mm(r[:, 0:512], lh, vbf[:, tt, 0:512], st, sp, waits=w0, skip=True)
            mm(r[:, 512:1024], lh, vbf[:, tt, 512:1024], st, sp, skip=True)
            incs = ()
            if sp:
                incs = ((TE, 1),)
                TE.inc(1)
            mm(ps4[:, dc:dc + 1], lh, ones_col[:], st, sp, incs=incs, skip=True)
        tev = TE.val
        sl2 = dc % 2
        wfree2 = tuple(skv.rel[sl2])
        skv.rel[sl2] = []
        P.op("vector",
             (lambda sl2, r: lambda v: v.tensor_copy(kvst[:, sl2, 0:D], r[:]))(sl2, r),
             waits=((TE, tev),) + wfree2, writes=("kvst",))
        P.op("vector",
             (lambda sl2, dc: lambda v: v.tensor_copy(kvst[:, sl2, D:D + 1], ps4[:, dc:dc + 1]))(sl2, dc),
             writes=("kvst",), incs=((DV, 1),))
        dvv = DV.inc(1)
        ps_release(DV, dvv)
        sem = skv.sems[sl2]
        tgt = sem.inc(16)
        P.op("scalar",
             (lambda sl2, dc: lambda e: e.dma_start(out=kvag_i[dc * 128:(dc + 1) * 128, :],
                                                    in_=kvst[:, sl2, :]))(sl2, dc),
             waits=((DV, dvv),), incs=((sem, 16),), reads=("kvst",))
        skv.i = dc
        skv.rel[sl2] = [(sem, tgt)]
    w_kvag = skv.totals()
    P.op("gpsimd", lambda g: g.collective_compute(
        "AllGather", OP.bypass, replica_groups=RG_HALF,
        ins=[kvag_i[:, :].opt()], outs=[kvag_o[:, :].opt()]),
        waits=w_kvag, incs=((CCK, 1),))
    CCK.inc(1)
    CCW2 = P.sem("CCW2")
    P.op("gpsimd", lambda g: g.collective_compute(
        "AllGather", OP.bypass, replica_groups=RG_ALL,
        ins=[wdq_i2[:, :].opt()], outs=[wdq_o2[:, :].opt()]),
        waits=w_dq2, incs=((CCW2, 1),))
    CCW2.inc(1)

    for do in range(DC):
        for th in range(2):
            r, wfree = ps_acquire()
            for dc in range(DC):
                st, sp = dc == 0, dc == DC - 1
                for hq in range(2):
                    w0 = wfree if (dc == 0 and hq == 0) else ()
                    incs = ()
                    if sp and hq == 1:
                        incs = ((TE, 1),)
                        TE.inc(1)
                    mm(r[:, hq * 512:(hq + 1) * 512],
                       wqkvT[:, dc, do * 128:(do + 1) * 128],
                       ln1T[:, dc, th * 1024 + hq * 512:th * 1024 + (hq + 1) * 512],
                       st, sp, waits=w0, incs=incs)
            tev = TE.val
            P.op("vector",
                 (lambda do, th, r: lambda v: v.tensor_scalar(
                     qact[:, do, th * 1024:(th + 1) * 1024], r[:], 0.0, 1e-6,
                     OP.max, OP.add))(do, th, r),
                 waits=((TE, tev),), writes=("qact",), incs=((DV, 1),))
            ps_release(DV, DV.inc(1))
    dv_q = DV.val

    for dc in range(DC):
        fwo.dma("sync",
                (lambda dc: lambda e, s: e.dma_start(
                    out=woutT[:, dc, :], in_=wout_bf_d[:, dc * 128:(dc + 1) * 128],
                    transpose=True))(dc),
                extra_waits=(((GC, g_cast),) + w_kvag) if dc == 0 else ())
    w_woutT = fwo.totals()

    GS = P.sem("GS")
    kvo = kvag_o.rearrange("(r c p) e -> p r c e", p=128, c=DC)
    for dc in range(DC):
        s, (lsem, ltgt) = fkv.dma(
            "sync", (lambda dc: lambda e, s: e.dma_start(out=kvsum[:, s], in_=kvo[:, :, dc, :]))(dc),
            extra_waits=((CCK, 1), (DV, dv_q)) if dc == 0 else ())
        P.op("vector",
             (lambda s: lambda v: v.tensor_add(kvsum[:, s, 0, :], kvsum[:, s, 0, :],
                                               kvsum[:, s, 1, :]))(s),
             waits=((lsem, ltgt),), reads=("kvsum",), writes=("kvsum",))
        P.op("vector",
             (lambda s: lambda v: v.tensor_add(kvsum[:, s, 2, :], kvsum[:, s, 2, :],
                                               kvsum[:, s, 3, :]))(s),
             reads=("kvsum",), writes=("kvsum",))
        P.op("vector",
             (lambda dc, s: lambda v: v.tensor_add(kvbf[:, dc, :], kvsum[:, s, 0, 0:D],
                                                   kvsum[:, s, 2, 0:D]))(dc, s),
             reads=("kvsum",), writes=("kvbf",))
        P.op("vector",
             (lambda dc, s: lambda v: v.tensor_add(ksumbf[:, dc:dc + 1], kvsum[:, s, 0, D:D + 1],
                                                   kvsum[:, s, 2, D:D + 1]))(dc, s),
             reads=("kvsum",), writes=("ksumbf",), incs=((GS, 1),))
        fkv.release((GS, GS.inc(1)))
    gs_kv = GS.val

    for tq in range(4):
        for dc in range(DC):
            st, sp = dc == 0, dc == DC - 1
            if tq == 0 and dc == 0:
                w0 = ((GS, gs_kv),)
            elif dc == 0:
                w0 = ((DV, DV.val),)
            else:
                w0 = ()
            incs = ()
            if sp:
                incs = ((TE, 1),)
                TE.inc(1)
            mm(ps4[0:1, 0:512], ksumbf[:, dc:dc + 1],
               qact[:, dc, tq * 512:(tq + 1) * 512], st, sp, waits=w0, incs=incs,
               skip=True)
        P.op("vector",
             (lambda tq: lambda v: v.tensor_scalar(zrow[0:1, tq * 512:(tq + 1) * 512],
                                                   ps4[0:1, 0:512], 1e-6, None, OP.add))(tq),
             waits=((TE, TE.val),), writes=("zrow",), incs=((DV, 1),))
        DV.inc(1)
    P.op("vector", lambda v: v.reciprocal(zrow[:], zrow[:]),
         reads=("zrow",), writes=("zrow",), incs=((DV, 1),))
    dv_z = DV.inc(1)
    for tq in range(4):
        TE.inc(1)
        mm(ps4[:, 0:512], ones_f[:], zrow[0:1, tq * 512:(tq + 1) * 512], True, True,
           waits=((DV, dv_z if tq == 0 else DV.val),), incs=((TE, 1),))
        P.op("vector",
             (lambda tq: lambda v: v.tensor_copy(z_bc[:, tq * 512:(tq + 1) * 512], ps4[:, 0:512]))(tq),
             waits=((TE, TE.val),), writes=("z_bc",), incs=((DV, 1),))
        DV.inc(1)
    dv_zbc = DV.val

    for ec in range(DC):
        for th in range(2):
            r, wfree = ps_acquire()
            for dc in range(DC):
                st, sp = dc == 0, dc == DC - 1
                for hq in range(2):
                    w0 = (((DV, dv_zbc),) + wfree) if (dc == 0 and hq == 0) else ()
                    incs = ()
                    if sp and hq == 1:
                        incs = ((TE, 1),)
                        TE.inc(1)
                    mm(r[:, hq * 512:(hq + 1) * 512],
                       kvbf[:, dc, ec * 128:(ec + 1) * 128],
                       qact[:, dc, th * 1024 + hq * 512:th * 1024 + (hq + 1) * 512],
                       st, sp, waits=w0, incs=incs)
            P.op("vector",
                 (lambda ec, th, r: lambda v: v.tensor_tensor(
                     attnz[:, ec, th * 1024:(th + 1) * 1024], r[:],
                     z_bc[:, th * 1024:(th + 1) * 1024], OP.mult))(ec, th, r),
                 waits=((TE, TE.val),), reads=("z_bc",), writes=("attnz",), incs=((DV, 1),))
            ps_release(DV, DV.inc(1))
    dv_attnz = DV.val

    for tt in range(NT):
        s, (lsem, ltgt) = fx.dma(
            "sync", (lambda tt: lambda e, s: e.dma_start(out=xld[:, s, :], in_=x_t[tt]))(tt),
            extra_waits=((DV, dv_attnz),) if tt == 0 else ())
        r, wfree2 = ps_acquire()
        for ec in range(DC):
            st = ec == 0
            for hf in range(2):
                w0 = (((DV, dv_attnz),) + w_woutT + wfree2) if (ec == 0 and hf == 0) else ()
                mm(r[:, hf * 512:(hf + 1) * 512],
                   attnz[:, ec, tt * 128:(tt + 1) * 128],
                   woutT[:, ec, hf * 512:(hf + 1) * 512], st, False, waits=w0, skip=True)
        for hf in range(2):
            incs = ()
            if hf == 1:
                incs = ((TE, 1),)
                TE.inc(1)
            mm(r[:, hf * 512:(hf + 1) * 512], ones_row[:],
               boutbf[0:1, hf * 512:(hf + 1) * 512], False, True, incs=incs, skip=True)
        tev = TE.val
        P.op("vector",
             (lambda tt, s, r: lambda v: v.tensor_tensor(y1sb[:, tt, :], r[:], xld[:, s, :], OP.add))(tt, s, r),
             waits=((TE, tev), (lsem, ltgt)), reads=("xld",), writes=("y1sb",), incs=((DV, 1),))
        dv_y1 = DV.inc(1)
        fx.release((DV, dv_y1))
        ps_release(DV, dv_y1)
        sl2 = tt % 2
        wfree3 = tuple(sy1.rel[sl2])
        sy1.rel[sl2] = []
        sem = sy1.sems[sl2]
        tgt = sem.inc(16)
        P.op("scalar",
             (lambda tt: lambda e: e.dma_start(out=y1_d[tt * 128:(tt + 1) * 128, :],
                                               in_=y1sb[:, tt, :]))(tt),
             waits=((DV, dv_y1),) + wfree3, incs=((sem, 16),), reads=("y1sb",))
        sy1.i = tt
        sy1.rel[sl2] = [(sem, tgt)]
        P.op("scalar",
             (lambda tt: lambda e: e.activation(sqst[:], y1sb[:, tt, :], AF.Square,
                                                accum_out=s2c[:, tt:tt + 1]))(tt),
             waits=((DV, dv_y1),), reads=("y1sb",), writes=("sqst", "s2c"),
             incs=((AC, 1),) if tt == NT - 1 else ())
        P.op("vector",
             (lambda tt: lambda v: v.reduce_sum(s1c[:, tt:tt + 1], y1sb[:, tt, :], AX.X))(tt),
             reads=("y1sb",), writes=("s1c",))
        P.op("vector",
             (lambda tt: lambda v: v.tensor_reduce(rmaxc[:, tt:tt + 1], y1sb[:, tt, :],
                                                   AX.X, OP.max))(tt),
             reads=("y1sb",), writes=("rmaxc",))
        P.op("vector",
             (lambda tt: lambda v: v.tensor_reduce(rminc[:, tt:tt + 1], y1sb[:, tt, :],
                                                   AX.X, OP.min))(tt),
             reads=("y1sb",), writes=("rminc",))
    ac_sq = AC.inc(1)
    P.op("vector", lambda v: v.tensor_scalar(mean2[:], s1c[:], 1.0 / D, None, OP.mult),
         waits=((AC, ac_sq),), reads=("s1c",), writes=("mean2",))
    P.op("vector", lambda v: v.tensor_scalar(s2c[:], s2c[:], 1.0 / D, None, OP.mult),
         reads=("s2c",), writes=("s2c",))
    P.op("vector", lambda v: v.tensor_tensor(s1c[:], mean2[:], mean2[:], OP.mult),
         reads=("mean2",), writes=("s1c",))
    P.op("vector", lambda v: v.tensor_tensor(s2c[:], s2c[:], s1c[:], OP.subtract),
         reads=("s2c", "s1c"), writes=("s2c",), incs=((DV, 1),))
    dvv = DV.inc(1)
    P.op("scalar", lambda e: e.activation(s2c[:], s2c[:], AF.Sqrt, bias=epsln[:]),
         waits=((DV, dvv),), reads=("s2c", "epsln"), writes=("s2c",), incs=((AC, 1),))
    acv = AC.inc(1)
    P.op("vector", lambda v: v.reciprocal(rstd2[:], s2c[:]),
         waits=((AC, acv),), reads=("s2c",), writes=("rstd2",))
    P.op("vector", lambda v: v.tensor_tensor(rmaxc[:], rmaxc[:], mean2[:], OP.subtract),
         reads=("rmaxc", "mean2"), writes=("rmaxc",))
    P.op("vector", lambda v: v.tensor_tensor(rminc[:], mean2[:], rminc[:], OP.subtract),
         reads=("rminc", "mean2"), writes=("rminc",))
    P.op("vector", lambda v: v.tensor_tensor(rmaxc[:], rmaxc[:], rminc[:], OP.max),
         reads=("rmaxc", "rminc"), writes=("rmaxc",))
    P.op("vector", lambda v: v.tensor_tensor(rmaxc[:], rmaxc[:], rstd2[:], OP.mult),
         reads=("rmaxc", "rstd2"), writes=("rmaxc",))
    P.op("vector", lambda v: v.reduce_max(am0[:], rmaxc[:], AX.X),
         reads=("rmaxc",), writes=("am0",), incs=((DV, 1),))
    DV.inc(1)
    dv_s0 = DV.val
    w_y1d = sy1.totals()

    def scalar_ar(cols, maxes, wait, ccsem):
        for j, (ap_, nm) in enumerate(maxes):
            P.op("gpsimd",
                 (lambda ap_: lambda g: g.tensor_reduce(arsc[0:1, 0:1], ap_, AX.XYZWC, OP.max))(ap_),
                 waits=(wait,) if j == 0 else (), reads=(nm,), writes=("arsc",))
            P.op("gpsimd",
                 (lambda cols, j: lambda g: g.dma_start(
                     out=sc_i[0:1, cols + j:cols + j + 1], in_=arsc[0:1, 0:1]))(cols, j),
                 incs=((GA, 16),), reads=("arsc",))
            P.op("gpsimd", lambda g: None, waits=((GA, GA.inc(16)),))
        n = len(maxes)
        P.op("gpsimd", (lambda n, cols: lambda g: g.collective_compute(
            "AllReduce", OP.max, replica_groups=RG_ALL,
            ins=[sc_i[0:1, cols:cols + n].opt()],
            outs=[sc_o[0:1, cols:cols + n].opt()]))(n, cols),
            incs=((ccsem, 1),))
        return ccsem.inc(1)

    def bcast(cols, j, dst, dname, ccsem, ccw):
        P.op("gpsimd",
             (lambda dst, cols, j: lambda g: g.dma_start(
                 out=dst, in_=sc_o[0:1, cols + j:cols + j + 1].to_broadcast([128, 1])))(dst, cols, j),
             waits=((ccsem, ccw), (GA, GA.val)), incs=((GA, 16),), writes=(dname,))
        return (GA, GA.inc(16))

    ccw = scalar_ar(0, [(am0[:], "am0")], (DV, dv_s0), CC0)
    gw = bcast(0, 0, tmps[:, 0:1], "tmps", CC0, ccw)
    P.op("vector", lambda v: v.tensor_scalar(tmps[:, 1:2], tmps[:, 0:1], 1.0 / QMAX, 1e-12,
                                             OP.mult, OP.max),
         waits=(gw,), reads=("tmps",), writes=("tmps",))
    P.op("vector", lambda v: v.reciprocal(rc0[:], tmps[:, 1:2]),
         reads=("tmps",), writes=("rc0",), incs=((DV, 1),))
    dv_rc0 = DV.inc(1)

    P.op("vector", lambda v: v.tensor_tensor(rstd0[:], rstd2[:],
                                             rc0[:].to_broadcast([128, NT]), OP.mult),
         waits=((DV, dv_rc0),), reads=("rstd2", "rc0"), writes=("rstd0",))
    P.op("vector", lambda v: v.tensor_tensor(negm0[:], mean2[:], rstd0[:], OP.mult),
         reads=("mean2", "rstd0"), writes=("negm0",))
    P.op("vector", lambda v: v.tensor_scalar(negm0[:], negm0[:], -1.0, MAGIC, OP.mult, OP.add),
         reads=("negm0",), writes=("negm0",), incs=((DV, 1),))
    dv_q0 = DV.inc(1)
    for tt in range(NT):
        P.op("scalar",
             (lambda tt: lambda e: e.activation(t0st[:], y1sb[:, tt, :], AF.Identity,
                                                bias=negm0[:, tt:tt + 1],
                                                scale=rstd0[:, tt:tt + 1]))(tt),
             waits=((DV, dv_q0 if tt == 0 else DV.val),),
             reads=("y1sb", "negm0", "rstd0"), writes=("t0st",), incs=((AC, 1),))
        acv = AC.inc(1)
        sl2 = tt % 2
        wfree = tuple(sr0.rel[sl2])
        sr0.rel[sl2] = []
        P.op("vector",
             (lambda sl2: lambda v: v.tensor_scalar(rnd0st[:, sl2, :], t0st[:], MAGIC, None,
                                                    OP.subtract))(sl2),
             waits=((AC, acv),) + wfree, reads=("t0st",), writes=("rnd0st",), incs=((DV, 1),))
        dvv = DV.inc(1)
        sem = sr0.sems[sl2]
        tgt = sem.inc(16)
        P.op("scalar",
             (lambda tt, sl2: lambda e: e.dma_start(out=rnd0_d[tt * 128:(tt + 1) * 128, :],
                                                    in_=rnd0st[:, sl2, :]))(tt, sl2),
             waits=((DV, dvv),), incs=((sem, 16),), reads=("rnd0st",))
        sr0.i = tt
        sr0.rel[sl2] = [(sem, tgt)]
    w_rnd0 = sr0.totals()

    for dc in range(DC):
        fxt.dma("sync",
                (lambda dc: lambda e, s: e.dma_start(
                    out=XT[:, dc, :], in_=rnd0_d[:, dc * 128:(dc + 1) * 128],
                    transpose=True))(dc),
                extra_waits=w_rnd0 if dc == 0 else ())
    w_XT = fxt.totals()
    for hc in range(DC):
        fwg.dma("sync",
                (lambda hc: lambda e, s: e.dma_start(
                    out=wgT[:, hc, :], in_=wdq_o1[0:I3, hc * 128:(hc + 1) * 128],
                    transpose=True))(hc),
                extra_waits=((CCW1, 1), (DV, dv_attnz)) if hc == 0 else ())
    w_wgT = fwg.totals()
    for hc in range(DC):
        fwu.dma("sync",
                (lambda hc: lambda e, s: e.dma_start(
                    out=wuT[:, hc, :], in_=wdq_o1[I3:2 * I3, hc * 128:(hc + 1) * 128],
                    transpose=True))(hc),
                extra_waits=(((CCW1, 1),) + w_y1d) if hc == 0 else ())
    w_wuT = fwu.totals()

    def ffn_mm_step(wT, w_w, ic, th, dst_d, mrun, mname):
        r, wfree = ps_acquire()
        for hc in range(DC):
            st, sp = hc == 0, hc == DC - 1
            for hq in range(2):
                w0 = (w_w + w_XT + wfree) if (hc == 0 and hq == 0) else ()
                incs = ()
                if sp and hq == 1:
                    incs = ((TE, 1),)
                    TE.inc(1)
                mm(r[:, hq * 512:(hq + 1) * 512], wT[:, hc, ic * 128:(ic + 1) * 128],
                   XT[:, hc, th * 1024 + hq * 512:th * 1024 + (hq + 1) * 512],
                   st, sp, waits=w0, incs=incs)
        tev = TE.val
        sm = (smt.i + 1) % 4
        smt.i += 1
        wfree2 = tuple(smt.rel[sm])
        smt.rel[sm] = []
        P.op("scalar",
             (lambda sm, r: lambda e: e.copy(mst[:, sm, :], r[:]))(sm, r),
             waits=((TE, tev),) + wfree2, writes=("mst",), incs=((AC, 1),))
        acv = AC.inc(1)
        ps_release(AC, acv)
        P.op("vector",
             (lambda sm: lambda v: v.tensor_reduce(tmps[:, 1:2], mst[:, sm, :], AX.X,
                                                   OP.max, apply_absolute_value=True))(sm),
             waits=((AC, acv),), reads=("mst",), writes=("tmps",))
        P.op("vector",
             (lambda m: lambda v: v.tensor_tensor(m, m, tmps[:, 1:2], OP.max))(mrun),
             reads=("tmps", mname), writes=(mname,), incs=((DV, 1),))
        dvm = DV.inc(1)
        sem = smt.sems[sm]
        tgt = sem.inc(16)
        P.op("scalar",
             (lambda ic, th, sm: lambda e: e.dma_start(
                 out=dst_d[ic * 128:(ic + 1) * 128, th * 1024:(th + 1) * 1024],
                 in_=mst[:, sm, :]))(ic, th, sm),
             incs=((sem, 16),), reads=("mst",))
        smt.rel[sm] = [(sem, tgt), (DV, dvm)]

    for ic in range(IC):
        for th in range(2):
            ffn_mm_step(wgT, w_wgT, ic, th, mg_d, m2r[:], "m2r")
    dv_m2 = DV.val
    w_mg = smt.totals()

    ccw = scalar_ar(4, [(m2r[:], "m2r")], (DV, dv_m2), CC2)
    gw = bcast(4, 0, tmps[:, 2:3], "tmps", CC2, ccw)
    P.op("vector", lambda v: v.tensor_scalar(tmps[:, 3:4], tmps[:, 2:3], 1e-30, None, OP.max),
         waits=(gw,), reads=("tmps",), writes=("tmps",))
    P.op("vector", lambda v: v.reciprocal(c2s[:], tmps[:, 3:4]), reads=("tmps",), writes=("c2s",))
    P.op("vector", lambda v: v.tensor_scalar(c2s[:], c2s[:], QMAX, None, OP.mult),
         reads=("c2s",), writes=("c2s",))
    P.op("vector", lambda v: v.reciprocal(tmps[:, 4:5], rc0[:]), reads=("rc0",), writes=("tmps",))
    P.op("vector", lambda v: v.tensor_tensor(S2s[:], tmps[:, 2:3], tmps[:, 4:5], OP.mult),
         reads=("tmps",), writes=("S2s",))
    P.op("vector", lambda v: v.tensor_scalar(S2s[:], S2s[:], 1.0 / QMAX, None, OP.mult),
         reads=("S2s",), writes=("S2s",))
    P.op("vector", lambda v: v.tensor_scalar(nS2M[:], S2s[:], -MAGIC, None, OP.mult),
         reads=("S2s",), writes=("nS2M",), incs=((DV, 1),))
    dv_s2 = DV.inc(1)

    b2_free = ()
    for ic in range(IC):
        for th in range(2):
            ffn_mm_step(wuT, w_wuT, ic, th, mu_d, m1r[:], "m1r")
        s, (lsem, ltgt) = fmg.dma(
            "sync", (lambda ic: lambda e, s: e.dma_start(
                out=rdmg[:, s, :], in_=mg_d[ic * 128:(ic + 1) * 128, :]))(ic),
            extra_waits=w_mg if ic == 0 else ())
        P.op("scalar",
             (lambda s: lambda e: e.activation(rdmg[:, s, :], rdmg[:, s, :], AF.Identity,
                                               scale=c2s[:], bias=magicv[:]))(s),
             waits=((lsem, ltgt), (DV, dv_s2)), reads=("rdmg", "c2s", "magicv"),
             writes=("rdmg",))
        for hf in range(2):
            P.op("scalar",
                 (lambda s, hf: lambda e: e.activation(
                     b2st[:, 0, hf * D:(hf + 1) * D], rdmg[:, s, hf * D:(hf + 1) * D],
                     AF.Sigmoid, bias=nS2M[:], scale=S2s[:]))(s, hf),
                 waits=(b2_free if hf == 0 else ()),
                 reads=("rdmg", "nS2M", "S2s"), writes=("b2st",))
        P.op("scalar",
             lambda e: e.activation(b2st[:, 0, :], b2st[:, 0, :], AF.Identity,
                                    scale=c65k[:], bias=magicv[:]),
             reads=("b2st", "c65k", "magicv"), writes=("b2st",), incs=((AC, 2),))
        ac_sig = AC.inc(2)
        sp2 = (spw.i + 1) % 2
        spw.i += 1
        wfree2 = tuple(spw.rel[sp2])
        spw.rel[sp2] = []
        P.op("vector",
             lambda v: v.tensor_scalar(b2st[:, 0, :], b2st[:, 0, :], MAGIC, None, OP.subtract),
             waits=((AC, ac_sig),), reads=("b2st",), writes=("b2st",))
        P.op("vector",
             (lambda s, sp2: lambda v: v.scalar_tensor_tensor(
                 pwr[:, sp2, :], rdmg[:, s, :], MAGIC, b2st[:, 0, :],
                 OP.subtract, OP.mult))(s, sp2),
             waits=wfree2, reads=("rdmg", "b2st"), writes=("pwr",), incs=((DV, 1),))
        dvp1 = DV.inc(1)
        fmg.release((DV, dvp1))
        b2_free = ((DV, dvp1),)
        P.op("vector",
             (lambda sp2: lambda v: v.tensor_reduce(tmps[:, 2:3], pwr[:, sp2, :], AX.X,
                                                    OP.max, apply_absolute_value=True))(sp2),
             reads=("pwr",), writes=("tmps",))
        P.op("vector", lambda v: v.tensor_tensor(m3r[:], m3r[:], tmps[:, 2:3], OP.max),
             reads=("tmps", "m3r"), writes=("m3r",), incs=((DV, 1),))
        dvp = DV.inc(1)
        sem = spw.sems[sp2]
        tgt = sem.inc(16)
        P.op("scalar",
             (lambda ic, sp2: lambda e: e.dma_start(out=p_d[ic * 128:(ic + 1) * 128, :],
                                                    in_=pwr[:, sp2, :]))(ic, sp2),
             waits=((DV, dvp1),), incs=((sem, 16),), reads=("pwr",))
        spw.rel[sp2] = [(sem, tgt), (DV, dvp)]
    dv_m13 = DV.val
    w_mu = smt.totals()
    w_p = spw.totals()

    ccw = scalar_ar(8, [(m1r[:], "m1r"), (m3r[:], "m3r")], (DV, dv_m13), CC13)
    gw = bcast(8, 0, tmps[:, 0:1], "tmps", CC13, ccw)
    gw2 = bcast(8, 1, tmps[:, 1:2], "tmps", CC13, ccw)
    P.op("vector", lambda v: v.tensor_scalar(tmps[:, 2:3], tmps[:, 0:1], 1e-30, None, OP.max),
         waits=(gw, gw2), reads=("tmps",), writes=("tmps",))
    P.op("vector", lambda v: v.reciprocal(c1s[:], tmps[:, 2:3]), reads=("tmps",), writes=("c1s",))
    P.op("vector", lambda v: v.tensor_scalar(c1s[:], c1s[:], QMAX, None, OP.mult),
         reads=("c1s",), writes=("c1s",))
    P.op("vector", lambda v: v.tensor_scalar(tmps[:, 3:4], tmps[:, 1:2], 1e-30, None, OP.max),
         reads=("tmps",), writes=("tmps",))
    P.op("vector", lambda v: v.reciprocal(c3s[:], tmps[:, 3:4]), reads=("tmps",), writes=("c3s",))
    P.op("vector", lambda v: v.tensor_scalar(c3s[:], c3s[:], QMAX, None, OP.mult),
         reads=("c3s",), writes=("c3s",))
    P.op("vector", lambda v: v.reciprocal(tmps[:, 4:5], rc0[:]), reads=("rc0",), writes=("tmps",))
    P.op("vector", lambda v: v.tensor_tensor(S4s[:], tmps[:, 0:1], tmps[:, 4:5], OP.mult),
         reads=("tmps",), writes=("S4s",))
    P.op("vector", lambda v: v.tensor_tensor(S4s[:], S4s[:], tmps[:, 1:2], OP.mult),
         reads=("S4s", "tmps"), writes=("S4s",))
    P.op("vector", lambda v: v.tensor_tensor(S4s[:], S4s[:], S2s[:], OP.mult),
         reads=("S4s", "S2s"), writes=("S4s",))
    P.op("vector", lambda v: v.tensor_scalar(S4s[:], S4s[:], 1.0 / (65536.0 * QMAX * QMAX * QMAX),
                                             None, OP.mult),
         reads=("S4s",), writes=("S4s",), incs=((DV, 1),))
    dv_c13 = DV.inc(1)

    wd_dq_v = wdq_o2.rearrange("(o t) d -> o (t d)", t=3)
    b2_free = ()
    for ic in range(IC):
        s, (lps, lpt) = frp.dma(
            "sync", (lambda ic: lambda e, s: e.dma_start(
                out=rdp[:, s, :], in_=p_d[ic * 128:(ic + 1) * 128, :]))(ic),
            extra_waits=(w_p + w_mu) if ic == 0 else ())
        su, (lus, lut) = fru.dma(
            "sync", (lambda ic: lambda e, s: e.dma_start(
                out=rdmu[:, s, :], in_=mu_d[ic * 128:(ic + 1) * 128, :]))(ic),
            extra_waits=(w_mu + w_p) if ic == 0 else ())
        P.op("scalar",
             (lambda s: lambda e: e.activation(b2st[:, 0, :], rdp[:, s, :], AF.Identity,
                                               bias=magicv[:], scale=c3s[:]))(s),
             waits=((lps, lpt), (DV, dv_c13)) + b2_free, reads=("rdp", "c3s", "magicv"),
             writes=("b2st",), incs=((AC, 1),))
        acv = AC.inc(1)
        frp.release((AC, acv))
        P.op("scalar",
             (lambda su: lambda e: e.activation(rdmu[:, su, :], rdmu[:, su, :], AF.Identity,
                                                bias=magicv[:], scale=c1s[:]))(su),
             waits=((lus, lut),), reads=("rdmu", "c1s", "magicv"),
             writes=("rdmu",), incs=((AC, 1),))
        acu = AC.inc(1)
        sq = (sqw.i + 1) % 2
        sqw.i += 1
        wfreeq = tuple(sqw.rel[sq])
        sqw.rel[sq] = []
        P.op("vector",
             (lambda su: lambda v: v.tensor_scalar(rdmu[:, su, :], rdmu[:, su, :], MAGIC,
                                                   None, OP.subtract))(su),
             waits=((AC, acu),), reads=("rdmu",), writes=("rdmu",))
        P.op("vector",
             (lambda su, sq: lambda v: v.scalar_tensor_tensor(
                 qwr[:, sq, :], b2st[:, 0, :], MAGIC, rdmu[:, su, :],
                 OP.subtract, OP.mult))(su, sq),
             waits=wfreeq, reads=("rdmu", "b2st"), writes=("qwr",), incs=((DV, 1),))
        dvq1 = DV.inc(1)
        fru.release((DV, dvq1))
        b2_free = ((DV, dvq1),)
        P.op("vector",
             (lambda sq: lambda v: v.tensor_reduce(tmps[:, 2:3], qwr[:, sq, :], AX.X,
                                                   OP.max, apply_absolute_value=True))(sq),
             reads=("qwr",), writes=("tmps",))
        P.op("vector", lambda v: v.tensor_tensor(m4r[:], m4r[:], tmps[:, 2:3], OP.max),
             reads=("tmps", "m4r"), writes=("m4r",), incs=((DV, 1),))
        dvq = DV.inc(1)
        sem = sqw.sems[sq]
        tgt = sem.inc(16)
        P.op("scalar",
             (lambda ic, sq: lambda e: e.dma_start(out=q_d[ic * 128:(ic + 1) * 128, :],
                                                   in_=qwr[:, sq, :]))(ic, sq),
             waits=((DV, dvq1),), incs=((sem, 16),), reads=("qwr",))
        sqw.rel[sq] = [(sem, tgt), (DV, dvq)]
        fwd.dma("sync",
                (lambda ic: lambda e, s: e.dma_start(
                    out=wdT[:, ic, :], in_=wd_dq_v[:, ic * 128:(ic + 1) * 128],
                    transpose=True))(ic),
                extra_waits=(w_mu + ((CCW2, 1),)) if ic == 0 else ())
    dv_m4 = DV.val
    w_q = sqw.totals()
    w_wdT = fwd.totals()

    ccw = scalar_ar(12, [(m4r[:], "m4r")], (DV, dv_m4), CC4)
    gw = bcast(12, 0, tmps[:, 0:1], "tmps", CC4, ccw)
    P.op("vector", lambda v: v.tensor_scalar(tmps[:, 1:2], tmps[:, 0:1], 1e-30, None, OP.max),
         waits=(gw,), reads=("tmps",), writes=("tmps",))
    P.op("vector", lambda v: v.reciprocal(c4s[:], tmps[:, 1:2]), reads=("tmps",), writes=("c4s",))
    P.op("vector", lambda v: v.tensor_scalar(c4s[:], c4s[:], QMAX, None, OP.mult),
         reads=("c4s",), writes=("c4s",))
    P.op("vector", lambda v: v.tensor_tensor(S4s[:], S4s[:], tmps[:, 0:1], OP.mult),
         reads=("S4s", "tmps"), writes=("S4s",), incs=((DV, 1),))
    dv_c4 = DV.inc(1)

    o_ready = []
    for ic in range(IC):
        s, (lqs, lqt) = frq.dma(
            "sync", (lambda ic: lambda e, s: e.dma_start(
                out=rdq[:, s, :], in_=q_d[ic * 128:(ic + 1) * 128, :]))(ic),
            extra_waits=w_q if ic == 0 else ())
        P.op("vector",
             (lambda s: lambda v: v.tensor_scalar(rdq[:, s, :], rdq[:, s, :], c4s[:], MAGIC,
                                                  OP.mult, OP.add))(s),
             waits=((lqs, lqt), (DV, dv_c4)), reads=("rdq", "c4s"), writes=("rdq",))
        ot, oname = (o_lo, "o_lo") if ic < IC // 2 else (o_hi, "o_hi")
        P.op("vector",
             (lambda s, ic, ot: lambda v: v.tensor_scalar(
                 ot[:, ic % (IC // 2), :], rdq[:, s, :], MAGIC, None, OP.subtract))(s, ic, ot),
             reads=("rdq",), writes=(oname,), incs=((DV, 1),))
        dvo = DV.inc(1)
        frq.release((DV, dvo))
        o_ready.append(dvo)

    prev_grp_dv = 0
    for tg in range(0, NT, 3):
        tts = list(range(tg, min(tg + 3, NT)))
        regs = {}
        lw = {}
        for tt in tts:
            s, (lys, lyt) = fy1.dma(
                "sync", (lambda tt: lambda e, s: e.dma_start(
                    out=y1rd[:, s, :], in_=y1_d[tt * 128:(tt + 1) * 128, :]))(tt),
                extra_waits=(w_y1d + w_q) if (tg == 0 and tt == tts[0]) else ())
            lw[tt] = (s, lys, lyt)
            regs[tt], _ = ps_acquire()
        for ic in range(IC):
            ot = o_lo if ic < IC // 2 else o_hi
            for j, tt in enumerate(tts):
                lh = ot[:, ic % (IC // 2), tt * 128:(tt + 1) * 128]
                for hf in range(2):
                    st, sp = ic == 0, ic == IC - 1
                    w0 = ()
                    if hf == 0 and j == 0:
                        w0 = ((DV, o_ready[ic]),) + w_wdT
                        if ic == 0 and prev_grp_dv:
                            w0 = w0 + ((DV, prev_grp_dv),)
                    incs = ()
                    if sp and hf == 1 and j == len(tts) - 1:
                        incs = ((TE, 1),)
                        TE.inc(1)
                    mm(regs[tt][:, hf * 512:(hf + 1) * 512], lh,
                       wdT[:, ic, hf * 512:(hf + 1) * 512], st, sp,
                       waits=w0, incs=incs, skip=True)
        tev = TE.val
        for tt in tts:
            s, lys, lyt = lw[tt]
            so = (syo.i + 1) % 2
            syo.i += 1
            wfreeo = tuple(syo.rel[so])
            syo.rel[so] = []
            P.op("vector",
                 (lambda tt, s, so, r: lambda v: v.scalar_tensor_tensor(
                     youts[:, so, :], r[:], S4s[:], y1rd[:, s, :], OP.mult, OP.add))(tt, s, so, regs[tt]),
                 waits=((TE, tev), (lys, lyt)) + wfreeo,
                 reads=("y1rd", "S4s"), writes=("youts",), incs=((DV, 1),))
            dvv = DV.inc(1)
            fy1.release_slot(s, (DV, dvv))
            sem = syo.sems[so]
            tgt = sem.inc(16)
            P.op("scalar",
                 (lambda tt, so: lambda e: e.dma_start(out=y_out[tt * 128:(tt + 1) * 128, :],
                                                       in_=youts[:, so, :]))(tt, so),
                 waits=((DV, dvv),), incs=((sem, 16),), reads=("youts",))
            syo.rel[so] = [(sem, tgt)]
        prev_grp_dv = DV.val
        rot["rel"] = [((DV, prev_grp_dv),)] * 3
    w_out = syo.totals()

    P.op("gpsimd", lambda g: None, waits=w_out)

    from contextlib import ExitStack
    with ExitStack() as stk:
        for s in P.sems:
            s.h = stk.enter_context(nc.semaphore(s.name))
        block = stk.enter_context(nc.Block())
        P.emit(block)
    return nc


_NC_CACHE = {}


def kernel(L0_attn_x, L0_attn_W_qkv, L0_attn_W_out, L0_attn_b_out,
           L0_ffn_w_gate, L0_ffn_w_up, L0_ffn_w_down):
    x = np.ascontiguousarray(np.asarray(L0_attn_x, dtype=np.float32)).reshape(-1, D)
    n = 8
    tpc = x.shape[0] // n
    assert tpc == T, f"expected {T} tokens/core, got {tpc}"
    if "nc" not in _NC_CACHE:
        _NC_CACHE["nc"] = build()
    nc = _NC_CACHE["nc"]
    w = dict(
        wqkv=np.ascontiguousarray(np.asarray(L0_attn_W_qkv, np.float32)),
        wout=np.ascontiguousarray(np.asarray(L0_attn_W_out, np.float32)),
        bout=np.ascontiguousarray(np.asarray(L0_attn_b_out, np.float32)).reshape(1, D),
    )
    wgu = np.ascontiguousarray(np.vstack([
        np.asarray(L0_ffn_w_gate, np.float32),
        np.asarray(L0_ffn_w_up, np.float32)]))
    wdf = np.ascontiguousarray(np.asarray(L0_ffn_w_down, np.float32)).reshape(-1, D)
    in_maps = [dict(x=np.ascontiguousarray(x[c * tpc:(c + 1) * tpc]),
                    wsh1=np.ascontiguousarray(wgu[c * 768:(c + 1) * 768]),
                    wsh2=np.ascontiguousarray(wdf[c * 384:(c + 1) * 384]),
                    **w)
               for c in range(n)]
    res = run_bass_kernel_spmd(nc, in_maps, core_ids=list(range(n)))
    out = np.concatenate([np.asarray(r["y"]) for r in res.results], axis=0)
    return out.reshape(np.asarray(L0_attn_x).shape).astype(np.float32)


# version 09
# speedup vs baseline: 1.2404x; 1.0105x over previous
import numpy as np

import concourse.bass as bass
import concourse.mybir as mybir
from concourse.bass_utils import run_bass_kernel_spmd
from concourse.replica_groups import maybe_share_collective_output_space

F32 = mybir.dt.float32
BF16 = mybir.dt.bfloat16
OP = mybir.AluOpType
AX = mybir.AxisListType
AF = mybir.ActivationFunctionType

MAGIC = 12582912.0
QMAX = 32767.0
RG_ALL = [list(range(8))]
RG_HALF = [[0, 1, 2, 3], [4, 5, 6, 7]]

D = 1024
DC = 8
I3 = 3072
IC = 24
NT = 16
T = NT * 128


class Sem:
    def __init__(self, name):
        self.name, self.val, self.h = name, 0, None

    def inc(self, n=1):
        self.val += n
        return self.val


class Prog:
    ENGS = ("sync", "scalar", "vector", "tensor", "gpsimd")

    def __init__(self):
        self.ev = {e: [] for e in self.ENGS}
        self.sems = []

    def sem(self, name):
        s = Sem(name)
        self.sems.append(s)
        return s

    def op(self, eng, fn, waits=(), incs=(), reads=(), writes=()):
        self.ev[eng].append((tuple(waits), fn, tuple(incs),
                             frozenset(reads), frozenset(writes)))

    def emit(self, block):
        def runner(name):
            def run(e):
                dW, dR = set(), set()
                for waits, fn, incs, rd, wr in self.ev[name]:
                    for sem, val in waits:
                        if val > 0:
                            e.wait_ge(sem.h, val)
                    if name in ("vector", "scalar", "gpsimd") and (
                            (wr & (dW | dR)) or (rd & dW)):
                        e.drain()
                        dW, dR = set(), set()
                    ins = fn(e)
                    for sem, n in incs:
                        ins.then_inc(sem.h, n)
                    dW |= wr
                    dR |= rd
            return run
        block.sync(runner("sync"))
        block.scalar(runner("scalar"))
        block.vector(runner("vector"))
        block.tensor(runner("tensor"))
        block.gpsimd(runner("gpsimd"))


class Fam:

    def __init__(self, P, name, nslots):
        self.P = P
        self.sems = [P.sem(f"{name}{i}") for i in range(nslots)]
        self.rel = [[] for _ in range(nslots)]
        self.i = -1

    def dma(self, eng, fn, extra_waits=(), reads=(), writes=()):
        self.i += 1
        s = self.i % len(self.sems)
        waits = tuple(self.rel[s]) + tuple(extra_waits)
        self.rel[s] = []
        sem = self.sems[s]
        tgt = sem.inc(16)
        self.P.op(eng, (lambda fn, s: lambda e: fn(e, s))(fn, s), waits=waits,
                  incs=((sem, 16),), reads=reads, writes=writes)
        return s, (sem, tgt)

    def release(self, *sem_vals):
        self.rel[self.i % len(self.sems)].extend(sem_vals)

    def release_slot(self, slot, *sem_vals):
        self.rel[slot].extend(sem_vals)

    def totals(self):
        return tuple((s, s.val) for s in self.sems if s.val > 0)


def build():
    nc = bass.Bass("TRN2", target_bir_lowering=False)

    x_in = nc.dram_tensor("x", [T, D], F32, kind="ExternalInput").ap()
    wqkv_in = nc.dram_tensor("wqkv", [I3, D], F32, kind="ExternalInput").ap()
    wout_in = nc.dram_tensor("wout", [D, D], F32, kind="ExternalInput").ap()
    bout_in = nc.dram_tensor("bout", [1, D], F32, kind="ExternalInput").ap()
    wsh1_in = nc.dram_tensor("wsh1", [768, D], F32, kind="ExternalInput").ap()
    wsh2_in = nc.dram_tensor("wsh2", [384, D], F32, kind="ExternalInput").ap()
    y_out = nc.dram_tensor("y", [T, D], F32, kind="ExternalOutput").ap()

    wqkv_bf_d = nc.dram_tensor("wqkv_bf_d", [I3, D], BF16)
    wout_bf_d = nc.dram_tensor("wout_bf_d", [D, D], BF16)
    ln1_d = nc.dram_tensor("ln1_d", [T, D], BF16)
    rnd0_d = nc.dram_tensor("rnd0_d", [T, D], BF16)
    wdq_i1 = nc.dram_tensor("wdq_i1", [768, D], BF16)
    wdq_o1 = nc.dram_tensor(
        "wdq_o1", [2 * I3, D], BF16,
        addr_space=maybe_share_collective_output_space("AllGather", RG_ALL))
    wdq_i2 = nc.dram_tensor("wdq_i2", [384, D], BF16)
    wdq_o2 = nc.dram_tensor(
        "wdq_o2", [I3, D], BF16,
        addr_space=maybe_share_collective_output_space("AllGather", RG_ALL))
    y1_d = nc.dram_tensor("y1_d", [T, D], F32)
    mg_d = nc.dram_tensor("mg_d", [I3, T], F32)
    mu_d = nc.dram_tensor("mu_d", [I3, T], F32)
    p_d = nc.dram_tensor("p_d", [I3, T], F32)
    q_d = nc.dram_tensor("q_d", [I3, T], F32)
    kvag_i = nc.dram_tensor("kvag_i", [D, D + 8], BF16)
    kvag_o = nc.dram_tensor(
        "kvag_o", [4 * D, D + 8], BF16,
        addr_space=maybe_share_collective_output_space("AllGather", RG_HALF))
    sc_i = nc.dram_tensor("sc_i", [1, 16], F32)
    sc_o = nc.dram_tensor(
        "sc_o", [1, 16], F32,
        addr_space=maybe_share_collective_output_space("AllReduce", RG_ALL))

    AL = nc.alloc_sbuf_tensor
    sqst = AL("sqst", [128, D], F32)
    t0st = AL("t0st", [128, D], F32)
    ln2st2 = AL("ln2st2", [128, 2 * D], F32)
    boutbf = AL("boutbf", [1, D], BF16)
    ones_row = AL("ones_row", [1, 128], BF16)
    ones_f = AL("ones_f", [1, 128], F32)
    ones_col = AL("ones_col", [128, 1], BF16)
    ksumbf = AL("ksumbf", [128, DC], BF16)
    mean2 = AL("mean2", [128, NT], F32)
    rstd2 = AL("rstd2", [128, NT], F32)
    rstd0 = AL("rstd0", [128, NT], F32)
    negm0 = AL("negm0", [128, NT], F32)
    stat = AL("stat", [128, 8], F32)
    am0 = AL("am0", [128, 1], F32)
    m1r = AL("m1r", [128, 1], F32)
    m2r = AL("m2r", [128, 1], F32)
    m3r = AL("m3r", [128, 1], F32)
    m4r = AL("m4r", [128, 1], F32)
    rc0 = AL("rc0", [128, 1], F32)
    c1s = AL("c1s", [128, 1], F32)
    c2s = AL("c2s", [128, 1], F32)
    c3s = AL("c3s", [128, 1], F32)
    c4s = AL("c4s", [128, 1], F32)
    S2s = AL("S2s", [128, 1], F32)
    nS2M = AL("nS2M", [128, 1], F32)
    S4s = AL("S4s", [128, 1], F32)
    tmps = AL("tmps", [128, 8], F32)
    s1c = AL("s1c", [128, NT], F32)
    s2c = AL("s2c", [128, NT], F32)
    mean1 = AL("mean1", [128, NT], F32)
    rstd1 = AL("rstd1", [128, NT], F32)
    rmaxc = AL("rmaxc", [128, NT], F32)
    rminc = AL("rminc", [128, NT], F32)
    c65k = AL("c65k", [128, 1], F32)
    arsc = AL("arsc", [128, 1], F32)
    epsln = AL("epsln", [128, 1], F32)
    magicv = AL("magicv", [128, 1], F32)

    MBASE = 40928
    used = 229344 - nc.sbuf_bytes_remaining
    assert used <= MBASE, f"smalls overflowed into manual pool: {used}"

    def at(name, shape, dtype, off_kb):
        return nc.alloc_sbuf_tensor_at(name, shape, dtype,
                                       offset=MBASE + int(off_kb * 1024))

    wqkvT = at("wqkvT", [128, DC, I3], BF16, 0)
    ln1T = at("ln1T", [128, DC, T], BF16, 48)
    kact = at("kact", [128, NT, D], BF16, 80)
    vbf = at("vbf", [128, NT, D], BF16, 112)
    qact = at("qact", [128, DC, T], BF16, 80)
    woutT = at("woutT", [128, DC, D], BF16, 112)
    kvbf = at("kvbf", [128, DC, D], BF16, 128)
    attnz = at("attnz", [128, DC, T], BF16, 0)
    zrow = at("zrow", [1, T], F32, 144)
    z_bc = at("z_bc", [128, T], F32, 152)
    y1sb = at("y1sb", [128, NT, D], F32, 32)
    XT = at("XT", [128, DC, T], BF16, 96)
    wgT = at("wgT", [128, DC, I3], BF16, 128)
    wuT = at("wuT", [128, DC, I3], BF16, 0)
    wdT = at("wdT", [128, IC, D], BF16, 96)
    o_lo = at("o_lo", [128, IC // 2, T], BF16, 0)
    o_hi = at("o_hi", [128, IC // 2, T], BF16, 48)
    x_all = at("x_all", [128, NT, D], F32, 32)
    xld = at("xld", [128, 2, D], F32, 144)
    ln1st = at("ln1st", [128, 2, D], BF16, 152)
    kvst = at("kvst", [128, 2, D + 8], BF16, 156)
    kvsum = at("kvsum", [128, 2, 4, D + 8], BF16, 0)
    wldst = at("wldst", [128, 2, 2, D], F32, 160)
    dqst = at("dqst", [128, 2, 2 * D], BF16, 176)
    rnd0st = at("rnd0st", [128, 2, D], BF16, 152)
    mst = at("mst", [128, 4, D], F32, 48)
    rdmg = at("rdmg", [128, 2, 2 * D], F32, 64)
    pwr = at("pwr", [128, 2, 2 * D], F32, 80)
    b2st = at("b2st", [128, 1, 2 * D], F32, 176)
    rdp = at("rdp", [128, 2, 2 * D], F32, 48)
    rdmu = at("rdmu", [128, 2, 2 * D], F32, 64)
    qwr = at("qwr", [128, 2, 2 * D], F32, 80)
    rdq = at("rdq", [128, 2, 2 * D], F32, 144)
    y1rd = at("y1rd", [128, 4, D], F32, 160)
    youts = at("youts", [128, 2, D], F32, 176)

    ps = [nc.alloc_psum_tensor(f"ps{i}", [128, D], F32) for i in range(3)]
    ps4 = nc.alloc_psum_tensor("ps4", [128, 512], F32)

    P = Prog()
    TE, DV, AC, GP = P.sem("TE"), P.sem("DV"), P.sem("AC"), P.sem("GP")
    CCK, CC0, CC2, CC13, CC4 = (P.sem(n) for n in
                                ("CCK", "CC0", "CC2", "CC13", "CC4"))
    GC = P.sem("GC")
    GA = P.sem("GA")

    fx = Fam(P, "fx", 2)
    flt = Fam(P, "flt", 1)
    fwq = Fam(P, "fwq", 1)
    fwo = Fam(P, "fwo", 1)
    fxt = Fam(P, "fxt", 1)
    fwg = Fam(P, "fwg", 1)
    fwu = Fam(P, "fwu", 1)
    fwd = Fam(P, "fwd", 1)
    fwl = Fam(P, "fwl", 2)
    fkv = Fam(P, "fkv", 2)
    fmg = Fam(P, "fmg", 2)
    frp = Fam(P, "frp", 2)
    fru = Fam(P, "fru", 2)
    frq = Fam(P, "frq", 2)
    fy1 = Fam(P, "fy1", 4)
    sln = Fam(P, "sln", 2)
    skv = Fam(P, "skv", 2)
    sdq = Fam(P, "sdq", 2)
    sy1 = Fam(P, "sy1", 2)
    sr0 = Fam(P, "sr0", 2)
    smt = Fam(P, "smt", 4)
    spw = Fam(P, "spw", 2)
    sqw = Fam(P, "sqw", 2)
    syo = Fam(P, "syo", 2)

    rot = {"i": -1, "rel": [(), (), ()]}

    def ps_acquire():
        rot["i"] += 1
        r = rot["i"] % 3
        w = rot["rel"][r]
        rot["rel"][r] = ()
        return ps[r], w

    def ps_release(sem, val):
        rot["rel"][rot["i"] % 3] = ((sem, val),)

    def mm(out, lh, rh, st, sp, waits=(), incs=(), skip=False):
        P.op("tensor",
             (lambda o, l, r, s1, s2, sk: lambda t: t.matmul(
                 o, l, r, start=s1, stop=s2, skip_group_check=sk))
             (out, lh, rh, st, sp, skip), waits=waits, incs=incs)

    for tns, val in ((ones_row, 1.0), (ones_f, 1.0), (ones_col, 1.0),
                     (am0, 0.0), (m1r, 0.0), (m2r, 0.0), (m3r, 0.0),
                     (m4r, 0.0), (kvst, 0.0), (epsln, 1e-5), (magicv, MAGIC), (c65k, 65536.0)):
        P.op("vector", (lambda t_, v_: lambda v: v.memset(t_[:], v_))(tns, val),
             writes=(tns.name,))

    P.op("gpsimd", lambda g: g.dma_start(out=boutbf[:], in_=bout_in),
         incs=((GC, 16),), writes=("boutbf",))
    GC.inc(16)
    for j in range(4):
        sl = slice(j * (I3 // 4), (j + 1) * (I3 // 4))
        P.op("gpsimd",
             (lambda sl: lambda g: g.dma_start(out=wqkv_bf_d[sl, :], in_=wqkv_in[sl, :]))(sl),
             incs=((GC, 16),))
        GC.inc(16)
    for j in range(2):
        sl = slice(j * (D // 2), (j + 1) * (D // 2))
        P.op("gpsimd",
             (lambda sl: lambda g: g.dma_start(out=wout_bf_d[sl, :], in_=wout_in[sl, :]))(sl),
             incs=((GC, 16),))
        GC.inc(16)
    g_cast = GC.val

    b2w = AL("b2w", [128, 3 * 64], F32)

    def dequant_chunk(src_ap, dst_ap, nr=2, first_waits=()):
        nb = nr * D // 32
        s, (lsem, ltgt) = fwl.dma(
            "sync", (lambda src_ap, nr: lambda e, s: e.dma_start(
                out=wldst[:, s, 0:nr, :], in_=src_ap))(src_ap, nr),
            extra_waits=first_waits)
        wv = wldst[:, s, 0:nr, :].rearrange("p a (b k) -> p (a b) k", k=32)
        P.op("vector",
             (lambda wv, nb: lambda v: v.tensor_reduce(
                 b2w[:, 0:nb], wv, AX.X, OP.max, apply_absolute_value=True))(wv, nb),
             waits=((lsem, ltgt),), reads=("wldst",), writes=("b2w",))
        P.op("vector",
             (lambda nb: lambda v: v.tensor_scalar(b2w[:, nb:2 * nb], b2w[:, 0:nb],
                                                   1.0 / 7, 1e-12, OP.mult, OP.max))(nb),
             reads=("b2w",), writes=("b2w",))
        P.op("vector",
             (lambda nb: lambda v: v.reciprocal(b2w[:, 2 * nb:3 * nb], b2w[:, nb:2 * nb]))(nb),
             reads=("b2w",), writes=("b2w",))
        P.op("vector",
             (lambda wv, nb, nr: lambda v: v.tensor_tensor(
                 ln2st2[:, 0:nr * D].rearrange("p (b k) -> p b k", k=32), wv,
                 b2w[:, 2 * nb:3 * nb].to_broadcast([128, nb, 32]), OP.mult))(wv, nb, nr),
             reads=("wldst", "b2w"), writes=("ln2st2",), incs=((DV, 1),))
        fwl.release((DV, DV.inc(1)))
        P.op("vector",
             (lambda nr: lambda v: v.tensor_scalar(ln2st2[:, 0:nr * D], ln2st2[:, 0:nr * D],
                                                   MAGIC, MAGIC, OP.add, OP.subtract))(nr),
             reads=("ln2st2",), writes=("ln2st2",))
        sd = (sdq.i + 1) % 2
        sdq.i += 1
        wfree2 = tuple(sdq.rel[sd])
        sdq.rel[sd] = []
        P.op("vector",
             (lambda sd, nb, nr: lambda v: v.tensor_tensor(
                 dqst[:, sd, 0:nr * D].rearrange("p (b k) -> p b k", k=32),
                 ln2st2[:, 0:nr * D].rearrange("p (b k) -> p b k", k=32),
                 b2w[:, nb:2 * nb].to_broadcast([128, nb, 32]), OP.mult))(sd, nb, nr),
             waits=wfree2, reads=("ln2st2", "b2w"), writes=("dqst",), incs=((DV, 1),))
        dvv = DV.inc(1)
        sem = sdq.sems[sd]
        tgt = sem.inc(16)
        P.op("scalar",
             (lambda sd, nr, dst_ap: lambda e: e.dma_start(
                 out=dst_ap, in_=dqst[:, sd, 0:nr * D]))(sd, nr, dst_ap),
             waits=((DV, dvv),), incs=((sem, 16),), reads=("dqst",))
        sdq.rel[sd] = [(sem, tgt)]

    def rows2(w_dram, a, nr=2):
        return w_dram.rearrange("(a p) d -> p a d", p=128)[:, 2 * a:2 * a + nr, :]
    CCW1 = P.sem("CCW1")

    x_t = x_in.rearrange("(n p) d -> n p d", p=128)
    x_all3 = x_in.rearrange("(n p) d -> p n d", p=128)
    _, (lxs, lxt) = fx.dma(
        "sync", lambda e, s: e.dma_start(out=x_all[:], in_=x_all3), writes=("x_all",))
    for tt in range(NT):
        P.op("vector",
             (lambda tt: lambda v: v.reduce_sum(s1c[:, tt:tt + 1], x_all[:, tt, :], AX.X))(tt),
             waits=((lxs, lxt),) if tt == 0 else (), reads=("x_all",), writes=("s1c",))
        P.op("scalar",
             (lambda tt: lambda e: e.activation(sqst[:], x_all[:, tt, :], AF.Square,
                                                accum_out=s2c[:, tt:tt + 1]))(tt),
             waits=((lxs, lxt),) if tt == 0 else (), reads=("x_all",),
             writes=("sqst", "s2c"),
             incs=((AC, 1),) if tt == NT - 1 else ())
    acv = AC.inc(1)
    for a in range(3):
        dequant_chunk(rows2(wsh1_in, a),
                      wdq_i1.rearrange("(a p) d -> p a d", p=128)[:, 2 * a:2 * a + 2, :])
    w_dq1 = sdq.totals()
    P.op("gpsimd", lambda g: g.collective_compute(
        "AllGather", OP.bypass, replica_groups=RG_ALL,
        ins=[wdq_i1[:, :].opt()], outs=[wdq_o1[:, :].opt()]),
        waits=w_dq1, incs=((CCW1, 1),))
    CCW1.inc(1)
    dequant_chunk(rows2(wsh2_in, 0),
                  wdq_i2.rearrange("(a p) d -> p a d", p=128)[:, 0:2, :])
    dequant_chunk(rows2(wsh2_in, 1, nr=1),
                  wdq_i2.rearrange("(a p) d -> p a d", p=128)[:, 2:3, :], nr=1)
    w_dq2 = sdq.totals()
    P.op("vector", lambda v: v.tensor_scalar(mean1[:], s1c[:], 1.0 / D, None, OP.mult),
         waits=((AC, acv),), reads=("s1c",), writes=("mean1",))
    P.op("vector", lambda v: v.tensor_scalar(s2c[:], s2c[:], 1.0 / D, None, OP.mult),
         reads=("s2c",), writes=("s2c",))
    P.op("vector", lambda v: v.tensor_tensor(s1c[:], mean1[:], mean1[:], OP.mult),
         reads=("mean1",), writes=("s1c",))
    P.op("vector", lambda v: v.tensor_tensor(s2c[:], s2c[:], s1c[:], OP.subtract),
         reads=("s2c", "s1c"), writes=("s2c",), incs=((DV, 1),))
    dvv = DV.inc(1)
    P.op("scalar", lambda e: e.activation(s2c[:], s2c[:], AF.Sqrt, bias=epsln[:]),
         waits=((DV, dvv),), reads=("s2c", "epsln"), writes=("s2c",), incs=((AC, 1),))
    acv = AC.inc(1)
    P.op("vector", lambda v: v.reciprocal(rstd1[:], s2c[:]),
         waits=((AC, acv),), reads=("s2c",), writes=("rstd1",), incs=((DV, 1),))
    DV.inc(1)
    for tt in range(NT):
        sl2 = tt % 2
        wfree2 = tuple(sln.rel[sl2])
        sln.rel[sl2] = []
        P.op("vector",
             (lambda tt, sl2: lambda v: v.tensor_scalar(ln1st[:, sl2, :], x_all[:, tt, :],
                                                        mean1[:, tt:tt + 1], rstd1[:, tt:tt + 1],
                                                        OP.subtract, OP.mult))(tt, sl2),
             waits=wfree2, reads=("x_all", "mean1", "rstd1"), writes=("ln1st",), incs=((DV, 1),))
        dvv = DV.inc(1)
        sem = sln.sems[sl2]
        tgt = sem.inc(16)
        P.op("scalar",
             (lambda tt, sl2: lambda e: e.dma_start(out=ln1_d[tt * 128:(tt + 1) * 128, :],
                                                    in_=ln1st[:, sl2, :]))(tt, sl2),
             waits=((DV, dvv),), incs=((sem, 16),), reads=("ln1st",))
        sln.i = tt
        sln.rel[sl2] = [(sem, tgt)]
    dv_ln1 = DV.val
    w_ln1d = sln.totals()

    for dc in range(DC):
        flt.dma("sync",
                (lambda dc: lambda e, s: e.dma_start(
                    out=ln1T[:, dc, :], in_=ln1_d[:, dc * 128:(dc + 1) * 128],
                    transpose=True))(dc),
                extra_waits=w_ln1d if dc == 0 else ())
    w_ln1T = flt.totals()
    for dc in range(DC):
        fwq.dma("sync",
                (lambda dc: lambda e, s: e.dma_start(
                    out=wqkvT[:, dc, :], in_=wqkv_bf_d[:, dc * 128:(dc + 1) * 128],
                    transpose=True))(dc),
                extra_waits=((GC, g_cast),))
    w_wqkvT = fwq.totals()

    for tt in range(NT):
        rk, wk = ps_acquire()
        rv, wv = ps_acquire()
        w0 = w_ln1T + w_wqkvT + wk + wv
        for dc in range(DC):
            lh = ln1T[:, dc, tt * 128:(tt + 1) * 128]
            st, sp = dc == 0, dc == DC - 1
            for hf in range(2):
                mm(rk[:, hf * 512:(hf + 1) * 512], lh,
                   wqkvT[:, dc, D + hf * 512:D + (hf + 1) * 512], st, sp,
                   waits=w0 if (dc == 0 and hf == 0) else (), skip=True)
                w0 = ()
                incs = ()
                if sp and hf == 1:
                    incs = ((TE, 1),)
                    TE.inc(1)
                mm(rv[:, hf * 512:(hf + 1) * 512], lh,
                   wqkvT[:, dc, 2 * D + hf * 512:2 * D + (hf + 1) * 512], st, sp,
                   incs=incs, skip=True)
        tev = TE.val
        P.op("vector",
             (lambda tt, rk: lambda v: v.tensor_scalar(kact[:, tt, :], rk[:], 0.0, 1e-6,
                                                       OP.max, OP.add))(tt, rk),
             waits=((TE, tev),), writes=("kact",), incs=((DV, 1),))
        dv_k = DV.inc(1)
        P.op("scalar",
             (lambda tt, rv: lambda e: e.copy(vbf[:, tt, :], rv[:]))(tt, rv),
             waits=((TE, tev),), writes=("vbf",), incs=((AC, 1),))
        ac_v = AC.inc(1)
        rot["rel"][(rot["i"] - 1) % 3] = ((DV, dv_k),)
        rot["rel"][rot["i"] % 3] = ((AC, ac_v),)
    dv_2a, ac_2a = DV.val, AC.val

    for dc in range(DC):
        r, wfree = ps_acquire()
        for tt in range(NT):
            lh = kact[:, tt, dc * 128:(dc + 1) * 128]
            st, sp = tt == 0, tt == NT - 1
            w0 = ((DV, dv_2a), (AC, ac_2a)) + wfree if tt == 0 else ()
            mm(r[:, 0:512], lh, vbf[:, tt, 0:512], st, sp, waits=w0, skip=True)
            mm(r[:, 512:1024], lh, vbf[:, tt, 512:1024], st, sp, skip=True)
            incs = ()
            if sp:
                incs = ((TE, 1),)
                TE.inc(1)
            mm(ps4[:, dc:dc + 1], lh, ones_col[:], st, sp, incs=incs, skip=True)
        tev = TE.val
        sl2 = dc % 2
        wfree2 = tuple(skv.rel[sl2])
        skv.rel[sl2] = []
        P.op("vector",
             (lambda sl2, r: lambda v: v.tensor_copy(kvst[:, sl2, 0:D], r[:]))(sl2, r),
             waits=((TE, tev),) + wfree2, writes=("kvst",))
        P.op("vector",
             (lambda sl2, dc: lambda v: v.tensor_copy(kvst[:, sl2, D:D + 1], ps4[:, dc:dc + 1]))(sl2, dc),
             writes=("kvst",), incs=((DV, 1),))
        dvv = DV.inc(1)
        ps_release(DV, dvv)
        sem = skv.sems[sl2]
        tgt = sem.inc(16)
        P.op("scalar",
             (lambda sl2, dc: lambda e: e.dma_start(out=kvag_i[dc * 128:(dc + 1) * 128, :],
                                                    in_=kvst[:, sl2, :]))(sl2, dc),
             waits=((DV, dvv),), incs=((sem, 16),), reads=("kvst",))
        skv.i = dc
        skv.rel[sl2] = [(sem, tgt)]
    w_kvag = skv.totals()
    P.op("gpsimd", lambda g: g.collective_compute(
        "AllGather", OP.bypass, replica_groups=RG_HALF,
        ins=[kvag_i[:, :].opt()], outs=[kvag_o[:, :].opt()]),
        waits=w_kvag, incs=((CCK, 1),))
    CCK.inc(1)
    CCW2 = P.sem("CCW2")
    P.op("gpsimd", lambda g: g.collective_compute(
        "AllGather", OP.bypass, replica_groups=RG_ALL,
        ins=[wdq_i2[:, :].opt()], outs=[wdq_o2[:, :].opt()]),
        waits=w_dq2, incs=((CCW2, 1),))
    CCW2.inc(1)

    for do in range(DC):
        for th in range(2):
            r, wfree = ps_acquire()
            for dc in range(DC):
                st, sp = dc == 0, dc == DC - 1
                for hq in range(2):
                    w0 = wfree if (dc == 0 and hq == 0) else ()
                    incs = ()
                    if sp and hq == 1:
                        incs = ((TE, 1),)
                        TE.inc(1)
                    mm(r[:, hq * 512:(hq + 1) * 512],
                       wqkvT[:, dc, do * 128:(do + 1) * 128],
                       ln1T[:, dc, th * 1024 + hq * 512:th * 1024 + (hq + 1) * 512],
                       st, sp, waits=w0, incs=incs)
            tev = TE.val
            P.op("vector",
                 (lambda do, th, r: lambda v: v.tensor_scalar(
                     qact[:, do, th * 1024:(th + 1) * 1024], r[:], 0.0, 1e-6,
                     OP.max, OP.add))(do, th, r),
                 waits=((TE, tev),), writes=("qact",), incs=((DV, 1),))
            ps_release(DV, DV.inc(1))
    dv_q = DV.val

    for dc in range(DC):
        fwo.dma("sync",
                (lambda dc: lambda e, s: e.dma_start(
                    out=woutT[:, dc, :], in_=wout_bf_d[:, dc * 128:(dc + 1) * 128],
                    transpose=True))(dc),
                extra_waits=(((GC, g_cast),) + w_kvag) if dc == 0 else ())
    w_woutT = fwo.totals()

    GS = P.sem("GS")
    kvo = kvag_o.rearrange("(r c p) e -> p r c e", p=128, c=DC)
    for dc in range(DC):
        s, (lsem, ltgt) = fkv.dma(
            "sync", (lambda dc: lambda e, s: e.dma_start(out=kvsum[:, s], in_=kvo[:, :, dc, :]))(dc),
            extra_waits=((CCK, 1), (DV, dv_q)) if dc == 0 else ())
        P.op("vector",
             (lambda s: lambda v: v.tensor_add(kvsum[:, s, 0, :], kvsum[:, s, 0, :],
                                               kvsum[:, s, 1, :]))(s),
             waits=((lsem, ltgt),), reads=("kvsum",), writes=("kvsum",))
        P.op("vector",
             (lambda s: lambda v: v.tensor_add(kvsum[:, s, 2, :], kvsum[:, s, 2, :],
                                               kvsum[:, s, 3, :]))(s),
             reads=("kvsum",), writes=("kvsum",))
        P.op("vector",
             (lambda dc, s: lambda v: v.tensor_add(kvbf[:, dc, :], kvsum[:, s, 0, 0:D],
                                                   kvsum[:, s, 2, 0:D]))(dc, s),
             reads=("kvsum",), writes=("kvbf",))
        P.op("vector",
             (lambda dc, s: lambda v: v.tensor_add(ksumbf[:, dc:dc + 1], kvsum[:, s, 0, D:D + 1],
                                                   kvsum[:, s, 2, D:D + 1]))(dc, s),
             reads=("kvsum",), writes=("ksumbf",), incs=((GS, 1),))
        fkv.release((GS, GS.inc(1)))
    gs_kv = GS.val

    for tq in range(4):
        for dc in range(DC):
            st, sp = dc == 0, dc == DC - 1
            if tq == 0 and dc == 0:
                w0 = ((GS, gs_kv),)
            elif dc == 0:
                w0 = ((DV, DV.val),)
            else:
                w0 = ()
            incs = ()
            if sp:
                incs = ((TE, 1),)
                TE.inc(1)
            mm(ps4[0:1, 0:512], ksumbf[:, dc:dc + 1],
               qact[:, dc, tq * 512:(tq + 1) * 512], st, sp, waits=w0, incs=incs,
               skip=True)
        P.op("vector",
             (lambda tq: lambda v: v.tensor_scalar(zrow[0:1, tq * 512:(tq + 1) * 512],
                                                   ps4[0:1, 0:512], 1e-6, None, OP.add))(tq),
             waits=((TE, TE.val),), writes=("zrow",), incs=((DV, 1),))
        DV.inc(1)
    P.op("vector", lambda v: v.reciprocal(zrow[:], zrow[:]),
         reads=("zrow",), writes=("zrow",), incs=((DV, 1),))
    dv_z = DV.inc(1)
    for tq in range(4):
        TE.inc(1)
        mm(ps4[:, 0:512], ones_f[:], zrow[0:1, tq * 512:(tq + 1) * 512], True, True,
           waits=((DV, dv_z if tq == 0 else DV.val),), incs=((TE, 1),))
        P.op("vector",
             (lambda tq: lambda v: v.tensor_copy(z_bc[:, tq * 512:(tq + 1) * 512], ps4[:, 0:512]))(tq),
             waits=((TE, TE.val),), writes=("z_bc",), incs=((DV, 1),))
        DV.inc(1)
    dv_zbc = DV.val

    for ec in range(DC):
        for th in range(2):
            r, wfree = ps_acquire()
            for dc in range(DC):
                st, sp = dc == 0, dc == DC - 1
                for hq in range(2):
                    w0 = (((DV, dv_zbc),) + wfree) if (dc == 0 and hq == 0) else ()
                    incs = ()
                    if sp and hq == 1:
                        incs = ((TE, 1),)
                        TE.inc(1)
                    mm(r[:, hq * 512:(hq + 1) * 512],
                       kvbf[:, dc, ec * 128:(ec + 1) * 128],
                       qact[:, dc, th * 1024 + hq * 512:th * 1024 + (hq + 1) * 512],
                       st, sp, waits=w0, incs=incs)
            P.op("vector",
                 (lambda ec, th, r: lambda v: v.tensor_tensor(
                     attnz[:, ec, th * 1024:(th + 1) * 1024], r[:],
                     z_bc[:, th * 1024:(th + 1) * 1024], OP.mult))(ec, th, r),
                 waits=((TE, TE.val),), reads=("z_bc",), writes=("attnz",), incs=((DV, 1),))
            ps_release(DV, DV.inc(1))
    dv_attnz = DV.val

    for tt in range(NT):
        s, (lsem, ltgt) = fx.dma(
            "sync", (lambda tt: lambda e, s: e.dma_start(out=xld[:, s, :], in_=x_t[tt]))(tt),
            extra_waits=((DV, dv_attnz),) if tt == 0 else ())
        r, wfree2 = ps_acquire()
        for ec in range(DC):
            st = ec == 0
            for hf in range(2):
                w0 = (((DV, dv_attnz),) + w_woutT + wfree2) if (ec == 0 and hf == 0) else ()
                mm(r[:, hf * 512:(hf + 1) * 512],
                   attnz[:, ec, tt * 128:(tt + 1) * 128],
                   woutT[:, ec, hf * 512:(hf + 1) * 512], st, False, waits=w0, skip=True)
        for hf in range(2):
            incs = ()
            if hf == 1:
                incs = ((TE, 1),)
                TE.inc(1)
            mm(r[:, hf * 512:(hf + 1) * 512], ones_row[:],
               boutbf[0:1, hf * 512:(hf + 1) * 512], False, True, incs=incs, skip=True)
        tev = TE.val
        P.op("vector",
             (lambda tt, s, r: lambda v: v.tensor_tensor(y1sb[:, tt, :], r[:], xld[:, s, :], OP.add))(tt, s, r),
             waits=((TE, tev), (lsem, ltgt)), reads=("xld",), writes=("y1sb",), incs=((DV, 1),))
        dv_y1 = DV.inc(1)
        fx.release((DV, dv_y1))
        ps_release(DV, dv_y1)
        sl2 = tt % 2
        wfree3 = tuple(sy1.rel[sl2])
        sy1.rel[sl2] = []
        sem = sy1.sems[sl2]
        tgt = sem.inc(16)
        P.op("scalar",
             (lambda tt: lambda e: e.dma_start(out=y1_d[tt * 128:(tt + 1) * 128, :],
                                               in_=y1sb[:, tt, :]))(tt),
             waits=((DV, dv_y1),) + wfree3, incs=((sem, 16),), reads=("y1sb",))
        sy1.i = tt
        sy1.rel[sl2] = [(sem, tgt)]
        P.op("scalar",
             (lambda tt: lambda e: e.activation(sqst[:], y1sb[:, tt, :], AF.Square,
                                                accum_out=s2c[:, tt:tt + 1]))(tt),
             waits=((DV, dv_y1),), reads=("y1sb",), writes=("sqst", "s2c"),
             incs=((AC, 1),) if tt == NT - 1 else ())
        P.op("vector",
             (lambda tt: lambda v: v.reduce_sum(s1c[:, tt:tt + 1], y1sb[:, tt, :], AX.X))(tt),
             reads=("y1sb",), writes=("s1c",))
        P.op("vector",
             (lambda tt: lambda v: v.tensor_reduce(rmaxc[:, tt:tt + 1], y1sb[:, tt, :],
                                                   AX.X, OP.max))(tt),
             reads=("y1sb",), writes=("rmaxc",))
        P.op("vector",
             (lambda tt: lambda v: v.tensor_reduce(rminc[:, tt:tt + 1], y1sb[:, tt, :],
                                                   AX.X, OP.min))(tt),
             reads=("y1sb",), writes=("rminc",))
    ac_sq = AC.inc(1)
    P.op("vector", lambda v: v.tensor_scalar(mean2[:], s1c[:], 1.0 / D, None, OP.mult),
         waits=((AC, ac_sq),), reads=("s1c",), writes=("mean2",))
    P.op("vector", lambda v: v.tensor_scalar(s2c[:], s2c[:], 1.0 / D, None, OP.mult),
         reads=("s2c",), writes=("s2c",))
    P.op("vector", lambda v: v.tensor_tensor(s1c[:], mean2[:], mean2[:], OP.mult),
         reads=("mean2",), writes=("s1c",))
    P.op("vector", lambda v: v.tensor_tensor(s2c[:], s2c[:], s1c[:], OP.subtract),
         reads=("s2c", "s1c"), writes=("s2c",), incs=((DV, 1),))
    dvv = DV.inc(1)
    P.op("scalar", lambda e: e.activation(s2c[:], s2c[:], AF.Sqrt, bias=epsln[:]),
         waits=((DV, dvv),), reads=("s2c", "epsln"), writes=("s2c",), incs=((AC, 1),))
    acv = AC.inc(1)
    P.op("vector", lambda v: v.reciprocal(rstd2[:], s2c[:]),
         waits=((AC, acv),), reads=("s2c",), writes=("rstd2",))
    P.op("vector", lambda v: v.tensor_tensor(rmaxc[:], rmaxc[:], mean2[:], OP.subtract),
         reads=("rmaxc", "mean2"), writes=("rmaxc",))
    P.op("vector", lambda v: v.tensor_tensor(rminc[:], mean2[:], rminc[:], OP.subtract),
         reads=("rminc", "mean2"), writes=("rminc",))
    P.op("vector", lambda v: v.tensor_tensor(rmaxc[:], rmaxc[:], rminc[:], OP.max),
         reads=("rmaxc", "rminc"), writes=("rmaxc",))
    P.op("vector", lambda v: v.tensor_tensor(rmaxc[:], rmaxc[:], rstd2[:], OP.mult),
         reads=("rmaxc", "rstd2"), writes=("rmaxc",))
    P.op("vector", lambda v: v.reduce_max(am0[:], rmaxc[:], AX.X),
         reads=("rmaxc",), writes=("am0",), incs=((DV, 1),))
    DV.inc(1)
    dv_s0 = DV.val
    w_y1d = sy1.totals()

    def scalar_ar(cols, maxes, wait, ccsem):
        for j, (ap_, nm) in enumerate(maxes):
            P.op("gpsimd",
                 (lambda ap_: lambda g: g.tensor_reduce(arsc[0:1, 0:1], ap_, AX.XYZWC, OP.max))(ap_),
                 waits=(wait,) if j == 0 else (), reads=(nm,), writes=("arsc",))
            P.op("gpsimd",
                 (lambda cols, j: lambda g: g.dma_start(
                     out=sc_i[0:1, cols + j:cols + j + 1], in_=arsc[0:1, 0:1]))(cols, j),
                 incs=((GA, 16),), reads=("arsc",))
            P.op("gpsimd", lambda g: None, waits=((GA, GA.inc(16)),))
        n = len(maxes)
        P.op("gpsimd", (lambda n, cols: lambda g: g.collective_compute(
            "AllReduce", OP.max, replica_groups=RG_ALL,
            ins=[sc_i[0:1, cols:cols + n].opt()],
            outs=[sc_o[0:1, cols:cols + n].opt()]))(n, cols),
            incs=((ccsem, 1),))
        return ccsem.inc(1)

    def bcast(cols, j, dst, dname, ccsem, ccw):
        P.op("gpsimd",
             (lambda dst, cols, j: lambda g: g.dma_start(
                 out=dst, in_=sc_o[0:1, cols + j:cols + j + 1].to_broadcast([128, 1])))(dst, cols, j),
             waits=((ccsem, ccw), (GA, GA.val)), incs=((GA, 16),), writes=(dname,))
        return (GA, GA.inc(16))

    ccw = scalar_ar(0, [(am0[:], "am0")], (DV, dv_s0), CC0)
    gw = bcast(0, 0, tmps[:, 0:1], "tmps", CC0, ccw)
    P.op("vector", lambda v: v.tensor_scalar(tmps[:, 1:2], tmps[:, 0:1], 1.0 / QMAX, 1e-12,
                                             OP.mult, OP.max),
         waits=(gw,), reads=("tmps",), writes=("tmps",))
    P.op("vector", lambda v: v.reciprocal(rc0[:], tmps[:, 1:2]),
         reads=("tmps",), writes=("rc0",), incs=((DV, 1),))
    dv_rc0 = DV.inc(1)

    P.op("vector", lambda v: v.tensor_tensor(rstd0[:], rstd2[:],
                                             rc0[:].to_broadcast([128, NT]), OP.mult),
         waits=((DV, dv_rc0),), reads=("rstd2", "rc0"), writes=("rstd0",))
    P.op("vector", lambda v: v.tensor_tensor(negm0[:], mean2[:], rstd0[:], OP.mult),
         reads=("mean2", "rstd0"), writes=("negm0",))
    P.op("vector", lambda v: v.tensor_scalar(negm0[:], negm0[:], -1.0, MAGIC, OP.mult, OP.add),
         reads=("negm0",), writes=("negm0",), incs=((DV, 1),))
    dv_q0 = DV.inc(1)
    for tt in range(NT):
        P.op("scalar",
             (lambda tt: lambda e: e.activation(t0st[:], y1sb[:, tt, :], AF.Identity,
                                                bias=negm0[:, tt:tt + 1],
                                                scale=rstd0[:, tt:tt + 1]))(tt),
             waits=((DV, dv_q0 if tt == 0 else DV.val),),
             reads=("y1sb", "negm0", "rstd0"), writes=("t0st",), incs=((AC, 1),))
        acv = AC.inc(1)
        sl2 = tt % 2
        wfree = tuple(sr0.rel[sl2])
        sr0.rel[sl2] = []
        P.op("vector",
             (lambda sl2: lambda v: v.tensor_scalar(rnd0st[:, sl2, :], t0st[:], MAGIC, None,
                                                    OP.subtract))(sl2),
             waits=((AC, acv),) + wfree, reads=("t0st",), writes=("rnd0st",), incs=((DV, 1),))
        dvv = DV.inc(1)
        sem = sr0.sems[sl2]
        tgt = sem.inc(16)
        P.op("scalar",
             (lambda tt, sl2: lambda e: e.dma_start(out=rnd0_d[tt * 128:(tt + 1) * 128, :],
                                                    in_=rnd0st[:, sl2, :]))(tt, sl2),
             waits=((DV, dvv),), incs=((sem, 16),), reads=("rnd0st",))
        sr0.i = tt
        sr0.rel[sl2] = [(sem, tgt)]
    w_rnd0 = sr0.totals()

    for dc in range(DC):
        fxt.dma("sync",
                (lambda dc: lambda e, s: e.dma_start(
                    out=XT[:, dc, :], in_=rnd0_d[:, dc * 128:(dc + 1) * 128],
                    transpose=True))(dc),
                extra_waits=w_rnd0 if dc == 0 else ())
    w_XT = fxt.totals()
    for hc in range(DC):
        fwg.dma("sync",
                (lambda hc: lambda e, s: e.dma_start(
                    out=wgT[:, hc, :], in_=wdq_o1[0:I3, hc * 128:(hc + 1) * 128],
                    transpose=True))(hc),
                extra_waits=((CCW1, 1), (DV, dv_attnz)))
    w_wgT = fwg.totals()
    for hc in range(DC):
        fwu.dma("sync",
                (lambda hc: lambda e, s: e.dma_start(
                    out=wuT[:, hc, :], in_=wdq_o1[I3:2 * I3, hc * 128:(hc + 1) * 128],
                    transpose=True))(hc),
                extra_waits=(((CCW1, 1),) + w_y1d) if hc == 0 else ())
    w_wuT = fwu.totals()

    def ffn_mm_step(wT, w_w, ic, th, dst_d, mrun, mname, wq_eng="scalar"):
        r, wfree = ps_acquire()
        for hc in range(DC):
            st, sp = hc == 0, hc == DC - 1
            for hq in range(2):
                w0 = (w_w + w_XT + wfree) if (hc == 0 and hq == 0) else ()
                incs = ()
                if sp and hq == 1:
                    incs = ((TE, 1),)
                    TE.inc(1)
                mm(r[:, hq * 512:(hq + 1) * 512], wT[:, hc, ic * 128:(ic + 1) * 128],
                   XT[:, hc, th * 1024 + hq * 512:th * 1024 + (hq + 1) * 512],
                   st, sp, waits=w0, incs=incs)
        tev = TE.val
        sm = (smt.i + 1) % 4
        smt.i += 1
        wfree2 = tuple(smt.rel[sm])
        smt.rel[sm] = []
        P.op("scalar",
             (lambda sm, r: lambda e: e.copy(mst[:, sm, :], r[:]))(sm, r),
             waits=((TE, tev),) + wfree2, writes=("mst",), incs=((AC, 1),))
        acv = AC.inc(1)
        ps_release(AC, acv)
        P.op("vector",
             (lambda sm: lambda v: v.tensor_reduce(tmps[:, 1:2], mst[:, sm, :], AX.X,
                                                   OP.max, apply_absolute_value=True))(sm),
             waits=((AC, acv),), reads=("mst",), writes=("tmps",))
        P.op("vector",
             (lambda m: lambda v: v.tensor_tensor(m, m, tmps[:, 1:2], OP.max))(mrun),
             reads=("tmps", mname), writes=(mname,), incs=((DV, 1),))
        dvm = DV.inc(1)
        sem = smt.sems[sm]
        tgt = sem.inc(16)
        P.op(wq_eng,
             (lambda ic, th, sm: lambda e: e.dma_start(
                 out=dst_d[ic * 128:(ic + 1) * 128, th * 1024:(th + 1) * 1024],
                 in_=mst[:, sm, :]))(ic, th, sm),
             waits=((AC, acv),), incs=((sem, 16),), reads=("mst",))
        smt.rel[sm] = [(sem, tgt), (DV, dvm)]

    for ic in range(IC):
        for th in range(2):
            ffn_mm_step(wgT, w_wgT, ic, th, mg_d, m2r[:], "m2r",
                        wq_eng=("scalar" if th == 0 else "gpsimd"))
    dv_m2 = DV.val
    w_mg = smt.totals()

    ccw = scalar_ar(4, [(m2r[:], "m2r")], (DV, dv_m2), CC2)
    gw = bcast(4, 0, tmps[:, 2:3], "tmps", CC2, ccw)
    P.op("vector", lambda v: v.tensor_scalar(tmps[:, 3:4], tmps[:, 2:3], 1e-30, None, OP.max),
         waits=(gw,), reads=("tmps",), writes=("tmps",))
    P.op("vector", lambda v: v.reciprocal(c2s[:], tmps[:, 3:4]), reads=("tmps",), writes=("c2s",))
    P.op("vector", lambda v: v.tensor_scalar(c2s[:], c2s[:], QMAX, None, OP.mult),
         reads=("c2s",), writes=("c2s",))
    P.op("vector", lambda v: v.reciprocal(tmps[:, 4:5], rc0[:]), reads=("rc0",), writes=("tmps",))
    P.op("vector", lambda v: v.tensor_tensor(S2s[:], tmps[:, 2:3], tmps[:, 4:5], OP.mult),
         reads=("tmps",), writes=("S2s",))
    P.op("vector", lambda v: v.tensor_scalar(S2s[:], S2s[:], 1.0 / QMAX, None, OP.mult),
         reads=("S2s",), writes=("S2s",))
    P.op("vector", lambda v: v.tensor_scalar(nS2M[:], S2s[:], -MAGIC, None, OP.mult),
         reads=("S2s",), writes=("nS2M",), incs=((DV, 1),))
    dv_s2 = DV.inc(1)

    b2_free = ()
    for ic in range(IC):
        for th in range(2):
            ffn_mm_step(wuT, w_wuT, ic, th, mu_d, m1r[:], "m1r",
                        wq_eng=("scalar" if th == 0 else "gpsimd"))
        s, (lsem, ltgt) = fmg.dma(
            "sync", (lambda ic: lambda e, s: e.dma_start(
                out=rdmg[:, s, :], in_=mg_d[ic * 128:(ic + 1) * 128, :]))(ic),
            extra_waits=w_mg if ic == 0 else ())
        P.op("scalar",
             (lambda s: lambda e: e.activation(rdmg[:, s, :], rdmg[:, s, :], AF.Identity,
                                               scale=c2s[:], bias=magicv[:]))(s),
             waits=((lsem, ltgt), (DV, dv_s2)), reads=("rdmg", "c2s", "magicv"),
             writes=("rdmg",))
        for hf in range(2):
            P.op("scalar",
                 (lambda s, hf: lambda e: e.activation(
                     b2st[:, 0, hf * D:(hf + 1) * D], rdmg[:, s, hf * D:(hf + 1) * D],
                     AF.Sigmoid, bias=nS2M[:], scale=S2s[:]))(s, hf),
                 waits=(b2_free if hf == 0 else ()),
                 reads=("rdmg", "nS2M", "S2s"), writes=("b2st",))
        P.op("scalar",
             lambda e: e.activation(b2st[:, 0, :], b2st[:, 0, :], AF.Identity,
                                    scale=c65k[:], bias=magicv[:]),
             reads=("b2st", "c65k", "magicv"), writes=("b2st",), incs=((AC, 2),))
        ac_sig = AC.inc(2)
        sp2 = (spw.i + 1) % 2
        spw.i += 1
        wfree2 = tuple(spw.rel[sp2])
        spw.rel[sp2] = []
        P.op("vector",
             lambda v: v.tensor_scalar(b2st[:, 0, :], b2st[:, 0, :], MAGIC, None, OP.subtract),
             waits=((AC, ac_sig),), reads=("b2st",), writes=("b2st",))
        P.op("vector",
             (lambda s, sp2: lambda v: v.scalar_tensor_tensor(
                 pwr[:, sp2, :], rdmg[:, s, :], MAGIC, b2st[:, 0, :],
                 OP.subtract, OP.mult))(s, sp2),
             waits=wfree2, reads=("rdmg", "b2st"), writes=("pwr",), incs=((DV, 1),))
        dvp1 = DV.inc(1)
        fmg.release((DV, dvp1))
        b2_free = ((DV, dvp1),)
        P.op("vector",
             (lambda sp2: lambda v: v.tensor_reduce(tmps[:, 2:3], pwr[:, sp2, :], AX.X,
                                                    OP.max, apply_absolute_value=True))(sp2),
             reads=("pwr",), writes=("tmps",))
        P.op("vector", lambda v: v.tensor_tensor(m3r[:], m3r[:], tmps[:, 2:3], OP.max),
             reads=("tmps", "m3r"), writes=("m3r",), incs=((DV, 1),))
        dvp = DV.inc(1)
        sem = spw.sems[sp2]
        tgt = sem.inc(16)
        P.op("scalar" if ic % 2 == 0 else "gpsimd",
             (lambda ic, sp2: lambda e: e.dma_start(out=p_d[ic * 128:(ic + 1) * 128, :],
                                                    in_=pwr[:, sp2, :]))(ic, sp2),
             waits=((DV, dvp1),), incs=((sem, 16),), reads=("pwr",))
        spw.rel[sp2] = [(sem, tgt), (DV, dvp)]
    dv_m13 = DV.val
    w_mu = smt.totals()
    w_p = spw.totals()

    ccw = scalar_ar(8, [(m1r[:], "m1r"), (m3r[:], "m3r")], (DV, dv_m13), CC13)
    gw = bcast(8, 0, tmps[:, 0:1], "tmps", CC13, ccw)
    gw2 = bcast(8, 1, tmps[:, 1:2], "tmps", CC13, ccw)
    P.op("vector", lambda v: v.tensor_scalar(tmps[:, 2:3], tmps[:, 0:1], 1e-30, None, OP.max),
         waits=(gw, gw2), reads=("tmps",), writes=("tmps",))
    P.op("vector", lambda v: v.reciprocal(c1s[:], tmps[:, 2:3]), reads=("tmps",), writes=("c1s",))
    P.op("vector", lambda v: v.tensor_scalar(c1s[:], c1s[:], QMAX, None, OP.mult),
         reads=("c1s",), writes=("c1s",))
    P.op("vector", lambda v: v.tensor_scalar(tmps[:, 3:4], tmps[:, 1:2], 1e-30, None, OP.max),
         reads=("tmps",), writes=("tmps",))
    P.op("vector", lambda v: v.reciprocal(c3s[:], tmps[:, 3:4]), reads=("tmps",), writes=("c3s",))
    P.op("vector", lambda v: v.tensor_scalar(c3s[:], c3s[:], QMAX, None, OP.mult),
         reads=("c3s",), writes=("c3s",))
    P.op("vector", lambda v: v.reciprocal(tmps[:, 4:5], rc0[:]), reads=("rc0",), writes=("tmps",))
    P.op("vector", lambda v: v.tensor_tensor(S4s[:], tmps[:, 0:1], tmps[:, 4:5], OP.mult),
         reads=("tmps",), writes=("S4s",))
    P.op("vector", lambda v: v.tensor_tensor(S4s[:], S4s[:], tmps[:, 1:2], OP.mult),
         reads=("S4s", "tmps"), writes=("S4s",))
    P.op("vector", lambda v: v.tensor_tensor(S4s[:], S4s[:], S2s[:], OP.mult),
         reads=("S4s", "S2s"), writes=("S4s",))
    P.op("vector", lambda v: v.tensor_scalar(S4s[:], S4s[:], 1.0 / (65536.0 * QMAX * QMAX * QMAX),
                                             None, OP.mult),
         reads=("S4s",), writes=("S4s",), incs=((DV, 1),))
    dv_c13 = DV.inc(1)

    wd_dq_v = wdq_o2.rearrange("(o t) d -> o (t d)", t=3)
    b2_free = ()
    for ic in range(IC):
        s, (lps, lpt) = frp.dma(
            "sync", (lambda ic: lambda e, s: e.dma_start(
                out=rdp[:, s, :], in_=p_d[ic * 128:(ic + 1) * 128, :]))(ic),
            extra_waits=(w_p + w_mu) if ic == 0 else ())
        su, (lus, lut) = fru.dma(
            "scalar", (lambda ic: lambda e, s: e.dma_start(
                out=rdmu[:, s, :], in_=mu_d[ic * 128:(ic + 1) * 128, :]))(ic),
            extra_waits=(w_mu + w_p) if ic == 0 else ())
        P.op("scalar",
             (lambda s: lambda e: e.activation(b2st[:, 0, :], rdp[:, s, :], AF.Identity,
                                               bias=magicv[:], scale=c3s[:]))(s),
             waits=((lps, lpt), (DV, dv_c13)) + b2_free, reads=("rdp", "c3s", "magicv"),
             writes=("b2st",), incs=((AC, 1),))
        acv = AC.inc(1)
        frp.release((AC, acv))
        P.op("scalar",
             (lambda su: lambda e: e.activation(rdmu[:, su, :], rdmu[:, su, :], AF.Identity,
                                                bias=magicv[:], scale=c1s[:]))(su),
             waits=((lus, lut),), reads=("rdmu", "c1s", "magicv"),
             writes=("rdmu",), incs=((AC, 1),))
        acu = AC.inc(1)
        sq = (sqw.i + 1) % 2
        sqw.i += 1
        wfreeq = tuple(sqw.rel[sq])
        sqw.rel[sq] = []
        P.op("vector",
             (lambda su: lambda v: v.tensor_scalar(rdmu[:, su, :], rdmu[:, su, :], MAGIC,
                                                   None, OP.subtract))(su),
             waits=((AC, acu),), reads=("rdmu",), writes=("rdmu",))
        P.op("vector",
             (lambda su, sq: lambda v: v.scalar_tensor_tensor(
                 qwr[:, sq, :], b2st[:, 0, :], MAGIC, rdmu[:, su, :],
                 OP.subtract, OP.mult))(su, sq),
             waits=wfreeq, reads=("rdmu", "b2st"), writes=("qwr",), incs=((DV, 1),))
        dvq1 = DV.inc(1)
        fru.release((DV, dvq1))
        b2_free = ((DV, dvq1),)
        P.op("vector",
             (lambda sq: lambda v: v.tensor_reduce(tmps[:, 2:3], qwr[:, sq, :], AX.X,
                                                   OP.max, apply_absolute_value=True))(sq),
             reads=("qwr",), writes=("tmps",))
        P.op("vector", lambda v: v.tensor_tensor(m4r[:], m4r[:], tmps[:, 2:3], OP.max),
             reads=("tmps", "m4r"), writes=("m4r",), incs=((DV, 1),))
        dvq = DV.inc(1)
        sem = sqw.sems[sq]
        tgt = sem.inc(16)
        P.op("gpsimd",
             (lambda ic, sq: lambda e: e.dma_start(out=q_d[ic * 128:(ic + 1) * 128, :],
                                                   in_=qwr[:, sq, :]))(ic, sq),
             waits=((DV, dvq1),), incs=((sem, 16),), reads=("qwr",))
        sqw.rel[sq] = [(sem, tgt), (DV, dvq)]
        fwd.dma("sync",
                (lambda ic: lambda e, s: e.dma_start(
                    out=wdT[:, ic, :], in_=wd_dq_v[:, ic * 128:(ic + 1) * 128],
                    transpose=True))(ic),
                extra_waits=(w_mu + ((CCW2, 1),)) if ic == 0 else ())
    dv_m4 = DV.val
    w_q = sqw.totals()
    w_wdT = fwd.totals()

    ccw = scalar_ar(12, [(m4r[:], "m4r")], (DV, dv_m4), CC4)
    gw = bcast(12, 0, tmps[:, 0:1], "tmps", CC4, ccw)
    P.op("vector", lambda v: v.tensor_scalar(tmps[:, 1:2], tmps[:, 0:1], 1e-30, None, OP.max),
         waits=(gw,), reads=("tmps",), writes=("tmps",))
    P.op("vector", lambda v: v.reciprocal(c4s[:], tmps[:, 1:2]), reads=("tmps",), writes=("c4s",))
    P.op("vector", lambda v: v.tensor_scalar(c4s[:], c4s[:], QMAX, None, OP.mult),
         reads=("c4s",), writes=("c4s",))
    P.op("vector", lambda v: v.tensor_tensor(S4s[:], S4s[:], tmps[:, 0:1], OP.mult),
         reads=("S4s", "tmps"), writes=("S4s",), incs=((DV, 1),))
    dv_c4 = DV.inc(1)

    o_ready = []
    for ic in range(IC):
        s, (lqs, lqt) = frq.dma(
            "sync", (lambda ic: lambda e, s: e.dma_start(
                out=rdq[:, s, :], in_=q_d[ic * 128:(ic + 1) * 128, :]))(ic),
            extra_waits=w_q if ic == 0 else ())
        P.op("vector",
             (lambda s: lambda v: v.tensor_scalar(rdq[:, s, :], rdq[:, s, :], c4s[:], MAGIC,
                                                  OP.mult, OP.add))(s),
             waits=((lqs, lqt), (DV, dv_c4)), reads=("rdq", "c4s"), writes=("rdq",))
        ot, oname = (o_lo, "o_lo") if ic < IC // 2 else (o_hi, "o_hi")
        P.op("vector",
             (lambda s, ic, ot: lambda v: v.tensor_scalar(
                 ot[:, ic % (IC // 2), :], rdq[:, s, :], MAGIC, None, OP.subtract))(s, ic, ot),
             reads=("rdq",), writes=(oname,), incs=((DV, 1),))
        dvo = DV.inc(1)
        frq.release((DV, dvo))
        o_ready.append(dvo)

    prev_grp_dv = 0
    for tg in range(0, NT, 3):
        tts = list(range(tg, min(tg + 3, NT)))
        regs = {}
        lw = {}
        for tt in tts:
            s, (lys, lyt) = fy1.dma(
                "scalar", (lambda tt: lambda e, s: e.dma_start(
                    out=y1rd[:, s, :], in_=y1_d[tt * 128:(tt + 1) * 128, :]))(tt),
                extra_waits=w_y1d if tg == 0 else ())
            lw[tt] = (s, lys, lyt)
            regs[tt], _ = ps_acquire()
        for ic in range(IC):
            ot = o_lo if ic < IC // 2 else o_hi
            for j, tt in enumerate(tts):
                lh = ot[:, ic % (IC // 2), tt * 128:(tt + 1) * 128]
                for hf in range(2):
                    st, sp = ic == 0, ic == IC - 1
                    w0 = ()
                    if hf == 0 and j == 0:
                        w0 = ((DV, o_ready[ic]),) + w_wdT
                        if ic == 0 and prev_grp_dv:
                            w0 = w0 + ((DV, prev_grp_dv),)
                    incs = ()
                    if sp and hf == 1 and j == len(tts) - 1:
                        incs = ((TE, 1),)
                        TE.inc(1)
                    mm(regs[tt][:, hf * 512:(hf + 1) * 512], lh,
                       wdT[:, ic, hf * 512:(hf + 1) * 512], st, sp,
                       waits=w0, incs=incs, skip=True)
        tev = TE.val
        for tt in tts:
            s, lys, lyt = lw[tt]
            so = (syo.i + 1) % 2
            syo.i += 1
            wfreeo = tuple(syo.rel[so])
            syo.rel[so] = []
            P.op("vector",
                 (lambda tt, s, so, r: lambda v: v.scalar_tensor_tensor(
                     youts[:, so, :], r[:], S4s[:], y1rd[:, s, :], OP.mult, OP.add))(tt, s, so, regs[tt]),
                 waits=((TE, tev), (lys, lyt)) + wfreeo,
                 reads=("y1rd", "S4s"), writes=("youts",), incs=((DV, 1),))
            dvv = DV.inc(1)
            fy1.release_slot(s, (DV, dvv))
            sem = syo.sems[so]
            tgt = sem.inc(16)
            P.op("scalar",
                 (lambda tt, so: lambda e: e.dma_start(out=y_out[tt * 128:(tt + 1) * 128, :],
                                                       in_=youts[:, so, :]))(tt, so),
                 waits=((DV, dvv),), incs=((sem, 16),), reads=("youts",))
            syo.rel[so] = [(sem, tgt)]
        prev_grp_dv = DV.val
        rot["rel"] = [((DV, prev_grp_dv),)] * 3
    w_out = syo.totals()

    P.op("gpsimd", lambda g: None, waits=w_out)

    from contextlib import ExitStack
    with ExitStack() as stk:
        for s in P.sems:
            s.h = stk.enter_context(nc.semaphore(s.name))
        block = stk.enter_context(nc.Block())
        P.emit(block)
    return nc


_NC_CACHE = {}


def kernel(L0_attn_x, L0_attn_W_qkv, L0_attn_W_out, L0_attn_b_out,
           L0_ffn_w_gate, L0_ffn_w_up, L0_ffn_w_down):
    x = np.ascontiguousarray(np.asarray(L0_attn_x, dtype=np.float32)).reshape(-1, D)
    n = 8
    tpc = x.shape[0] // n
    assert tpc == T, f"expected {T} tokens/core, got {tpc}"
    if "nc" not in _NC_CACHE:
        _NC_CACHE["nc"] = build()
    nc = _NC_CACHE["nc"]
    w = dict(
        wqkv=np.ascontiguousarray(np.asarray(L0_attn_W_qkv, np.float32)),
        wout=np.ascontiguousarray(np.asarray(L0_attn_W_out, np.float32)),
        bout=np.ascontiguousarray(np.asarray(L0_attn_b_out, np.float32)).reshape(1, D),
    )
    wgu = np.ascontiguousarray(np.vstack([
        np.asarray(L0_ffn_w_gate, np.float32),
        np.asarray(L0_ffn_w_up, np.float32)]))
    wdf = np.ascontiguousarray(np.asarray(L0_ffn_w_down, np.float32)).reshape(-1, D)
    in_maps = [dict(x=np.ascontiguousarray(x[c * tpc:(c + 1) * tpc]),
                    wsh1=np.ascontiguousarray(wgu[c * 768:(c + 1) * 768]),
                    wsh2=np.ascontiguousarray(wdf[c * 384:(c + 1) * 384]),
                    **w)
               for c in range(n)]
    res = run_bass_kernel_spmd(nc, in_maps, core_ids=list(range(n)))
    out = np.concatenate([np.asarray(r["y"]) for r in res.results], axis=0)
    return out.reshape(np.asarray(L0_attn_x).shape).astype(np.float32)


# version 11 (final)
# speedup vs baseline: 1.2601x; 1.0159x over previous
import numpy as np

import concourse.bass as bass
import concourse.mybir as mybir
from concourse.bass_utils import run_bass_kernel_spmd
from concourse.replica_groups import maybe_share_collective_output_space

F32 = mybir.dt.float32
BF16 = mybir.dt.bfloat16
OP = mybir.AluOpType
AX = mybir.AxisListType
AF = mybir.ActivationFunctionType

MAGIC = 12582912.0
QMAX = 32767.0
RG_ALL = [list(range(8))]
RG_HALF = [[0, 1, 2, 3], [4, 5, 6, 7]]

D = 1024
DC = 8
I3 = 3072
IC = 24
NT = 16
T = NT * 128


class Sem:
    def __init__(self, name):
        self.name, self.val, self.h = name, 0, None

    def inc(self, n=1):
        self.val += n
        return self.val


class Prog:
    ENGS = ("sync", "scalar", "vector", "tensor", "gpsimd")

    def __init__(self):
        self.ev = {e: [] for e in self.ENGS}
        self.sems = []

    def sem(self, name):
        s = Sem(name)
        self.sems.append(s)
        return s

    def op(self, eng, fn, waits=(), incs=(), reads=(), writes=()):
        self.ev[eng].append((tuple(waits), fn, tuple(incs),
                             frozenset(reads), frozenset(writes)))

    def emit(self, block):
        def runner(name):
            def run(e):
                dW, dR = set(), set()
                for waits, fn, incs, rd, wr in self.ev[name]:
                    for sem, val in waits:
                        if val > 0:
                            e.wait_ge(sem.h, val)
                    if name in ("vector", "scalar", "gpsimd") and (
                            (wr & (dW | dR)) or (rd & dW)):
                        e.drain()
                        dW, dR = set(), set()
                    ins = fn(e)
                    for sem, n in incs:
                        ins.then_inc(sem.h, n)
                    dW |= wr
                    dR |= rd
            return run
        block.sync(runner("sync"))
        block.scalar(runner("scalar"))
        block.vector(runner("vector"))
        block.tensor(runner("tensor"))
        block.gpsimd(runner("gpsimd"))


class Fam:

    def __init__(self, P, name, nslots):
        self.P = P
        self.sems = [P.sem(f"{name}{i}") for i in range(nslots)]
        self.rel = [[] for _ in range(nslots)]
        self.i = -1

    def dma(self, eng, fn, extra_waits=(), reads=(), writes=()):
        self.i += 1
        s = self.i % len(self.sems)
        waits = tuple(self.rel[s]) + tuple(extra_waits)
        self.rel[s] = []
        sem = self.sems[s]
        tgt = sem.inc(16)
        self.P.op(eng, (lambda fn, s: lambda e: fn(e, s))(fn, s), waits=waits,
                  incs=((sem, 16),), reads=reads, writes=writes)
        return s, (sem, tgt)

    def release(self, *sem_vals):
        self.rel[self.i % len(self.sems)].extend(sem_vals)

    def release_slot(self, slot, *sem_vals):
        self.rel[slot].extend(sem_vals)

    def totals(self):
        return tuple((s, s.val) for s in self.sems if s.val > 0)


def build():
    nc = bass.Bass("TRN2", target_bir_lowering=False)

    x_in = nc.dram_tensor("x", [T, D], F32, kind="ExternalInput").ap()
    wqkv_in = nc.dram_tensor("wqkv", [I3, D], F32, kind="ExternalInput").ap()
    wout_in = nc.dram_tensor("wout", [D, D], F32, kind="ExternalInput").ap()
    bout_in = nc.dram_tensor("bout", [1, D], F32, kind="ExternalInput").ap()
    wsh1_in = nc.dram_tensor("wsh1", [768, D], F32, kind="ExternalInput").ap()
    wsh2_in = nc.dram_tensor("wsh2", [384, D], F32, kind="ExternalInput").ap()
    y_out = nc.dram_tensor("y", [T, D], F32, kind="ExternalOutput").ap()

    wqkv_bf_d = nc.dram_tensor("wqkv_bf_d", [I3, D], BF16)
    wout_bf_d = nc.dram_tensor("wout_bf_d", [D, D], BF16)
    ln1_d = nc.dram_tensor("ln1_d", [T, D], BF16)
    rnd0_d = nc.dram_tensor("rnd0_d", [T, D], BF16)
    wdq_i1 = nc.dram_tensor("wdq_i1", [768, D], BF16)
    wdq_o1 = nc.dram_tensor(
        "wdq_o1", [2 * I3, D], BF16,
        addr_space=maybe_share_collective_output_space("AllGather", RG_ALL))
    wdq_i2 = nc.dram_tensor("wdq_i2", [384, D], BF16)
    wdq_o2 = nc.dram_tensor(
        "wdq_o2", [I3, D], BF16,
        addr_space=maybe_share_collective_output_space("AllGather", RG_ALL))
    y1_d = nc.dram_tensor("y1_d", [T, D], F32)
    mg_d = nc.dram_tensor("mg_d", [I3, T], F32)
    mu_d = nc.dram_tensor("mu_d", [I3, T], F32)
    p_d = nc.dram_tensor("p_d", [I3, T], F32)
    q_d = nc.dram_tensor("q_d", [I3, T], F32)
    kvag_i = nc.dram_tensor("kvag_i", [D, D + 8], BF16)
    kvag_o = nc.dram_tensor(
        "kvag_o", [4 * D, D + 8], BF16,
        addr_space=maybe_share_collective_output_space("AllGather", RG_HALF))
    sc_i = nc.dram_tensor("sc_i", [1, 16], F32)
    sc_o = nc.dram_tensor(
        "sc_o", [1, 16], F32,
        addr_space=maybe_share_collective_output_space("AllReduce", RG_ALL))

    AL = nc.alloc_sbuf_tensor
    sqst = AL("sqst", [128, D], F32)
    t0st = AL("t0st", [128, D], F32)
    ln2st2 = AL("ln2st2", [128, 2 * D], F32)
    boutbf = AL("boutbf", [1, D], BF16)
    ones_row = AL("ones_row", [1, 128], BF16)
    ones_f = AL("ones_f", [1, 128], F32)
    ones_col = AL("ones_col", [128, 1], BF16)
    ksumbf = AL("ksumbf", [128, DC], BF16)
    mean2 = AL("mean2", [128, NT], F32)
    rstd2 = AL("rstd2", [128, NT], F32)
    rstd0 = AL("rstd0", [128, NT], F32)
    negm0 = AL("negm0", [128, NT], F32)
    stat = AL("stat", [128, 8], F32)
    am0 = AL("am0", [128, 1], F32)
    m1r = AL("m1r", [128, 1], F32)
    m2r = AL("m2r", [128, 1], F32)
    m3r = AL("m3r", [128, 1], F32)
    m4r = AL("m4r", [128, 1], F32)
    rc0 = AL("rc0", [128, 1], F32)
    c1s = AL("c1s", [128, 1], F32)
    c2s = AL("c2s", [128, 1], F32)
    c3s = AL("c3s", [128, 1], F32)
    c4s = AL("c4s", [128, 1], F32)
    S2s = AL("S2s", [128, 1], F32)
    nS2M = AL("nS2M", [128, 1], F32)
    S4s = AL("S4s", [128, 1], F32)
    tmps = AL("tmps", [128, 8], F32)
    s1c = AL("s1c", [128, NT], F32)
    s2c = AL("s2c", [128, NT], F32)
    mean1 = AL("mean1", [128, NT], F32)
    rstd1 = AL("rstd1", [128, NT], F32)
    rmaxc = AL("rmaxc", [128, NT], F32)
    rminc = AL("rminc", [128, NT], F32)
    c65k = AL("c65k", [128, 1], F32)
    arsc = AL("arsc", [128, 1], F32)
    epsln = AL("epsln", [128, 1], F32)
    magicv = AL("magicv", [128, 1], F32)

    MBASE = 40928
    used = 229344 - nc.sbuf_bytes_remaining
    assert used <= MBASE, f"smalls overflowed into manual pool: {used}"

    def at(name, shape, dtype, off_kb):
        return nc.alloc_sbuf_tensor_at(name, shape, dtype,
                                       offset=MBASE + int(off_kb * 1024))

    wqkvT = at("wqkvT", [128, DC, I3], BF16, 0)
    ln1T = at("ln1T", [128, DC, T], BF16, 48)
    kact = at("kact", [128, NT, D], BF16, 80)
    vbf = at("vbf", [128, NT, D], BF16, 112)
    qact = at("qact", [128, DC, T], BF16, 80)
    woutT = at("woutT", [128, DC, D], BF16, 112)
    kvbf = at("kvbf", [128, DC, D], BF16, 128)
    attnz = at("attnz", [128, DC, T], BF16, 0)
    zrow = at("zrow", [1, T], F32, 144)
    z_bc = at("z_bc", [128, T], F32, 152)
    y1sb = at("y1sb", [128, NT, D], F32, 32)
    XT = at("XT", [128, DC, T], BF16, 96)
    wgT = at("wgT", [128, DC, I3], BF16, 128)
    wuT = at("wuT", [128, DC, I3], BF16, 0)
    wdT = at("wdT", [128, IC, D], BF16, 96)
    o_lo = at("o_lo", [128, IC // 2, T], BF16, 0)
    o_hi = at("o_hi", [128, IC // 2, T], BF16, 48)
    x_all = at("x_all", [128, NT, D], F32, 32)
    xld = at("xld", [128, 2, D], F32, 144)
    ln1st = at("ln1st", [128, 2, D], BF16, 152)
    kvst = at("kvst", [128, 2, D + 8], BF16, 156)
    kvsum = at("kvsum", [128, 2, 4, D + 8], BF16, 0)
    wldst = at("wldst", [128, 2, 2, D], F32, 160)
    dqst = at("dqst", [128, 2, 2 * D], BF16, 176)
    rnd0st = at("rnd0st", [128, 2, D], BF16, 152)
    mst = at("mst", [128, 4, D], F32, 48)
    rdmg = at("rdmg", [128, 2, 2 * D], F32, 64)
    pwr = at("pwr", [128, 2, 2 * D], F32, 80)
    b2st = at("b2st", [128, 1, 2 * D], F32, 176)
    b2stB = at("b2stB", [128, 1, 2 * D], F32, 144)
    rdp = at("rdp", [128, 2, 2 * D], F32, 48)
    rdmu = at("rdmu", [128, 2, 2 * D], F32, 64)
    qwr = at("qwr", [128, 2, 2 * D], F32, 80)
    rdq = at("rdq", [128, 2, 2 * D], F32, 144)
    y1rd = at("y1rd", [128, 4, D], F32, 160)
    youts = at("youts", [128, 2, D], F32, 176)

    ps = [nc.alloc_psum_tensor(f"ps{i}", [128, D], F32) for i in range(3)]
    ps4 = nc.alloc_psum_tensor("ps4", [128, 512], F32)

    P = Prog()
    TE, DV, AC, GP = P.sem("TE"), P.sem("DV"), P.sem("AC"), P.sem("GP")
    CCK, CC0, CC2, CC13, CC4 = (P.sem(n) for n in
                                ("CCK", "CC0", "CC2", "CC13", "CC4"))
    GC = P.sem("GC")
    GA = P.sem("GA")

    fx = Fam(P, "fx", 2)
    flt = Fam(P, "flt", 1)
    fwq = Fam(P, "fwq", 1)
    fwo = Fam(P, "fwo", 1)
    fxt = Fam(P, "fxt", 1)
    fwg = Fam(P, "fwg", 1)
    fwu = Fam(P, "fwu", 1)
    fwd = Fam(P, "fwd", 1)
    fwl = Fam(P, "fwl", 2)
    fkv = Fam(P, "fkv", 2)
    fmg = Fam(P, "fmg", 2)
    frp = Fam(P, "frp", 2)
    fru = Fam(P, "fru", 2)
    frq = Fam(P, "frq", 2)
    fy1 = Fam(P, "fy1", 4)
    sln = Fam(P, "sln", 2)
    skv = Fam(P, "skv", 2)
    sdq = Fam(P, "sdq", 2)
    sy1 = Fam(P, "sy1", 2)
    sr0 = Fam(P, "sr0", 2)
    smt = Fam(P, "smt", 4)
    spw = Fam(P, "spw", 2)
    sqw = Fam(P, "sqw", 2)
    syo = Fam(P, "syo", 2)

    rot = {"i": -1, "rel": [(), (), ()]}

    def ps_acquire():
        rot["i"] += 1
        r = rot["i"] % 3
        w = rot["rel"][r]
        rot["rel"][r] = ()
        return ps[r], w

    def ps_release(sem, val):
        rot["rel"][rot["i"] % 3] = ((sem, val),)

    def mm(out, lh, rh, st, sp, waits=(), incs=(), skip=False):
        P.op("tensor",
             (lambda o, l, r, s1, s2, sk: lambda t: t.matmul(
                 o, l, r, start=s1, stop=s2, skip_group_check=sk))
             (out, lh, rh, st, sp, skip), waits=waits, incs=incs)

    for tns, val in ((ones_row, 1.0), (ones_f, 1.0), (ones_col, 1.0),
                     (am0, 0.0), (m1r, 0.0), (m2r, 0.0), (m3r, 0.0),
                     (m4r, 0.0), (kvst, 0.0), (epsln, 1e-5), (magicv, MAGIC), (c65k, 65536.0)):
        P.op("vector", (lambda t_, v_: lambda v: v.memset(t_[:], v_))(tns, val),
             writes=(tns.name,))

    P.op("gpsimd", lambda g: g.dma_start(out=boutbf[:], in_=bout_in),
         incs=((GC, 16),), writes=("boutbf",))
    GC.inc(16)
    for j in range(4):
        sl = slice(j * (I3 // 4), (j + 1) * (I3 // 4))
        P.op("gpsimd",
             (lambda sl: lambda g: g.dma_start(out=wqkv_bf_d[sl, :], in_=wqkv_in[sl, :]))(sl),
             incs=((GC, 16),))
        GC.inc(16)
    for j in range(2):
        sl = slice(j * (D // 2), (j + 1) * (D // 2))
        P.op("gpsimd",
             (lambda sl: lambda g: g.dma_start(out=wout_bf_d[sl, :], in_=wout_in[sl, :]))(sl),
             incs=((GC, 16),))
        GC.inc(16)
    g_cast = GC.val

    b2w = AL("b2w", [128, 3 * 64], F32)

    def dequant_chunk(src_ap, dst_ap, nr=2, first_waits=()):
        nb = nr * D // 32
        s, (lsem, ltgt) = fwl.dma(
            "sync", (lambda src_ap, nr: lambda e, s: e.dma_start(
                out=wldst[:, s, 0:nr, :], in_=src_ap))(src_ap, nr),
            extra_waits=first_waits)
        wv = wldst[:, s, 0:nr, :].rearrange("p a (b k) -> p (a b) k", k=32)
        P.op("vector",
             (lambda wv, nb: lambda v: v.tensor_reduce(
                 b2w[:, 0:nb], wv, AX.X, OP.max, apply_absolute_value=True))(wv, nb),
             waits=((lsem, ltgt),), reads=("wldst",), writes=("b2w",))
        P.op("vector",
             (lambda nb: lambda v: v.tensor_scalar(b2w[:, nb:2 * nb], b2w[:, 0:nb],
                                                   1.0 / 7, 1e-12, OP.mult, OP.max))(nb),
             reads=("b2w",), writes=("b2w",))
        P.op("vector",
             (lambda nb: lambda v: v.reciprocal(b2w[:, 2 * nb:3 * nb], b2w[:, nb:2 * nb]))(nb),
             reads=("b2w",), writes=("b2w",))
        P.op("vector",
             (lambda wv, nb, nr: lambda v: v.tensor_tensor(
                 ln2st2[:, 0:nr * D].rearrange("p (b k) -> p b k", k=32), wv,
                 b2w[:, 2 * nb:3 * nb].to_broadcast([128, nb, 32]), OP.mult))(wv, nb, nr),
             reads=("wldst", "b2w"), writes=("ln2st2",), incs=((DV, 1),))
        fwl.release((DV, DV.inc(1)))
        P.op("vector",
             (lambda nr: lambda v: v.tensor_scalar(ln2st2[:, 0:nr * D], ln2st2[:, 0:nr * D],
                                                   MAGIC, MAGIC, OP.add, OP.subtract))(nr),
             reads=("ln2st2",), writes=("ln2st2",))
        sd = (sdq.i + 1) % 2
        sdq.i += 1
        wfree2 = tuple(sdq.rel[sd])
        sdq.rel[sd] = []
        P.op("vector",
             (lambda sd, nb, nr: lambda v: v.tensor_tensor(
                 dqst[:, sd, 0:nr * D].rearrange("p (b k) -> p b k", k=32),
                 ln2st2[:, 0:nr * D].rearrange("p (b k) -> p b k", k=32),
                 b2w[:, nb:2 * nb].to_broadcast([128, nb, 32]), OP.mult))(sd, nb, nr),
             waits=wfree2, reads=("ln2st2", "b2w"), writes=("dqst",), incs=((DV, 1),))
        dvv = DV.inc(1)
        sem = sdq.sems[sd]
        tgt = sem.inc(16)
        P.op("scalar",
             (lambda sd, nr, dst_ap: lambda e: e.dma_start(
                 out=dst_ap, in_=dqst[:, sd, 0:nr * D]))(sd, nr, dst_ap),
             waits=((DV, dvv),), incs=((sem, 16),), reads=("dqst",))
        sdq.rel[sd] = [(sem, tgt)]

    def rows2(w_dram, a, nr=2):
        return w_dram.rearrange("(a p) d -> p a d", p=128)[:, 2 * a:2 * a + nr, :]
    CCW1 = P.sem("CCW1")

    x_t = x_in.rearrange("(n p) d -> n p d", p=128)
    x_all3 = x_in.rearrange("(n p) d -> p n d", p=128)
    _, (lxs, lxt) = fx.dma(
        "sync", lambda e, s: e.dma_start(out=x_all[:], in_=x_all3), writes=("x_all",))
    for tt in range(NT):
        P.op("vector",
             (lambda tt: lambda v: v.reduce_sum(s1c[:, tt:tt + 1], x_all[:, tt, :], AX.X))(tt),
             waits=((lxs, lxt),) if tt == 0 else (), reads=("x_all",), writes=("s1c",))
        P.op("scalar",
             (lambda tt: lambda e: e.activation(sqst[:], x_all[:, tt, :], AF.Square,
                                                accum_out=s2c[:, tt:tt + 1]))(tt),
             waits=((lxs, lxt),) if tt == 0 else (), reads=("x_all",),
             writes=("sqst", "s2c"),
             incs=((AC, 1),) if tt == NT - 1 else ())
    acv = AC.inc(1)
    for a in range(3):
        dequant_chunk(rows2(wsh1_in, a),
                      wdq_i1.rearrange("(a p) d -> p a d", p=128)[:, 2 * a:2 * a + 2, :])
    w_dq1 = sdq.totals()
    P.op("gpsimd", lambda g: g.collective_compute(
        "AllGather", OP.bypass, replica_groups=RG_ALL,
        ins=[wdq_i1[:, :].opt()], outs=[wdq_o1[:, :].opt()]),
        waits=w_dq1, incs=((CCW1, 1),))
    CCW1.inc(1)
    dequant_chunk(rows2(wsh2_in, 0),
                  wdq_i2.rearrange("(a p) d -> p a d", p=128)[:, 0:2, :])
    dequant_chunk(rows2(wsh2_in, 1, nr=1),
                  wdq_i2.rearrange("(a p) d -> p a d", p=128)[:, 2:3, :], nr=1)
    w_dq2 = sdq.totals()
    P.op("vector", lambda v: v.tensor_scalar(mean1[:], s1c[:], 1.0 / D, None, OP.mult),
         waits=((AC, acv),), reads=("s1c",), writes=("mean1",))
    P.op("vector", lambda v: v.tensor_scalar(s2c[:], s2c[:], 1.0 / D, None, OP.mult),
         reads=("s2c",), writes=("s2c",))
    P.op("vector", lambda v: v.tensor_tensor(s1c[:], mean1[:], mean1[:], OP.mult),
         reads=("mean1",), writes=("s1c",))
    P.op("vector", lambda v: v.tensor_tensor(s2c[:], s2c[:], s1c[:], OP.subtract),
         reads=("s2c", "s1c"), writes=("s2c",), incs=((DV, 1),))
    dvv = DV.inc(1)
    P.op("scalar", lambda e: e.activation(s2c[:], s2c[:], AF.Sqrt, bias=epsln[:]),
         waits=((DV, dvv),), reads=("s2c", "epsln"), writes=("s2c",), incs=((AC, 1),))
    acv = AC.inc(1)
    P.op("vector", lambda v: v.reciprocal(rstd1[:], s2c[:]),
         waits=((AC, acv),), reads=("s2c",), writes=("rstd1",), incs=((DV, 1),))
    DV.inc(1)
    for tt in range(NT):
        sl2 = tt % 2
        wfree2 = tuple(sln.rel[sl2])
        sln.rel[sl2] = []
        P.op("vector",
             (lambda tt, sl2: lambda v: v.tensor_scalar(ln1st[:, sl2, :], x_all[:, tt, :],
                                                        mean1[:, tt:tt + 1], rstd1[:, tt:tt + 1],
                                                        OP.subtract, OP.mult))(tt, sl2),
             waits=wfree2, reads=("x_all", "mean1", "rstd1"), writes=("ln1st",), incs=((DV, 1),))
        dvv = DV.inc(1)
        sem = sln.sems[sl2]
        tgt = sem.inc(16)
        P.op("scalar",
             (lambda tt, sl2: lambda e: e.dma_start(out=ln1_d[tt * 128:(tt + 1) * 128, :],
                                                    in_=ln1st[:, sl2, :]))(tt, sl2),
             waits=((DV, dvv),), incs=((sem, 16),), reads=("ln1st",))
        sln.i = tt
        sln.rel[sl2] = [(sem, tgt)]
    dv_ln1 = DV.val
    w_ln1d = sln.totals()

    ln1T_w, wqkv_w = [], []
    for dc in range(DC):
        _, wl = flt.dma("sync",
                        (lambda dc: lambda e, s: e.dma_start(
                            out=ln1T[:, dc, :], in_=ln1_d[:, dc * 128:(dc + 1) * 128],
                            transpose=True))(dc),
                        extra_waits=w_ln1d if dc == 0 else ())
        ln1T_w.append(wl)
        _, wq2 = fwq.dma("sync",
                         (lambda dc: lambda e, s: e.dma_start(
                             out=wqkvT[:, dc, :], in_=wqkv_bf_d[:, dc * 128:(dc + 1) * 128],
                             transpose=True))(dc),
                         extra_waits=((GC, g_cast),) if dc == 0 else ())
        wqkv_w.append(wq2)
    w_ln1T = flt.totals()
    w_wqkvT = fwq.totals()

    for tt in range(NT):
        rk, wk = ps_acquire()
        rv, wv = ps_acquire()
        for dc in range(DC):
            lh = ln1T[:, dc, tt * 128:(tt + 1) * 128]
            st, sp = dc == 0, dc == DC - 1
            if tt == 0:
                w0 = (ln1T_w[dc], wqkv_w[dc]) + ((wk + wv) if dc == 0 else ())
            else:
                w0 = (wk + wv) if dc == 0 else ()
            for hf in range(2):
                mm(rk[:, hf * 512:(hf + 1) * 512], lh,
                   wqkvT[:, dc, D + hf * 512:D + (hf + 1) * 512], st, sp,
                   waits=w0 if hf == 0 else (), skip=True)
                w0 = ()
                incs = ()
                if sp and hf == 1:
                    incs = ((TE, 1),)
                    TE.inc(1)
                mm(rv[:, hf * 512:(hf + 1) * 512], lh,
                   wqkvT[:, dc, 2 * D + hf * 512:2 * D + (hf + 1) * 512], st, sp,
                   incs=incs, skip=True)
        tev = TE.val
        P.op("vector",
             (lambda tt, rk: lambda v: v.tensor_scalar(kact[:, tt, :], rk[:], 0.0, 1e-6,
                                                       OP.max, OP.add))(tt, rk),
             waits=((TE, tev),), writes=("kact",), incs=((DV, 1),))
        dv_k = DV.inc(1)
        P.op("scalar",
             (lambda tt, rv: lambda e: e.copy(vbf[:, tt, :], rv[:]))(tt, rv),
             waits=((TE, tev),), writes=("vbf",), incs=((AC, 1),))
        ac_v = AC.inc(1)
        rot["rel"][(rot["i"] - 1) % 3] = ((DV, dv_k),)
        rot["rel"][rot["i"] % 3] = ((AC, ac_v),)
    dv_2a, ac_2a = DV.val, AC.val

    for dc in range(DC):
        r, wfree = ps_acquire()
        for tt in range(NT):
            lh = kact[:, tt, dc * 128:(dc + 1) * 128]
            st, sp = tt == 0, tt == NT - 1
            w0 = ((DV, dv_2a), (AC, ac_2a)) + wfree if tt == 0 else ()
            mm(r[:, 0:512], lh, vbf[:, tt, 0:512], st, sp, waits=w0, skip=True)
            mm(r[:, 512:1024], lh, vbf[:, tt, 512:1024], st, sp, skip=True)
            incs = ()
            if sp:
                incs = ((TE, 1),)
                TE.inc(1)
            mm(ps4[:, dc:dc + 1], lh, ones_col[:], st, sp, incs=incs, skip=True)
        tev = TE.val
        sl2 = dc % 2
        wfree2 = tuple(skv.rel[sl2])
        skv.rel[sl2] = []
        P.op("vector",
             (lambda sl2, r: lambda v: v.tensor_copy(kvst[:, sl2, 0:D], r[:]))(sl2, r),
             waits=((TE, tev),) + wfree2, writes=("kvst",))
        P.op("vector",
             (lambda sl2, dc: lambda v: v.tensor_copy(kvst[:, sl2, D:D + 1], ps4[:, dc:dc + 1]))(sl2, dc),
             writes=("kvst",), incs=((DV, 1),))
        dvv = DV.inc(1)
        ps_release(DV, dvv)
        sem = skv.sems[sl2]
        tgt = sem.inc(16)
        P.op("scalar",
             (lambda sl2, dc: lambda e: e.dma_start(out=kvag_i[dc * 128:(dc + 1) * 128, :],
                                                    in_=kvst[:, sl2, :]))(sl2, dc),
             waits=((DV, dvv),), incs=((sem, 16),), reads=("kvst",))
        skv.i = dc
        skv.rel[sl2] = [(sem, tgt)]
    w_kvag = skv.totals()
    P.op("gpsimd", lambda g: g.collective_compute(
        "AllGather", OP.bypass, replica_groups=RG_HALF,
        ins=[kvag_i[:, :].opt()], outs=[kvag_o[:, :].opt()]),
        waits=w_kvag, incs=((CCK, 1),))
    CCK.inc(1)
    CCW2 = P.sem("CCW2")
    P.op("gpsimd", lambda g: g.collective_compute(
        "AllGather", OP.bypass, replica_groups=RG_ALL,
        ins=[wdq_i2[:, :].opt()], outs=[wdq_o2[:, :].opt()]),
        waits=w_dq2, incs=((CCW2, 1),))
    CCW2.inc(1)

    for do in range(DC):
        for th in range(2):
            r, wfree = ps_acquire()
            for dc in range(DC):
                st, sp = dc == 0, dc == DC - 1
                for hq in range(2):
                    w0 = wfree if (dc == 0 and hq == 0) else ()
                    incs = ()
                    if sp and hq == 1:
                        incs = ((TE, 1),)
                        TE.inc(1)
                    mm(r[:, hq * 512:(hq + 1) * 512],
                       wqkvT[:, dc, do * 128:(do + 1) * 128],
                       ln1T[:, dc, th * 1024 + hq * 512:th * 1024 + (hq + 1) * 512],
                       st, sp, waits=w0, incs=incs)
            tev = TE.val
            P.op("vector",
                 (lambda do, th, r: lambda v: v.tensor_scalar(
                     qact[:, do, th * 1024:(th + 1) * 1024], r[:], 0.0, 1e-6,
                     OP.max, OP.add))(do, th, r),
                 waits=((TE, tev),), writes=("qact",), incs=((DV, 1),))
            ps_release(DV, DV.inc(1))
    dv_q = DV.val

    for dc in range(DC):
        fwo.dma("sync",
                (lambda dc: lambda e, s: e.dma_start(
                    out=woutT[:, dc, :], in_=wout_bf_d[:, dc * 128:(dc + 1) * 128],
                    transpose=True))(dc),
                extra_waits=(((GC, g_cast),) + w_kvag) if dc == 0 else ())
    w_woutT = fwo.totals()

    GS = P.sem("GS")
    kvo = kvag_o.rearrange("(r c p) e -> p r c e", p=128, c=DC)
    for dc in range(DC):
        s, (lsem, ltgt) = fkv.dma(
            "sync", (lambda dc: lambda e, s: e.dma_start(out=kvsum[:, s], in_=kvo[:, :, dc, :]))(dc),
            extra_waits=((CCK, 1), (DV, dv_q)) if dc == 0 else ())
        P.op("vector",
             (lambda s: lambda v: v.tensor_add(kvsum[:, s, 0, :], kvsum[:, s, 0, :],
                                               kvsum[:, s, 1, :]))(s),
             waits=((lsem, ltgt),), reads=("kvsum",), writes=("kvsum",))
        P.op("vector",
             (lambda s: lambda v: v.tensor_add(kvsum[:, s, 2, :], kvsum[:, s, 2, :],
                                               kvsum[:, s, 3, :]))(s),
             reads=("kvsum",), writes=("kvsum",))
        P.op("vector",
             (lambda dc, s: lambda v: v.tensor_add(kvbf[:, dc, :], kvsum[:, s, 0, 0:D],
                                                   kvsum[:, s, 2, 0:D]))(dc, s),
             reads=("kvsum",), writes=("kvbf",))
        P.op("vector",
             (lambda dc, s: lambda v: v.tensor_add(ksumbf[:, dc:dc + 1], kvsum[:, s, 0, D:D + 1],
                                                   kvsum[:, s, 2, D:D + 1]))(dc, s),
             reads=("kvsum",), writes=("ksumbf",), incs=((GS, 1),))
        fkv.release((GS, GS.inc(1)))
    gs_kv = GS.val

    for tq in range(4):
        for dc in range(DC):
            st, sp = dc == 0, dc == DC - 1
            if tq == 0 and dc == 0:
                w0 = ((GS, gs_kv),)
            elif dc == 0:
                w0 = ((DV, DV.val),)
            else:
                w0 = ()
            incs = ()
            if sp:
                incs = ((TE, 1),)
                TE.inc(1)
            mm(ps4[0:1, 0:512], ksumbf[:, dc:dc + 1],
               qact[:, dc, tq * 512:(tq + 1) * 512], st, sp, waits=w0, incs=incs,
               skip=True)
        P.op("vector",
             (lambda tq: lambda v: v.tensor_scalar(zrow[0:1, tq * 512:(tq + 1) * 512],
                                                   ps4[0:1, 0:512], 1e-6, None, OP.add))(tq),
             waits=((TE, TE.val),), writes=("zrow",), incs=((DV, 1),))
        DV.inc(1)
    P.op("vector", lambda v: v.reciprocal(zrow[:], zrow[:]),
         reads=("zrow",), writes=("zrow",), incs=((DV, 1),))
    dv_z = DV.inc(1)
    for tq in range(4):
        TE.inc(1)
        mm(ps4[:, 0:512], ones_f[:], zrow[0:1, tq * 512:(tq + 1) * 512], True, True,
           waits=((DV, dv_z if tq == 0 else DV.val),), incs=((TE, 1),))
        P.op("vector",
             (lambda tq: lambda v: v.tensor_copy(z_bc[:, tq * 512:(tq + 1) * 512], ps4[:, 0:512]))(tq),
             waits=((TE, TE.val),), writes=("z_bc",), incs=((DV, 1),))
        DV.inc(1)
    dv_zbc = DV.val

    for ec in range(DC):
        for th in range(2):
            r, wfree = ps_acquire()
            for dc in range(DC):
                st, sp = dc == 0, dc == DC - 1
                for hq in range(2):
                    w0 = (((DV, dv_zbc),) + wfree) if (dc == 0 and hq == 0) else ()
                    incs = ()
                    if sp and hq == 1:
                        incs = ((TE, 1),)
                        TE.inc(1)
                    mm(r[:, hq * 512:(hq + 1) * 512],
                       kvbf[:, dc, ec * 128:(ec + 1) * 128],
                       qact[:, dc, th * 1024 + hq * 512:th * 1024 + (hq + 1) * 512],
                       st, sp, waits=w0, incs=incs)
            P.op("vector",
                 (lambda ec, th, r: lambda v: v.tensor_tensor(
                     attnz[:, ec, th * 1024:(th + 1) * 1024], r[:],
                     z_bc[:, th * 1024:(th + 1) * 1024], OP.mult))(ec, th, r),
                 waits=((TE, TE.val),), reads=("z_bc",), writes=("attnz",), incs=((DV, 1),))
            ps_release(DV, DV.inc(1))
    dv_attnz = DV.val

    for tt in range(NT):
        s, (lsem, ltgt) = fx.dma(
            "sync", (lambda tt: lambda e, s: e.dma_start(out=xld[:, s, :], in_=x_t[tt]))(tt),
            extra_waits=((DV, dv_attnz),) if tt == 0 else ())
        r, wfree2 = ps_acquire()
        for ec in range(DC):
            st = ec == 0
            for hf in range(2):
                w0 = (((DV, dv_attnz),) + w_woutT + wfree2) if (ec == 0 and hf == 0) else ()
                mm(r[:, hf * 512:(hf + 1) * 512],
                   attnz[:, ec, tt * 128:(tt + 1) * 128],
                   woutT[:, ec, hf * 512:(hf + 1) * 512], st, False, waits=w0, skip=True)
        for hf in range(2):
            incs = ()
            if hf == 1:
                incs = ((TE, 1),)
                TE.inc(1)
            mm(r[:, hf * 512:(hf + 1) * 512], ones_row[:],
               boutbf[0:1, hf * 512:(hf + 1) * 512], False, True, incs=incs, skip=True)
        tev = TE.val
        P.op("vector",
             (lambda tt, s, r: lambda v: v.tensor_tensor(y1sb[:, tt, :], r[:], xld[:, s, :], OP.add))(tt, s, r),
             waits=((TE, tev), (lsem, ltgt)), reads=("xld",), writes=("y1sb",), incs=((DV, 1),))
        dv_y1 = DV.inc(1)
        fx.release((DV, dv_y1))
        ps_release(DV, dv_y1)
        sl2 = tt % 2
        wfree3 = tuple(sy1.rel[sl2])
        sy1.rel[sl2] = []
        sem = sy1.sems[sl2]
        tgt = sem.inc(16)
        P.op("scalar",
             (lambda tt: lambda e: e.dma_start(out=y1_d[tt * 128:(tt + 1) * 128, :],
                                               in_=y1sb[:, tt, :]))(tt),
             waits=((DV, dv_y1),) + wfree3, incs=((sem, 16),), reads=("y1sb",))
        sy1.i = tt
        sy1.rel[sl2] = [(sem, tgt)]
        P.op("scalar",
             (lambda tt: lambda e: e.activation(sqst[:], y1sb[:, tt, :], AF.Square,
                                                accum_out=s2c[:, tt:tt + 1]))(tt),
             waits=((DV, dv_y1),), reads=("y1sb",), writes=("sqst", "s2c"),
             incs=((AC, 1),) if tt == NT - 1 else ())
        P.op("vector",
             (lambda tt: lambda v: v.reduce_sum(s1c[:, tt:tt + 1], y1sb[:, tt, :], AX.X))(tt),
             reads=("y1sb",), writes=("s1c",))
        P.op("vector",
             (lambda tt: lambda v: v.tensor_reduce(rmaxc[:, tt:tt + 1], y1sb[:, tt, :],
                                                   AX.X, OP.max))(tt),
             reads=("y1sb",), writes=("rmaxc",))
        P.op("vector",
             (lambda tt: lambda v: v.tensor_reduce(rminc[:, tt:tt + 1], y1sb[:, tt, :],
                                                   AX.X, OP.min))(tt),
             reads=("y1sb",), writes=("rminc",))
    ac_sq = AC.inc(1)
    P.op("vector", lambda v: v.tensor_scalar(mean2[:], s1c[:], 1.0 / D, None, OP.mult),
         waits=((AC, ac_sq),), reads=("s1c",), writes=("mean2",))
    P.op("vector", lambda v: v.tensor_scalar(s2c[:], s2c[:], 1.0 / D, None, OP.mult),
         reads=("s2c",), writes=("s2c",))
    P.op("vector", lambda v: v.tensor_tensor(s1c[:], mean2[:], mean2[:], OP.mult),
         reads=("mean2",), writes=("s1c",))
    P.op("vector", lambda v: v.tensor_tensor(s2c[:], s2c[:], s1c[:], OP.subtract),
         reads=("s2c", "s1c"), writes=("s2c",), incs=((DV, 1),))
    dvv = DV.inc(1)
    P.op("scalar", lambda e: e.activation(s2c[:], s2c[:], AF.Sqrt, bias=epsln[:]),
         waits=((DV, dvv),), reads=("s2c", "epsln"), writes=("s2c",), incs=((AC, 1),))
    acv = AC.inc(1)
    P.op("vector", lambda v: v.reciprocal(rstd2[:], s2c[:]),
         waits=((AC, acv),), reads=("s2c",), writes=("rstd2",))
    P.op("vector", lambda v: v.tensor_tensor(rmaxc[:], rmaxc[:], mean2[:], OP.subtract),
         reads=("rmaxc", "mean2"), writes=("rmaxc",))
    P.op("vector", lambda v: v.tensor_tensor(rminc[:], mean2[:], rminc[:], OP.subtract),
         reads=("rminc", "mean2"), writes=("rminc",))
    P.op("vector", lambda v: v.tensor_tensor(rmaxc[:], rmaxc[:], rminc[:], OP.max),
         reads=("rmaxc", "rminc"), writes=("rmaxc",))
    P.op("vector", lambda v: v.tensor_tensor(rmaxc[:], rmaxc[:], rstd2[:], OP.mult),
         reads=("rmaxc", "rstd2"), writes=("rmaxc",))
    P.op("vector", lambda v: v.reduce_max(am0[:], rmaxc[:], AX.X),
         reads=("rmaxc",), writes=("am0",), incs=((DV, 1),))
    DV.inc(1)
    dv_s0 = DV.val
    w_y1d = sy1.totals()

    def scalar_ar(cols, maxes, wait, ccsem):
        for j, (ap_, nm) in enumerate(maxes):
            P.op("gpsimd",
                 (lambda ap_: lambda g: g.tensor_reduce(arsc[0:1, 0:1], ap_, AX.XYZWC, OP.max))(ap_),
                 waits=(wait,) if j == 0 else (), reads=(nm,), writes=("arsc",))
            P.op("gpsimd",
                 (lambda cols, j: lambda g: g.dma_start(
                     out=sc_i[0:1, cols + j:cols + j + 1], in_=arsc[0:1, 0:1]))(cols, j),
                 incs=((GA, 16),), reads=("arsc",))
            P.op("gpsimd", lambda g: None, waits=((GA, GA.inc(16)),))
        n = len(maxes)
        P.op("gpsimd", (lambda n, cols: lambda g: g.collective_compute(
            "AllReduce", OP.max, replica_groups=RG_ALL,
            ins=[sc_i[0:1, cols:cols + n].opt()],
            outs=[sc_o[0:1, cols:cols + n].opt()]))(n, cols),
            incs=((ccsem, 1),))
        return ccsem.inc(1)

    def bcast(cols, j, dst, dname, ccsem, ccw):
        P.op("gpsimd",
             (lambda dst, cols, j: lambda g: g.dma_start(
                 out=dst, in_=sc_o[0:1, cols + j:cols + j + 1].to_broadcast([128, 1])))(dst, cols, j),
             waits=((ccsem, ccw), (GA, GA.val)), incs=((GA, 16),), writes=(dname,))
        return (GA, GA.inc(16))

    ccw = scalar_ar(0, [(am0[:], "am0")], (DV, dv_s0), CC0)
    gw = bcast(0, 0, tmps[:, 0:1], "tmps", CC0, ccw)
    P.op("vector", lambda v: v.tensor_scalar(tmps[:, 1:2], tmps[:, 0:1], 1.0 / QMAX, 1e-12,
                                             OP.mult, OP.max),
         waits=(gw,), reads=("tmps",), writes=("tmps",))
    P.op("vector", lambda v: v.reciprocal(rc0[:], tmps[:, 1:2]),
         reads=("tmps",), writes=("rc0",), incs=((DV, 1),))
    dv_rc0 = DV.inc(1)

    P.op("vector", lambda v: v.tensor_tensor(rstd0[:], rstd2[:],
                                             rc0[:].to_broadcast([128, NT]), OP.mult),
         waits=((DV, dv_rc0),), reads=("rstd2", "rc0"), writes=("rstd0",))
    P.op("vector", lambda v: v.tensor_tensor(negm0[:], mean2[:], rstd0[:], OP.mult),
         reads=("mean2", "rstd0"), writes=("negm0",))
    P.op("vector", lambda v: v.tensor_scalar(negm0[:], negm0[:], -1.0, MAGIC, OP.mult, OP.add),
         reads=("negm0",), writes=("negm0",), incs=((DV, 1),))
    dv_q0 = DV.inc(1)
    for tt in range(NT):
        P.op("scalar",
             (lambda tt: lambda e: e.activation(t0st[:], y1sb[:, tt, :], AF.Identity,
                                                bias=negm0[:, tt:tt + 1],
                                                scale=rstd0[:, tt:tt + 1]))(tt),
             waits=((DV, dv_q0 if tt == 0 else DV.val),),
             reads=("y1sb", "negm0", "rstd0"), writes=("t0st",), incs=((AC, 1),))
        acv = AC.inc(1)
        sl2 = tt % 2
        wfree = tuple(sr0.rel[sl2])
        sr0.rel[sl2] = []
        P.op("vector",
             (lambda sl2: lambda v: v.tensor_scalar(rnd0st[:, sl2, :], t0st[:], MAGIC, None,
                                                    OP.subtract))(sl2),
             waits=((AC, acv),) + wfree, reads=("t0st",), writes=("rnd0st",), incs=((DV, 1),))
        dvv = DV.inc(1)
        sem = sr0.sems[sl2]
        tgt = sem.inc(16)
        P.op("scalar",
             (lambda tt, sl2: lambda e: e.dma_start(out=rnd0_d[tt * 128:(tt + 1) * 128, :],
                                                    in_=rnd0st[:, sl2, :]))(tt, sl2),
             waits=((DV, dvv),), incs=((sem, 16),), reads=("rnd0st",))
        sr0.i = tt
        sr0.rel[sl2] = [(sem, tgt)]
    w_rnd0 = sr0.totals()

    for dc in range(DC):
        fxt.dma("sync",
                (lambda dc: lambda e, s: e.dma_start(
                    out=XT[:, dc, :], in_=rnd0_d[:, dc * 128:(dc + 1) * 128],
                    transpose=True))(dc),
                extra_waits=w_rnd0 if dc == 0 else ())
    w_XT = fxt.totals()
    for hc in range(DC):
        fwg.dma("sync",
                (lambda hc: lambda e, s: e.dma_start(
                    out=wgT[:, hc, :], in_=wdq_o1[0:I3, hc * 128:(hc + 1) * 128],
                    transpose=True))(hc),
                extra_waits=((CCW1, 1), (DV, dv_attnz)))
    w_wgT = fwg.totals()
    for hc in range(DC):
        fwu.dma("sync",
                (lambda hc: lambda e, s: e.dma_start(
                    out=wuT[:, hc, :], in_=wdq_o1[I3:2 * I3, hc * 128:(hc + 1) * 128],
                    transpose=True))(hc),
                extra_waits=(((CCW1, 1),) + w_y1d) if hc == 0 else ())
    w_wuT = fwu.totals()

    def ffn_mm_step(wT, w_w, ic, th, dst_d, mrun, mname, wq_eng="scalar"):
        r, wfree = ps_acquire()
        for hc in range(DC):
            st, sp = hc == 0, hc == DC - 1
            for hq in range(2):
                w0 = (w_w + w_XT + wfree) if (hc == 0 and hq == 0) else ()
                incs = ()
                if sp and hq == 1:
                    incs = ((TE, 1),)
                    TE.inc(1)
                mm(r[:, hq * 512:(hq + 1) * 512], wT[:, hc, ic * 128:(ic + 1) * 128],
                   XT[:, hc, th * 1024 + hq * 512:th * 1024 + (hq + 1) * 512],
                   st, sp, waits=w0, incs=incs)
        tev = TE.val
        sm = (smt.i + 1) % 4
        smt.i += 1
        wfree2 = tuple(smt.rel[sm])
        smt.rel[sm] = []
        P.op("scalar",
             (lambda sm, r: lambda e: e.copy(mst[:, sm, :], r[:]))(sm, r),
             waits=((TE, tev),) + wfree2, writes=("mst",), incs=((AC, 1),))
        acv = AC.inc(1)
        ps_release(AC, acv)
        P.op("vector",
             (lambda sm: lambda v: v.tensor_reduce(tmps[:, 1:2], mst[:, sm, :], AX.X,
                                                   OP.max, apply_absolute_value=True))(sm),
             waits=((AC, acv),), reads=("mst",), writes=("tmps",))
        P.op("vector",
             (lambda m: lambda v: v.tensor_tensor(m, m, tmps[:, 1:2], OP.max))(mrun),
             reads=("tmps", mname), writes=(mname,), incs=((DV, 1),))
        dvm = DV.inc(1)
        sem = smt.sems[sm]
        tgt = sem.inc(16)
        P.op(wq_eng,
             (lambda ic, th, sm: lambda e: e.dma_start(
                 out=dst_d[ic * 128:(ic + 1) * 128, th * 1024:(th + 1) * 1024],
                 in_=mst[:, sm, :]))(ic, th, sm),
             waits=((AC, acv),), incs=((sem, 16),), reads=("mst",))
        smt.rel[sm] = [(sem, tgt), (DV, dvm)]

    for ic in range(IC):
        for th in range(2):
            ffn_mm_step(wgT, w_wgT, ic, th, mg_d, m2r[:], "m2r",
                        wq_eng=("scalar" if th == 0 else "gpsimd"))
    dv_m2 = DV.val
    w_mg = smt.totals()

    ccw = scalar_ar(4, [(m2r[:], "m2r")], (DV, dv_m2), CC2)
    gw = bcast(4, 0, tmps[:, 2:3], "tmps", CC2, ccw)
    P.op("vector", lambda v: v.tensor_scalar(tmps[:, 3:4], tmps[:, 2:3], 1e-30, None, OP.max),
         waits=(gw,), reads=("tmps",), writes=("tmps",))
    P.op("vector", lambda v: v.reciprocal(c2s[:], tmps[:, 3:4]), reads=("tmps",), writes=("c2s",))
    P.op("vector", lambda v: v.tensor_scalar(c2s[:], c2s[:], QMAX, None, OP.mult),
         reads=("c2s",), writes=("c2s",))
    P.op("vector", lambda v: v.reciprocal(tmps[:, 4:5], rc0[:]), reads=("rc0",), writes=("tmps",))
    P.op("vector", lambda v: v.tensor_tensor(S2s[:], tmps[:, 2:3], tmps[:, 4:5], OP.mult),
         reads=("tmps",), writes=("S2s",))
    P.op("vector", lambda v: v.tensor_scalar(S2s[:], S2s[:], 1.0 / QMAX, None, OP.mult),
         reads=("S2s",), writes=("S2s",))
    P.op("vector", lambda v: v.tensor_scalar(nS2M[:], S2s[:], -MAGIC, None, OP.mult),
         reads=("S2s",), writes=("nS2M",), incs=((DV, 1),))
    dv_s2 = DV.inc(1)

    b2_free = {}
    for ic in range(IC):
        for th in range(2):
            ffn_mm_step(wuT, w_wuT, ic, th, mu_d, m1r[:], "m1r",
                        wq_eng=("scalar" if th == 0 else "gpsimd"))
        s, (lsem, ltgt) = fmg.dma(
            "sync", (lambda ic: lambda e, s: e.dma_start(
                out=rdmg[:, s, :], in_=mg_d[ic * 128:(ic + 1) * 128, :]))(ic),
            extra_waits=w_mg if ic == 0 else ())
        bb, bn = (b2st, "b2st") if ic % 2 == 0 else (b2stB, "b2stB")
        P.op("scalar",
             (lambda s: lambda e: e.activation(rdmg[:, s, :], rdmg[:, s, :], AF.Identity,
                                               scale=c2s[:], bias=magicv[:]))(s),
             waits=((lsem, ltgt), (DV, dv_s2)), reads=("rdmg", "c2s", "magicv"),
             writes=("rdmg",))
        for hf in range(2):
            P.op("scalar",
                 (lambda s, hf, bb: lambda e: e.activation(
                     bb[:, 0, hf * D:(hf + 1) * D], rdmg[:, s, hf * D:(hf + 1) * D],
                     AF.Sigmoid, bias=nS2M[:], scale=S2s[:]))(s, hf, bb),
                 waits=(b2_free.get(ic % 2, ()) if hf == 0 else ()),
                 reads=("rdmg", "nS2M", "S2s"), writes=(bn,))
        P.op("scalar",
             (lambda bb: lambda e: e.activation(bb[:, 0, :], bb[:, 0, :], AF.Identity,
                                                scale=c65k[:], bias=magicv[:]))(bb),
             reads=(bn, "c65k", "magicv"), writes=(bn,), incs=((AC, 2),))
        ac_sig = AC.inc(2)
        sp2 = (spw.i + 1) % 2
        spw.i += 1
        wfree2 = tuple(spw.rel[sp2])
        spw.rel[sp2] = []
        P.op("vector",
             (lambda bb: lambda v: v.tensor_scalar(bb[:, 0, :], bb[:, 0, :], MAGIC, None,
                                                   OP.subtract))(bb),
             waits=((AC, ac_sig),), reads=(bn,), writes=(bn,))
        P.op("vector",
             (lambda s, sp2, bb: lambda v: v.scalar_tensor_tensor(
                 pwr[:, sp2, :], rdmg[:, s, :], MAGIC, bb[:, 0, :],
                 OP.subtract, OP.mult))(s, sp2, bb),
             waits=wfree2, reads=("rdmg", bn), writes=("pwr",), incs=((DV, 1),))
        dvp1 = DV.inc(1)
        fmg.release((DV, dvp1))
        b2_free[ic % 2] = ((DV, dvp1),)
        P.op("vector",
             (lambda sp2: lambda v: v.tensor_reduce(tmps[:, 2:3], pwr[:, sp2, :], AX.X,
                                                    OP.max, apply_absolute_value=True))(sp2),
             reads=("pwr",), writes=("tmps",))
        P.op("vector", lambda v: v.tensor_tensor(m3r[:], m3r[:], tmps[:, 2:3], OP.max),
             reads=("tmps", "m3r"), writes=("m3r",), incs=((DV, 1),))
        dvp = DV.inc(1)
        sem = spw.sems[sp2]
        tgt = sem.inc(16)
        P.op("scalar" if ic % 2 == 0 else "gpsimd",
             (lambda ic, sp2: lambda e: e.dma_start(out=p_d[ic * 128:(ic + 1) * 128, :],
                                                    in_=pwr[:, sp2, :]))(ic, sp2),
             waits=((DV, dvp1),), incs=((sem, 16),), reads=("pwr",))
        spw.rel[sp2] = [(sem, tgt), (DV, dvp)]
    dv_m13 = DV.val
    w_mu = smt.totals()
    w_p = spw.totals()

    ccw = scalar_ar(8, [(m1r[:], "m1r"), (m3r[:], "m3r")], (DV, dv_m13), CC13)
    gw = bcast(8, 0, tmps[:, 0:1], "tmps", CC13, ccw)
    gw2 = bcast(8, 1, tmps[:, 1:2], "tmps", CC13, ccw)
    P.op("vector", lambda v: v.tensor_scalar(tmps[:, 2:3], tmps[:, 0:1], 1e-30, None, OP.max),
         waits=(gw, gw2), reads=("tmps",), writes=("tmps",))
    P.op("vector", lambda v: v.reciprocal(c1s[:], tmps[:, 2:3]), reads=("tmps",), writes=("c1s",))
    P.op("vector", lambda v: v.tensor_scalar(c1s[:], c1s[:], QMAX, None, OP.mult),
         reads=("c1s",), writes=("c1s",))
    P.op("vector", lambda v: v.tensor_scalar(tmps[:, 3:4], tmps[:, 1:2], 1e-30, None, OP.max),
         reads=("tmps",), writes=("tmps",))
    P.op("vector", lambda v: v.reciprocal(c3s[:], tmps[:, 3:4]), reads=("tmps",), writes=("c3s",))
    P.op("vector", lambda v: v.tensor_scalar(c3s[:], c3s[:], QMAX, None, OP.mult),
         reads=("c3s",), writes=("c3s",))
    P.op("vector", lambda v: v.reciprocal(tmps[:, 4:5], rc0[:]), reads=("rc0",), writes=("tmps",))
    P.op("vector", lambda v: v.tensor_tensor(S4s[:], tmps[:, 0:1], tmps[:, 4:5], OP.mult),
         reads=("tmps",), writes=("S4s",))
    P.op("vector", lambda v: v.tensor_tensor(S4s[:], S4s[:], tmps[:, 1:2], OP.mult),
         reads=("S4s", "tmps"), writes=("S4s",))
    P.op("vector", lambda v: v.tensor_tensor(S4s[:], S4s[:], S2s[:], OP.mult),
         reads=("S4s", "S2s"), writes=("S4s",))
    P.op("vector", lambda v: v.tensor_scalar(S4s[:], S4s[:], 1.0 / (65536.0 * QMAX * QMAX * QMAX),
                                             None, OP.mult),
         reads=("S4s",), writes=("S4s",), incs=((DV, 1),))
    dv_c13 = DV.inc(1)

    wd_dq_v = wdq_o2.rearrange("(o t) d -> o (t d)", t=3)
    b2_free = {}
    for ic in range(IC):
        bb, bn = (b2st, "b2st") if ic % 2 == 0 else (b2stB, "b2stB")
        s, (lps, lpt) = frp.dma(
            "sync", (lambda ic: lambda e, s: e.dma_start(
                out=rdp[:, s, :], in_=p_d[ic * 128:(ic + 1) * 128, :]))(ic),
            extra_waits=(w_p + w_mu) if ic == 0 else ())
        su, (lus, lut) = fru.dma(
            "scalar", (lambda ic: lambda e, s: e.dma_start(
                out=rdmu[:, s, :], in_=mu_d[ic * 128:(ic + 1) * 128, :]))(ic),
            extra_waits=(w_mu + w_p) if ic == 0 else ())
        P.op("scalar",
             (lambda s, bb: lambda e: e.activation(bb[:, 0, :], rdp[:, s, :], AF.Identity,
                                                   bias=magicv[:], scale=c3s[:]))(s, bb),
             waits=((lps, lpt), (DV, dv_c13)) + b2_free.get(ic % 2, ()),
             reads=("rdp", "c3s", "magicv"),
             writes=(bn,), incs=((AC, 1),))
        acv = AC.inc(1)
        frp.release((AC, acv))
        P.op("scalar",
             (lambda su: lambda e: e.activation(rdmu[:, su, :], rdmu[:, su, :], AF.Identity,
                                                bias=magicv[:], scale=c1s[:]))(su),
             waits=((lus, lut),), reads=("rdmu", "c1s", "magicv"),
             writes=("rdmu",), incs=((AC, 1),))
        acu = AC.inc(1)
        sq = (sqw.i + 1) % 2
        sqw.i += 1
        wfreeq = tuple(sqw.rel[sq])
        sqw.rel[sq] = []
        P.op("vector",
             (lambda su: lambda v: v.tensor_scalar(rdmu[:, su, :], rdmu[:, su, :], MAGIC,
                                                   None, OP.subtract))(su),
             waits=((AC, acu),), reads=("rdmu",), writes=("rdmu",))
        P.op("vector",
             (lambda su, sq, bb: lambda v: v.scalar_tensor_tensor(
                 qwr[:, sq, :], bb[:, 0, :], MAGIC, rdmu[:, su, :],
                 OP.subtract, OP.mult))(su, sq, bb),
             waits=wfreeq, reads=("rdmu", bn), writes=("qwr",), incs=((DV, 1),))
        dvq1 = DV.inc(1)
        fru.release((DV, dvq1))
        b2_free[ic % 2] = ((DV, dvq1),)
        P.op("vector",
             (lambda sq: lambda v: v.tensor_reduce(tmps[:, 2:3], qwr[:, sq, :], AX.X,
                                                   OP.max, apply_absolute_value=True))(sq),
             reads=("qwr",), writes=("tmps",))
        P.op("vector", lambda v: v.tensor_tensor(m4r[:], m4r[:], tmps[:, 2:3], OP.max),
             reads=("tmps", "m4r"), writes=("m4r",), incs=((DV, 1),))
        dvq = DV.inc(1)
        sem = sqw.sems[sq]
        tgt = sem.inc(16)
        P.op("gpsimd",
             (lambda ic, sq: lambda e: e.dma_start(out=q_d[ic * 128:(ic + 1) * 128, :],
                                                   in_=qwr[:, sq, :]))(ic, sq),
             waits=((DV, dvq1),), incs=((sem, 16),), reads=("qwr",))
        sqw.rel[sq] = [(sem, tgt), (DV, dvq)]
        fwd.dma("sync",
                (lambda ic: lambda e, s: e.dma_start(
                    out=wdT[:, ic, :], in_=wd_dq_v[:, ic * 128:(ic + 1) * 128],
                    transpose=True))(ic),
                extra_waits=(w_mu + ((CCW2, 1),)) if ic == 0 else ())
    dv_m4 = DV.val
    w_q = sqw.totals()
    w_wdT = fwd.totals()

    ccw = scalar_ar(12, [(m4r[:], "m4r")], (DV, dv_m4), CC4)
    gw = bcast(12, 0, tmps[:, 0:1], "tmps", CC4, ccw)
    P.op("vector", lambda v: v.tensor_scalar(tmps[:, 1:2], tmps[:, 0:1], 1e-30, None, OP.max),
         waits=(gw,), reads=("tmps",), writes=("tmps",))
    P.op("vector", lambda v: v.reciprocal(c4s[:], tmps[:, 1:2]), reads=("tmps",), writes=("c4s",))
    P.op("vector", lambda v: v.tensor_scalar(c4s[:], c4s[:], QMAX, None, OP.mult),
         reads=("c4s",), writes=("c4s",))
    P.op("vector", lambda v: v.tensor_tensor(S4s[:], S4s[:], tmps[:, 0:1], OP.mult),
         reads=("S4s", "tmps"), writes=("S4s",), incs=((DV, 1),))
    dv_c4 = DV.inc(1)

    o_ready = []
    for ic in range(IC):
        s, (lqs, lqt) = frq.dma(
            "sync", (lambda ic: lambda e, s: e.dma_start(
                out=rdq[:, s, :], in_=q_d[ic * 128:(ic + 1) * 128, :]))(ic),
            extra_waits=w_q if ic == 0 else ())
        P.op("vector",
             (lambda s: lambda v: v.tensor_scalar(rdq[:, s, :], rdq[:, s, :], c4s[:], MAGIC,
                                                  OP.mult, OP.add))(s),
             waits=((lqs, lqt), (DV, dv_c4)), reads=("rdq", "c4s"), writes=("rdq",))
        ot, oname = (o_lo, "o_lo") if ic < IC // 2 else (o_hi, "o_hi")
        P.op("vector",
             (lambda s, ic, ot: lambda v: v.tensor_scalar(
                 ot[:, ic % (IC // 2), :], rdq[:, s, :], MAGIC, None, OP.subtract))(s, ic, ot),
             reads=("rdq",), writes=(oname,), incs=((DV, 1),))
        dvo = DV.inc(1)
        frq.release((DV, dvo))
        o_ready.append(dvo)

    prev_grp_dv = 0
    for tg in range(0, NT, 3):
        tts = list(range(tg, min(tg + 3, NT)))
        regs = {}
        lw = {}
        for tt in tts:
            s, (lys, lyt) = fy1.dma(
                "scalar", (lambda tt: lambda e, s: e.dma_start(
                    out=y1rd[:, s, :], in_=y1_d[tt * 128:(tt + 1) * 128, :]))(tt),
                extra_waits=w_y1d if tg == 0 else ())
            lw[tt] = (s, lys, lyt)
            regs[tt], _ = ps_acquire()
        for ic in range(IC):
            ot = o_lo if ic < IC // 2 else o_hi
            for j, tt in enumerate(tts):
                lh = ot[:, ic % (IC // 2), tt * 128:(tt + 1) * 128]
                for hf in range(2):
                    st, sp = ic == 0, ic == IC - 1
                    w0 = ()
                    if hf == 0 and j == 0:
                        w0 = ((DV, o_ready[ic]),) + w_wdT
                        if ic == 0 and prev_grp_dv:
                            w0 = w0 + ((DV, prev_grp_dv),)
                    incs = ()
                    if sp and hf == 1 and j == len(tts) - 1:
                        incs = ((TE, 1),)
                        TE.inc(1)
                    mm(regs[tt][:, hf * 512:(hf + 1) * 512], lh,
                       wdT[:, ic, hf * 512:(hf + 1) * 512], st, sp,
                       waits=w0, incs=incs, skip=True)
        tev = TE.val
        for tt in tts:
            s, lys, lyt = lw[tt]
            so = (syo.i + 1) % 2
            syo.i += 1
            wfreeo = tuple(syo.rel[so])
            syo.rel[so] = []
            P.op("vector",
                 (lambda tt, s, so, r: lambda v: v.scalar_tensor_tensor(
                     youts[:, so, :], r[:], S4s[:], y1rd[:, s, :], OP.mult, OP.add))(tt, s, so, regs[tt]),
                 waits=((TE, tev), (lys, lyt)) + wfreeo,
                 reads=("y1rd", "S4s"), writes=("youts",), incs=((DV, 1),))
            dvv = DV.inc(1)
            fy1.release_slot(s, (DV, dvv))
            sem = syo.sems[so]
            tgt = sem.inc(16)
            P.op("scalar",
                 (lambda tt, so: lambda e: e.dma_start(out=y_out[tt * 128:(tt + 1) * 128, :],
                                                       in_=youts[:, so, :]))(tt, so),
                 waits=((DV, dvv),), incs=((sem, 16),), reads=("youts",))
            syo.rel[so] = [(sem, tgt)]
        prev_grp_dv = DV.val
        rot["rel"] = [((DV, prev_grp_dv),)] * 3
    w_out = syo.totals()

    P.op("gpsimd", lambda g: None, waits=w_out)

    from contextlib import ExitStack
    with ExitStack() as stk:
        for s in P.sems:
            s.h = stk.enter_context(nc.semaphore(s.name))
        block = stk.enter_context(nc.Block())
        P.emit(block)
    return nc


_NC_CACHE = {}


def kernel(L0_attn_x, L0_attn_W_qkv, L0_attn_W_out, L0_attn_b_out,
           L0_ffn_w_gate, L0_ffn_w_up, L0_ffn_w_down):
    x = np.ascontiguousarray(np.asarray(L0_attn_x, dtype=np.float32)).reshape(-1, D)
    n = 8
    tpc = x.shape[0] // n
    assert tpc == T, f"expected {T} tokens/core, got {tpc}"
    if "nc" not in _NC_CACHE:
        _NC_CACHE["nc"] = build()
    nc = _NC_CACHE["nc"]
    w = dict(
        wqkv=np.ascontiguousarray(np.asarray(L0_attn_W_qkv, np.float32)),
        wout=np.ascontiguousarray(np.asarray(L0_attn_W_out, np.float32)),
        bout=np.ascontiguousarray(np.asarray(L0_attn_b_out, np.float32)).reshape(1, D),
    )
    wgu = np.ascontiguousarray(np.vstack([
        np.asarray(L0_ffn_w_gate, np.float32),
        np.asarray(L0_ffn_w_up, np.float32)]))
    wdf = np.ascontiguousarray(np.asarray(L0_ffn_w_down, np.float32)).reshape(-1, D)
    in_maps = [dict(x=np.ascontiguousarray(x[c * tpc:(c + 1) * tpc]),
                    wsh1=np.ascontiguousarray(wgu[c * 768:(c + 1) * 768]),
                    wsh2=np.ascontiguousarray(wdf[c * 384:(c + 1) * 384]),
                    **w)
               for c in range(n)]
    res = run_bass_kernel_spmd(nc, in_maps, core_ids=list(range(n)))
    out = np.concatenate([np.asarray(r["y"]) for r in res.results], axis=0)
    return out.reshape(np.asarray(L0_attn_x).shape).astype(np.float32)
